# version 04
# baseline (speedup 1.0000x reference)
import os
import sys
import numpy as np
from contextlib import ExitStack

sys.path.insert(0, "/opt/trn_rl_repo")
import concourse.bass as bass
import concourse.mybir as mybir
from concourse.bass_utils import run_bass_kernel_spmd

F32 = mybir.dt.float32
F32R = mybir.dt.float32r
BF16 = mybir.dt.bfloat16
AF = mybir.ActivationFunctionType
ALU = mybir.AluOpType
AX = mybir.AxisListType

NCORES = 8
B, S, D = 32, 2048, 1024
NSEQ = B // NCORES
NT = S // 128
H, HD = 16, 64
DH = 4096
EPS = 1e-5

ENGINES = ("sync", "tensor", "vector", "scalar", "gpsimd")



def lhsT_image(W):
    O, I = W.shape
    Oc, Ic = O // 128, I // 128
    return np.ascontiguousarray(W.reshape(Oc, 128, Ic, 128).transpose(3, 0, 2, 1).reshape(128, Oc * Ic * 128))


def rhs_image(WT):
    K, O = WT.shape
    Kc = K // 128
    return np.ascontiguousarray(WT.reshape(Kc, 128, O).transpose(1, 0, 2).reshape(128, Kc * O))


def cols_image(v):
    n = v.shape[0] // 128
    return np.ascontiguousarray(v.reshape(n, 128).T)


def bcast_image(v):
    return np.ascontiguousarray(np.tile(np.asarray(v)[None, :], (128, 1)))


def build_images(inp):
    import ml_dtypes
    img = {}
    img["p2_w1"] = lhsT_image(inp["L0_ffn_w1_w"])
    img["p2_w2"] = lhsT_image(inp["L0_ffn_w2_w"])
    img["p2_h1"] = lhsT_image(inp["L0_ffn_h1_w"])
    img["p2_h2"] = lhsT_image(inp["L0_ffn_h2_w"])
    img["p2_w3"] = rhs_image(np.ascontiguousarray(inp["L0_ffn_w3_w"].T))
    img["p2_s12"] = np.concatenate([cols_image(inp["L0_ffn_w1_s"]),
                                    cols_image(inp["L0_ffn_w2_s"]),
                                    cols_image(inp["L0_ffn_h1_s"]),
                                    cols_image(inp["L0_ffn_h2_s"])], axis=1)
    img["p2_w3sb"] = bcast_image(inp["L0_ffn_w3_s"])

    perm_q = []
    for gi in range(4):
        for half in range(2):
            for hh in range(4):
                h = 4 * gi + hh
                for i in range(32):
                    perm_q.append(h * 64 + 2 * i + half)
    perm_q = np.array(perm_q)
    Wq = inp["L0_attn_W_qkv"][0:1024][perm_q]
    Wk = inp["L0_attn_W_qkv"][1024:2048][perm_q]
    Wv = inp["L0_attn_W_qkv"][2048:3072]
    img["p1_qkv"] = lhsT_image(np.concatenate([Wq, Wk, Wv], axis=0))
    img["p1_wo"] = rhs_image(np.ascontiguousarray(inp["L0_attn_W_o"].T))
    t = np.arange(S, dtype=np.float64)
    theta = 10000.0 ** (-2.0 * (np.arange(32, dtype=np.float64) / 64.0))
    ang = t[None, :] * theta[:, None]
    img["p1_cc"] = np.ascontiguousarray(np.tile(np.cos(ang).astype(np.float32), (4, 1)))
    img["p1_ss"] = np.ascontiguousarray(np.tile(np.sin(ang).astype(np.float32), (4, 1)))
    m = np.zeros((128, 4 * 512), dtype=np.float32)
    for v in range(4):
        blk = np.zeros((128, 512), dtype=np.float32)
        blk[:, v * 128:(v + 1) * 128] = np.triu(np.full((128, 128), -1e30, dtype=np.float32), 1)
        blk[:, (v + 1) * 128:] = -1e30
        m[:, v * 512:(v + 1) * 512] = blk
    img["p1_mask"] = m
    img["ident_bf"] = np.eye(128, dtype=ml_dtypes.bfloat16)
    img["ident_f32"] = np.eye(128, dtype=np.float32)

    w3q = np.concatenate(
        [np.ascontiguousarray(inp["L1_attn_Wq_w"].T),
         np.ascontiguousarray(inp["L1_attn_Wk_w"].T),
         np.ascontiguousarray(inp["L1_attn_Wv_w"].T)], axis=1)
    img["p3_wqkv"] = np.ascontiguousarray(np.tile(w3q, (2, 1)))
    b3q = np.stack([inp["L1_attn_Wq_b"], inp["L1_attn_Wk_b"], inp["L1_attn_Wv_b"]], axis=1)
    img["p3_bqkv"] = np.ascontiguousarray(np.tile(b3q, (2, 1)))
    img["p3_proj"] = rhs_image(np.ascontiguousarray(inp["L1_attn_proj_w"].T))
    img["p3_projb"] = bcast_image(inp["L1_attn_proj_b"])

    w1 = inp["L1_ffn_w1"]
    img["p4_w1"] = np.ascontiguousarray(
        w1.reshape(8, 8, 128, 32, 128).transpose(2, 0, 3, 1, 4).reshape(128, 8 * 32 * 8 * 128))
    w2 = inp["L1_ffn_w2"]
    img["p4_w2"] = np.ascontiguousarray(
        w2.reshape(8, 32, 128, 1024).transpose(2, 0, 1, 3).reshape(128, 8 * 32 * 1024))
    img["p4_b1"] = np.concatenate([cols_image(inp["L1_ffn_b1"][e]) for e in range(8)], axis=1)
    img["p4_beta"] = np.concatenate([cols_image(inp["L1_ffn_beta"][e]) for e in range(8)], axis=1)
    img["p4_gamma"] = np.concatenate([cols_image(inp["L1_ffn_gamma"][e]) for e in range(8)], axis=1)
    img["p4_b2"] = np.ascontiguousarray(inp["L1_ffn_b2"].reshape(1, 8 * 1024))
    cent = inp["L1_ffn_centroids"]
    img["p4_cent"] = np.ascontiguousarray(cent.T.reshape(8, 128, 8).transpose(1, 0, 2).reshape(128, 64))
    img["ones_row"] = np.ones((1, 128), dtype=np.float32)
    return img


IMG_DTYPES = {"ident_bf": BF16}



class Sem:
    __slots__ = ("name", "n", "h")

    def __init__(self, name):
        self.name = name
        self.n = 0
        self.h = None


class Rec:
    def __init__(self, nc, stack):
        self.nc = nc
        self.root_stack = stack
        self.stack = stack
        self.ops = {e: [] for e in ENGINES}
        self.nsems = 0
        self.epsc = None
        self._sem_pool = []
        self._sem_idx = 0

    def init_epsc(self):
        if self.epsc is None:
            old = self.stack
            self.stack = self.root_stack
            self.epsc = self.sb("epsc", [128, 1], F32)
            self.stack = old
            self.op("vector", lambda e: e.memset(self.epsc[:, :], float(EPS)))
            self.op("vector", lambda e: e.drain())

    def sem(self, name):
        if self._sem_idx < len(self._sem_pool):
            s = self._sem_pool[self._sem_idx]
        else:
            s = Sem(name)
            s.h = self.root_stack.enter_context(self.nc.semaphore(f"s{len(self._sem_pool)}"))
            self._sem_pool.append(s)
            self.nsems += 1
        self._sem_idx += 1
        return s

    def phase_begin(self):
        self._sem_idx = 0

    def sb(self, name, shape, dt=F32):
        return self.stack.enter_context(self.nc.sbuf_tensor(name, shape, dt))

    def ps(self, name, shape, dt=F32):
        return self.stack.enter_context(self.nc.psum_tensor(name, shape, dt))

    def op(self, eng, fn):
        self.ops[eng].append(fn)

    def inc(self, eng, fn, sem, k=1):
        sem.n += k
        self.ops[eng].append(lambda e, fn=fn, h=sem.h, k=k: fn(e).then_inc(h, k))

    def wait(self, eng, sem, val=None):
        v = sem.n if val is None else val
        if v <= 0:
            return
        self.ops[eng].append(lambda e, h=sem.h, v=v: e.wait_ge(h, v))

    def wait_rel(self, eng, sem, behind):
        self.wait(eng, sem, sem.n - behind)

    def attach_inc(self, eng, sem, k=1):
        sem.n += k
        old = self.ops[eng][-1]
        self.ops[eng][-1] = lambda e, old=old, h=sem.h, k=k: old(e).then_inc(h, k)

    def barrier(self, name):
        s = self.sem(name)
        for e in ENGINES:
            s.n += 1
            self.ops[e].append(lambda x, h=s.h: x.sem_inc(h, 1))
        tgt = s.n
        for e in ENGINES:
            self.wait(e, s, tgt)

    def replay(self):
        block = self.root_stack.enter_context(self.nc.Block())
        for ename in ENGINES:
            oplist = self.ops[ename]

            def body(eng, oplist=oplist):
                for fn in oplist:
                    fn(eng)
            getattr(block, ename)(body)



def ln_rstd_chain(R, v, t):
    R.op("vector", lambda e, t=t: e.bn_aggr(v["MV"][:, 2 * t:2 * t + 2], v["BN6"][:, 12 * t:12 * t + 12]))
    R.inc("vector", lambda e: e.drain(), v["s_d2a"])
    R.wait("scalar", v["s_d2a"])
    R.inc("scalar", lambda e, t=t: e.activation(out=v["SD"][:, t:t + 1], in_=v["MV"][:, 2 * t + 1:2 * t + 2],
                                                func=AF.Sqrt, bias=R.epsc[:, 0:1]), v["s_a2d"])
    R.wait("vector", v["s_a2d"])
    R.op("vector", lambda e, t=t: e.reciprocal(v["RSTD"][:, t:t + 1], v["SD"][:, t:t + 1]))
    R.op("vector", lambda e: e.drain())
    R.op("vector", lambda e, t=t: e.scalar_tensor_tensor(v["NMR"][:, t:t + 1], v["MV"][:, 2 * t:2 * t + 1],
                                                         -1.0, v["RSTD"][:, t:t + 1],
                                                         op0=ALU.mult, op1=ALU.mult))
    R.op("vector", lambda e: e.drain())


def mk_preln_sems(R, nm):
    return dict(xld=R.sem(nm + "xld"), s_d2a=R.sem(nm + "d2a"), s_a2d=R.sem(nm + "a2d"),
                s_app=R.sem(nm + "app"), s_tp=R.sem(nm + "tp"), s_ev=R.sem(nm + "ev"))


def mk_preln_tensors(R, nm, need_ufm=True, XT=None):
    d = dict(XT=XT if XT is not None else R.sb(nm + "XT", [128, 2 * 1024], F32),
             BN6=R.sb(nm + "BN6", [128, 12 * NT], F32), MV=R.sb(nm + "MV", [128, 2 * NT], F32),
             SD=R.sb(nm + "SD", [128, NT], F32), RSTD=R.sb(nm + "RSTD", [128, NT], F32),
             NMR=R.sb(nm + "NMR", [128, NT], F32))
    if need_ufm:
        d["UB"] = R.sb(nm + "UB", [128, 2 * 1024], BF16)
    return d


def emit_preln(R, pt, sm, x_src, seq, U_fm=None, u_f32=None):
    v = dict(BN6=pt["BN6"], MV=pt["MV"], SD=pt["SD"], RSTD=pt["RSTD"], NMR=pt["NMR"],
             s_d2a=sm["s_d2a"], s_a2d=sm["s_a2d"])
    for t in range(NT):
        sl = 1024 * (t % 2)
        R.wait_rel("sync", sm["s_app"], 1)
        R.inc("sync", lambda e, t=t, sl=sl: e.dma_start(out=pt["XT"][:, sl:sl + 1024],
                                                        in_=x_src[(seq * S + 128 * t):(seq * S + 128 * (t + 1)), :]),
              sm["xld"], 16)
        R.wait("vector", sm["xld"])
        R.op("vector", lambda e, t=t, sl=sl: e.bn_stats(pt["BN6"][:, 12 * t:12 * t + 6], pt["XT"][:, sl:sl + 512]))
        R.op("vector", lambda e, t=t, sl=sl: e.bn_stats(pt["BN6"][:, 12 * t + 6:12 * t + 12], pt["XT"][:, sl + 512:sl + 1024]))
        R.op("vector", lambda e: e.drain())
        ln_rstd_chain(R, v, t)
        if u_f32 is not None:
            R.inc("vector", lambda e, t=t, sl=sl: e.tensor_scalar(
                u_f32[:, 1024 * t:1024 * (t + 1)], pt["XT"][:, sl:sl + 1024],
                pt["RSTD"][:, t:t + 1], pt["NMR"][:, t:t + 1], op0=ALU.mult, op1=ALU.add), sm["s_app"])
        if U_fm is not None:
            R.wait_rel("vector", sm["s_tp"], 1)
            R.inc("vector", lambda e, t=t, sl=sl: e.tensor_scalar(
                pt["UB"][:, sl:sl + 1024], pt["XT"][:, sl:sl + 1024],
                pt["RSTD"][:, t:t + 1], pt["NMR"][:, t:t + 1], op0=ALU.mult, op1=ALU.add), sm["s_app"])
            R.wait("tensor", sm["s_app"])
            R.wait_rel("tensor", sm["s_ev"], 1)
            for d in range(8):
                fn = lambda e, t=t, d=d, sl=sl: e.transpose(
                    pt["ps_bf"][:, sl + 128 * d:sl + 128 * (d + 1)],
                    pt["UB"][:, sl + 128 * d:sl + 128 * (d + 1)], pt["ident"][:, :])
                if d == 7:
                    R.inc("tensor", fn, sm["s_tp"])
                else:
                    R.op("tensor", fn)
            R.wait("vector", sm["s_tp"])
            R.inc("vector", lambda e, t=t, sl=sl: e.tensor_copy(
                U_fm[:, :].rearrange("p (d s) -> p d s", d=8)[:, :, 128 * t:128 * (t + 1)],
                pt["ps_bf"][:, sl:sl + 1024].rearrange("p (d s) -> p d s", d=8)),
                sm["s_ev"])


def mk_epi_sems(R, nm):
    return dict(xld=R.sem(nm + "xld"), st=R.sem(nm + "st"), s_apply=R.sem(nm + "ap"),
                s_d2a=R.sem(nm + "Ed2a"), s_a2d=R.sem(nm + "Ea2d"))


def mk_epi_tensors(R, nm, DSB, XT2, stats_from=None):
    if stats_from is not None:
        return dict(DSB=DSB, XT2=XT2, BN6=stats_from["BN6"], MV=stats_from["MV"],
                    SD=stats_from["SD"], RSTD=stats_from["RSTD"], NMR=stats_from["NMR"])
    return dict(
        DSB=DSB,
        XT2=XT2,
        BN6=R.sb(nm + "BN6", [128, 12 * NT], F32),
        MV=R.sb(nm + "MV", [128, 2 * NT], F32),
        SD=R.sb(nm + "SD", [128, NT], F32),
        RSTD=R.sb(nm + "RSTD", [128, NT], F32),
        NMR=R.sb(nm + "NMR", [128, NT], F32),
    )


def epi_prefetch_x(R, ep, sm, x_src, seq, t):
    R.wait("sync", sm["st"])
    R.inc("sync", lambda e, t=t: e.dma_start(
        out=ep["XT2"][:, :],
        in_=x_src[(seq * S + 128 * t):(seq * S + 128 * (t + 1)), :]), sm["xld"], 16)


def emit_epilogue_tile(R, ep, sm, seq, t, delta_src_fn, x_dst, bias_bcast=None, scale_bcast=None,
                       direct_sbuf=False):
    v = dict(BN6=ep["BN6"], MV=ep["MV"], SD=ep["SD"], RSTD=ep["RSTD"], NMR=ep["NMR"],
             s_d2a=sm["s_d2a"], s_a2d=sm["s_a2d"])
    if direct_sbuf:
        src = delta_src_fn
    else:
        if scale_bcast is not None:
            R.op("vector", lambda e: e.tensor_tensor(ep["DSB"][:, :], delta_src_fn(), scale_bcast[:, :], op=ALU.mult))
        elif bias_bcast is not None:
            R.op("vector", lambda e: e.tensor_tensor(ep["DSB"][:, :], delta_src_fn(), bias_bcast[:, :], op=ALU.add))
        else:
            R.op("vector", lambda e: e.tensor_copy(ep["DSB"][:, :], delta_src_fn()))
        R.op("vector", lambda e: e.drain())
        src = lambda: ep["DSB"][:, :]
    R.op("vector", lambda e, t=t: e.bn_stats(ep["BN6"][:, 12 * t:12 * t + 6], src()[:, 0:512]))
    R.op("vector", lambda e, t=t: e.bn_stats(ep["BN6"][:, 12 * t + 6:12 * t + 12], src()[:, 512:1024]))
    R.op("vector", lambda e: e.drain())
    ln_rstd_chain(R, v, t)
    R.wait("vector", sm["xld"])
    R.op("vector", lambda e, t=t: e.scalar_tensor_tensor(
        ep["XT2"][:, :], src(), ep["RSTD"][:, t:t + 1],
        ep["XT2"][:, :], op0=ALU.mult, op1=ALU.add))
    R.op("vector", lambda e: e.drain())
    R.inc("vector", lambda e, t=t: e.tensor_scalar(
        ep["XT2"][:, :], ep["XT2"][:, :],
        ep["NMR"][:, t:t + 1], None, op0=ALU.add), sm["s_apply"])
    R.wait("gpsimd", sm["s_apply"])
    R.inc("gpsimd", lambda e, t=t: e.dma_start(
        out=x_dst[(seq * S + 128 * t):(seq * S + 128 * (t + 1)), :],
        in_=ep["XT2"][:, :]), sm["st"], 16)



def emit_phase2(R, g, x_src, x_dst):
    R.init_epsc()
    cl = R.sem("p2cl")
    ident = R.sb("p2ident", [128, 128], BF16)
    S12 = R.sb("p2S12", [128, 128], F32)
    W3SB = R.sb("p2w3sb", [128, 1024], F32)
    R.inc("sync", lambda e: e.dma_start(out=ident[:], in_=g["ident_bf"][:, :]), cl, 16)
    R.inc("sync", lambda e: e.dma_start(out=S12[:], in_=g["p2_s12"][:, :]), cl, 16)
    R.inc("sync", lambda e: e.dma_start(out=W3SB[:], in_=g["p2_w3sb"][:, :]), cl, 16)
    for e in ENGINES:
        R.wait(e, cl)

    ps_all = R.ps("p2ps", [128, 4096], F32)
    ps_bf = ps_all[:, 0:1024].bitcast(BF16)
    U_fm = R.sb("p2U", [128, 8 * 2048], BF16)
    A_ = R.sb("p2A", [128, 32 * 1024], BF16)
    B_ = R.sb("p2B", [128, 32 * 1024], BF16)
    W1B = R.sb("p2W1B", [128, 2 * 1024], BF16)
    W2B = R.sb("p2W2B", [128, 2 * 1024], BF16)
    HB = R.sb("p2HB", [128, 2 * 4096], BF16)
    W3B = W1B
    SCR = R.sb("p2SCR", [128, 1024], F32)
    SILU = SCR
    pt = mk_preln_tensors(R, "p2pl")
    pt.update(ps_bf=ps_bf, ident=ident)
    psm = mk_preln_sems(R, "p2pl")
    ep = mk_epi_tensors(R, "p2ep", SCR, pt["XT"][:, 0:1024])
    esm = mk_epi_sems(R, "p2ep")
    wl = R.sem("p2wl")
    mmA = R.sem("p2mmA")
    acA = R.sem("p2acA")
    dvA = R.sem("p2dvA")
    hl = R.sem("p2hl")
    mmH = R.sem("p2mmH")
    acH = R.sem("p2acH")
    w3l = R.sem("p2w3l")
    mmW3c = R.sem("p2mmW3c")

    for seq in range(NSEQ):
        if seq > 0:
            R.wait("sync", esm["st"])
            R.wait("vector", mmA)
            R.wait("tensor", esm["s_apply"])
            R.wait("tensor", esm["st"])
        emit_preln(R, pt, psm, x_src, seq, U_fm=U_fm)
        R.wait("sync", psm["s_app"])
        epi_prefetch_x(R, ep, esm, x_src, seq, 0)

        for tc in range(2):
            R.wait("tensor", psm["s_ev"])
            if tc == 1:
                R.wait("vector", mmW3c)
                R.wait("tensor", esm["s_apply"])
                R.wait("tensor", esm["st"])
            R.wait("gpsimd", mmW3c)
            for c in range(32):
                par = c % 2
                R.wait_rel("gpsimd", mmA, 1)
                R.inc("gpsimd", lambda e, c=c, par=par: e.dma_start(
                    out=W1B[:, 1024 * par:1024 * (par + 1)], in_=g["p2_w1"][:, 1024 * c:1024 * (c + 1)]), wl, 16)
                R.inc("gpsimd", lambda e, c=c, par=par: e.dma_start(
                    out=W2B[:, 1024 * par:1024 * (par + 1)], in_=g["p2_w2"][:, 1024 * c:1024 * (c + 1)]), wl, 16)
                R.wait("tensor", wl)
                R.wait_rel("tensor", dvA, 1)
                for pre in range(2):
                    WB = W1B if pre == 0 else W2B
                    for p in range(2):
                        for d in range(8):
                            fn = lambda e, p=p, d=d, par=par, tc=tc, WB=WB, pre=pre: e.matmul(
                                ps_all[:, 512 * (4 * par + 2 * pre + p):512 * (4 * par + 2 * pre + p + 1)],
                                WB[:, 1024 * par + 128 * d:1024 * par + 128 * (d + 1)],
                                U_fm[:, 2048 * d + 1024 * tc + 512 * p:2048 * d + 1024 * tc + 512 * (p + 1)],
                                start=(d == 0), stop=(d == 7))
                            if pre == 1 and p == 1 and d == 7:
                                R.inc("tensor", fn, mmA)
                            else:
                                R.op("tensor", fn)
                R.wait("scalar", mmA)
                R.wait("scalar", dvA)
                for p in range(2):
                    fn = lambda e, c=c, p=p, par=par: e.activation(
                        out=SILU[:, 512 * p:512 * (p + 1)],
                        in_=ps_all[:, 512 * (4 * par + p):512 * (4 * par + p + 1)],
                        func=AF.Silu, scale=S12[:, c:c + 1])
                    R.inc("scalar", fn, acA) if p == 1 else R.op("scalar", fn)
                R.wait("vector", acA)
                for p in range(2):
                    fn = lambda e, c=c, p=p, par=par: e.scalar_tensor_tensor(
                        A_[:, 1024 * c + 512 * p:1024 * c + 512 * (p + 1)],
                        ps_all[:, 512 * (4 * par + 2 + p):512 * (4 * par + 2 + p + 1)],
                        S12[:, 32 + c:32 + c + 1], SILU[:, 512 * p:512 * (p + 1)],
                        op0=ALU.mult, op1=ALU.mult)
                    R.inc("vector", fn, dvA) if p == 1 else R.op("vector", fn)

            def h_stage(img, srcT, dstT, scol_off, first):
                R.wait("tensor", dvA)
                for c in range(32):
                    par = c % 2
                    R.wait_rel("gpsimd", mmH, 1)
                    R.inc("gpsimd", lambda e, c=c, par=par, img=img: e.dma_start(
                        out=HB[:, 4096 * par:4096 * (par + 1)], in_=img[:, 4096 * c:4096 * (c + 1)]), hl, 16)
                    R.wait("tensor", hl)
                    R.wait_rel("tensor", acH, 1)
                    if first and c == 0:
                        R.wait("tensor", acA)
                        R.wait("tensor", dvA)
                    for p in range(2):
                        for k in range(32):
                            fn = lambda e, c=c, p=p, k=k, par=par, srcT=srcT: e.matmul(
                                ps_all[:, 512 * (2 * par + p):512 * (2 * par + p + 1)],
                                HB[:, 4096 * par + 128 * k:4096 * par + 128 * (k + 1)],
                                srcT[:, 1024 * k + 512 * p:1024 * k + 512 * (p + 1)],
                                start=(k == 0), stop=(k == 31))
                            if p == 1 and k == 31:
                                R.inc("tensor", fn, mmH)
                            else:
                                R.op("tensor", fn)
                    R.wait("scalar", mmH)
                    for p in range(2):
                        fn = lambda e, c=c, p=p, par=par, dstT=dstT, so=scol_off: e.activation(
                            out=dstT[:, 1024 * c + 512 * p:1024 * c + 512 * (p + 1)],
                            in_=ps_all[:, 512 * (2 * par + p):512 * (2 * par + p + 1)],
                            func=AF.Gelu, scale=S12[:, so + c:so + c + 1])
                        R.inc("scalar", fn, acH) if p == 1 else R.op("scalar", fn)

            h_stage(g["p2_h1"], A_, B_, 64, True)
            h_stage(g["p2_h2"], B_, A_, 96, False)

            for tg in range(2):
                R.wait("gpsimd", mmA)
                for c in range(32):
                    par = c % 2
                    R.wait_rel("gpsimd", mmW3c, 1)
                    R.inc("gpsimd", lambda e, c=c, par=par: e.dma_start(
                        out=W3B[:, 1024 * par:1024 * (par + 1)], in_=g["p2_w3"][:, 1024 * c:1024 * (c + 1)]), w3l, 16)
                    R.wait("tensor", w3l)
                    if c == 0:
                        R.wait("tensor", acH)
                        R.wait("tensor", esm["s_apply"])
                    for tt in range(4):
                        for j in range(2):
                            fn = lambda e, c=c, tt=tt, j=j, tc=tc, tg=tg, par=par: e.matmul(
                                ps_all[:, 512 * (2 * tt + j):512 * (2 * tt + j + 1)],
                                A_[:, 1024 * c + 128 * (tg * 4 + tt):1024 * c + 128 * (tg * 4 + tt) + 128],
                                W3B[:, 1024 * par + 512 * j:1024 * par + 512 * (j + 1)],
                                start=(c == 0), stop=(c == 31))
                            if tt == 3 and j == 1:
                                R.inc("tensor", fn, mmW3c)
                            else:
                                R.op("tensor", fn)
                R.wait("vector", mmW3c)
                for tt in range(4):
                    t_glob = tc * 8 + tg * 4 + tt
                    emit_epilogue_tile(
                        R, ep, esm, seq, t_glob,
                        (lambda tt=tt: ps_all[:, 1024 * tt:1024 * (tt + 1)]),
                        x_dst, scale_bcast=W3SB)
                    if t_glob < NT - 1:
                        epi_prefetch_x(R, ep, esm, x_src, seq, t_glob + 1)
    R.wait("gpsimd", esm["st"])





P1_DEBUG = {"stage": ""}


def emit_phase1(R, g, x_src, x_dst):
    R.init_epsc()
    dbg = P1_DEBUG["stage"]
    cl = R.sem("p1cl")
    ident = R.sb("p1ident", [128, 128], BF16)
    MASKD = R.sb("p1MASKD", [128, 128], F32)
    R.inc("sync", lambda e: e.dma_start(out=ident[:], in_=g["ident_bf"][:, :]), cl, 16)
    R.inc("sync", lambda e: e.dma_start(out=MASKD[:], in_=g["p1_mask"][:, 0:128]), cl, 16)
    for e in ENGINES:
        R.wait(e, cl)

    ps_all = R.ps("p1ps", [128, 4096], F32)
    ps_bf = ps_all[:, 0:1024].bitcast(BF16)
    psTv = ps_all[:, 3072:4096].bitcast(BF16)
    U_fm = R.sb("p1U", [128, 8 * 2048], BF16)
    AO = U_fm
    RQ = R.sb("p1RQ", [128, 8 * 2048], BF16)
    RK = R.sb("p1RK", [128, 8 * 2048], BF16)
    VT = R.sb("p1VT", [128, 16 * 1024], BF16)
    SCO = R.sb("p1SCO", [128, 4 * 2048], F32)
    CC = SCO[:, 0:2048]
    SS = SCO[:, 2048:4096]
    PBF = R.sb("p1PBF", [128, 4 * 2048], BF16)
    PTS = R.sb("p1PTS", [128, 2 * 512], BF16)
    RTMP = R.sb("p1RTMP", [128, 2 * 512], F32)
    VSB = RTMP[:, 0:1024].bitcast(BF16)
    QWB = R.sb("p1QWB", [128, 2 * 2048], BF16)
    WOB = QWB[:, 0:2048]
    SMT = R.sb("p1SMT", [128, 16], F32)
    pt = mk_preln_tensors(R, "p1pl")
    pt.update(ps_bf=ps_bf, ident=ident)
    psm = mk_preln_sems(R, "p1pl")
    ep = mk_epi_tensors(R, "p1ep", None, pt["XT"][:, 0:1024])
    esm = mk_epi_sems(R, "p1ep")
    qw = R.sem("p1qw")
    mmQ = R.sem("p1mmQ")
    rpD = R.sem("p1rpD")
    vE = R.sem("p1vE")
    mmS = R.sem("p1mmS")
    sevE = R.sem("p1sevE")
    sf_d2a = R.sem("p1sfd2a")
    sf_a2d = R.sem("p1sfa2d")
    pN = R.sem("p1pN")
    mmPT = R.sem("p1mmPT")
    ptE = R.sem("p1ptE")
    mmAV = R.sem("p1mmAV")
    aoE = R.sem("p1aoE")
    wol = R.sem("p1wol")
    mmPc = R.sem("p1mmPc")
    ccl = R.sem("p1ccl")

    for seq in range(1 if dbg else NSEQ):
        if seq > 0:
            R.wait("sync", esm["st"])
            R.wait("vector", mmQ)
            R.wait("vector", mmPc)
            R.wait("tensor", esm["s_apply"])
            R.wait("tensor", esm["st"])
        emit_preln(R, pt, psm, x_src, seq, U_fm=U_fm)

        R.wait("sync", sf_a2d)
        R.wait("sync", sevE)
        R.inc("sync", lambda e: e.dma_start(out=CC, in_=g["p1_cc"][:, :]), ccl, 16)
        R.inc("sync", lambda e: e.dma_start(out=SS, in_=g["p1_ss"][:, :]), ccl, 16)
        R.wait("vector", ccl)
        R.wait("tensor", psm["s_ev"])
        for pi in range(8):
            slot = pi % 2
            dst = RQ if pi < 4 else RK
            gcol = (pi % 4) * 2 * 2048
            R.wait_rel("gpsimd", mmQ, 4)
            R.inc("gpsimd", lambda e, pi=pi, slot=slot: e.dma_start(
                out=QWB[:, 2048 * slot:2048 * slot + 1024], in_=g["p1_qkv"][:, 2048 * pi:2048 * pi + 1024]), qw, 16)
            R.inc("gpsimd", lambda e, pi=pi, slot=slot: e.dma_start(
                out=QWB[:, 2048 * slot + 1024:2048 * slot + 2048], in_=g["p1_qkv"][:, 2048 * pi + 1024:2048 * pi + 2048]), qw, 16)
            for p in range(4):
                pp = 4 * pi + p
                bA = 2 * (pp % 2)
                if p == 0:
                    R.wait("tensor", qw)
                R.wait_rel("tensor", rpD, 1)
                for ab in range(2):
                    for d in range(8):
                        fn = lambda e, p=p, d=d, ab=ab, slot=slot, bA=bA: e.matmul(
                            ps_all[:, 512 * (bA + ab):512 * (bA + ab + 1)],
                            QWB[:, 2048 * slot + 1024 * ab + 128 * d:2048 * slot + 1024 * ab + 128 * (d + 1)],
                            U_fm[:, 2048 * d + 512 * p:2048 * d + 512 * (p + 1)],
                            start=(d == 0), stop=(d == 7))
                        if ab == 1 and d == 7:
                            R.inc("tensor", fn, mmQ)
                        else:
                            R.op("tensor", fn)
                R.wait("vector", mmQ)
                psA = lambda bA=bA: ps_all[:, 512 * bA:512 * (bA + 1)]
                psB = lambda bA=bA: ps_all[:, 512 * (bA + 1):512 * (bA + 2)]
                ccp = lambda p=p: CC[:, 512 * p:512 * (p + 1)]
                ssp = lambda p=p: SS[:, 512 * p:512 * (p + 1)]
                R.op("vector", lambda e, psA=psA, ccp=ccp: e.tensor_tensor(RTMP[:, 0:512], ccp(), psA(), op=ALU.mult))
                R.op("vector", lambda e, psB=psB, ssp=ssp: e.tensor_tensor(RTMP[:, 512:1024], ssp(), psB(), op=ALU.mult))
                R.op("vector", lambda e: e.drain())
                R.op("vector", lambda e, p=p, gcol=gcol, dst=dst: e.tensor_tensor(
                    dst[:, gcol + 512 * p:gcol + 512 * (p + 1)], RTMP[:, 0:512], RTMP[:, 512:1024], op=ALU.subtract))
                R.op("vector", lambda e: e.drain())
                R.op("vector", lambda e, psA=psA, ssp=ssp: e.tensor_tensor(RTMP[:, 0:512], ssp(), psA(), op=ALU.mult))
                R.op("vector", lambda e, psB=psB, ccp=ccp: e.tensor_tensor(RTMP[:, 512:1024], ccp(), psB(), op=ALU.mult))
                R.op("vector", lambda e: e.drain())
                R.inc("vector", lambda e, p=p, gcol=gcol, dst=dst: e.tensor_tensor(
                    dst[:, gcol + 2048 + 512 * p:gcol + 2048 + 512 * (p + 1)], RTMP[:, 0:512], RTMP[:, 512:1024], op=ALU.add),
                    rpD)

        for vc in range(8):
            slot = vc % 2
            R.wait_rel("gpsimd", mmQ, 4)
            R.inc("gpsimd", lambda e, vc=vc, slot=slot: e.dma_start(
                out=QWB[:, 2048 * slot:2048 * slot + 1024], in_=g["p1_qkv"][:, 1024 * (16 + vc):1024 * (17 + vc)]), qw, 16)
            for p in range(4):
                pp2 = 4 * vc + p
                bk = 4 + (pp2 % 2)
                if p == 0:
                    R.wait("tensor", qw)
                    if vc == 0:
                        R.wait("tensor", rpD)
                R.wait_rel("tensor", vE, 1)
                for d in range(8):
                    fn = lambda e, p=p, d=d, slot=slot, bk=bk: e.matmul(
                        ps_all[:, 512 * bk:512 * (bk + 1)],
                        QWB[:, 2048 * slot + 128 * d:2048 * slot + 128 * (d + 1)],
                        U_fm[:, 2048 * d + 512 * p:2048 * d + 512 * (p + 1)],
                        start=(d == 0), stop=(d == 7))
                    if d == 7:
                        R.inc("tensor", fn, mmQ)
                    else:
                        R.op("tensor", fn)
                R.wait("scalar", mmQ)
                if p == 0:
                    R.wait("scalar", mmPT)
                if p == 0 and vc == 0:
                    R.wait("scalar", rpD)
                R.inc("scalar", lambda e, p=p, bk=bk: e.activation(
                    out=VSB[:, 512 * p:512 * (p + 1)], in_=ps_all[:, 512 * bk:512 * (bk + 1)], func=AF.Identity),
                    vE)
            R.wait("tensor", vE)
            for j in range(2):
                TB = mmPT.n % 2
                R.wait_rel("tensor", ptE, 1)
                for i in range(8):
                    fn = lambda e, j=j, i=i, TB=TB: e.transpose(
                        psTv[:, 1024 * TB + 128 * i:1024 * TB + 128 * (i + 1)],
                        VSB[:, 128 * (8 * j + i):128 * (8 * j + i + 1)], ident[:, :])
                    if i == 7:
                        R.inc("tensor", fn, mmPT)
                    else:
                        R.op("tensor", fn)
                R.wait("vector", mmPT)
                R.inc("vector", lambda e, j=j, vc=vc, TB=TB: e.tensor_copy(
                    VT[:, :].rearrange("p (kt r) -> p kt r", kt=16)[:, 8 * j:8 * (j + 1), 128 * vc:128 * (vc + 1)],
                    psTv[:, 1024 * TB:1024 * (TB + 1)].rearrange("p (k r) -> p k r", k=8)), ptE)

        if dbg == "qkv":
            dsem = R.sem("p1dbg")
            R.wait("gpsimd", rpD)
            R.wait("gpsimd", ptE)
            for nm2, buf in [("d_rq", RQ), ("d_rk", RK), ("d_vt", VT), ("d_u", U_fm)]:
                R.inc("gpsimd", lambda e, nm2=nm2, buf=buf: e.dma_start(out=g[nm2][:, :], in_=buf[:, :]), dsem, 16)
            R.wait("gpsimd", dsem)
            return
        R.wait("tensor", rpD)
        R.wait("tensor", ptE)
        R.wait("tensor", vE)
        for gi in range(4):
            for t in range(NT):
                kpn = t // 4 + 1
                for kp in range(kpn):
                    width = min(512, 128 * (t + 1) - 512 * kp)
                    R.wait("tensor", sevE)
                    for h in range(4):
                        for ab in range(2):
                            fn = lambda e, h=h, ab=ab, gi=gi, t=t, kp=kp, width=width: e.matmul(
                                ps_all[:, 512 * h:512 * h + width],
                                RQ[32 * h:32 * (h + 1), (2 * gi + ab) * 2048 + 128 * t:(2 * gi + ab) * 2048 + 128 * (t + 1)],
                                RK[32 * h:32 * (h + 1), (2 * gi + ab) * 2048 + 512 * kp:(2 * gi + ab) * 2048 + 512 * kp + width],
                                start=(ab == 0), stop=(ab == 1), tile_position=(32 * h, 0))
                            if h == 3 and ab == 1:
                                R.inc("tensor", fn, mmS)
                            else:
                                R.op("tensor", fn)
                    R.wait("vector", mmS)
                    if kp == 0:
                        R.wait("vector", sf_a2d)
                        R.wait("vector", rpD)
                    for h in range(4):
                        if kp < t // 4:
                            fn = lambda e, h=h, kp=kp: e.tensor_copy(
                                SCO[:, 2048 * h + 512 * kp:2048 * h + 512 * (kp + 1)], ps_all[:, 512 * h:512 * (h + 1)])
                        else:
                            vloc = t % 4
                            if vloc > 0:
                                R.op("vector", lambda e, h=h, kp=kp, vloc=vloc: e.tensor_copy(
                                    SCO[:, 2048 * h + 512 * kp:2048 * h + 512 * kp + 128 * vloc],
                                    ps_all[:, 512 * h:512 * h + 128 * vloc]))
                            fn = lambda e, h=h, kp=kp, vloc=vloc, width=width: e.tensor_tensor(
                                SCO[:, 2048 * h + 512 * kp + 128 * vloc:2048 * h + 512 * kp + width],
                                ps_all[:, 512 * h + 128 * vloc:512 * h + width],
                                MASKD[:, 0:width - 128 * vloc], op=ALU.add)
                        if h == 3:
                            R.inc("vector", fn, sevE)
                        else:
                            R.op("vector", fn)
                L = 128 * (t + 1)
                R.wait("scalar", mmPT)
                for h in range(4):
                    R.op("vector", lambda e, h=h, L=L: e.reduce_max(
                        SMT[:, h:h + 1], SCO[:, 2048 * h:2048 * h + L], axis=AX.X, negate=True))
                    R.op("vector", lambda e: e.drain())
                    R.inc("vector", lambda e, h=h: e.tensor_scalar(
                        SMT[:, 4 + h:5 + h], SMT[:, h:h + 1], 0.125, None, op0=ALU.mult), sf_d2a)
                    R.wait("scalar", sf_d2a)
                    R.inc("scalar", lambda e, h=h, L=L: e.activation(
                        out=PBF[:, 2048 * h:2048 * h + L], in_=SCO[:, 2048 * h:2048 * h + L],
                        func=AF.Exp, bias=SMT[:, 4 + h:5 + h], scale=0.125), sf_a2d)
                    R.wait("vector", sf_a2d)
                    R.op("vector", lambda e, h=h, L=L: e.reduce_sum(
                        SMT[:, 8 + h:9 + h], PBF[:, 2048 * h:2048 * h + L], axis=AX.X))
                    R.op("vector", lambda e: e.drain())
                    R.op("vector", lambda e, h=h: e.reciprocal(SMT[:, 12 + h:13 + h], SMT[:, 8 + h:9 + h]))
                    R.op("vector", lambda e: e.drain())
                    R.inc("vector", lambda e, h=h, L=L: e.tensor_scalar(
                        PBF[:, 2048 * h:2048 * h + L], PBF[:, 2048 * h:2048 * h + L],
                        SMT[:, 12 + h:13 + h], None, op0=ALU.mult), pN)
                R.wait("tensor", pN)
                R.wait("tensor", aoE)

                def emit_T(kt):
                    TB = mmPT.n % 2
                    R.wait_rel("tensor", ptE, 1)
                    for h in range(4):
                        fn = lambda e, kt=kt, h=h, TB=TB: e.transpose(
                            psTv[:, 1024 * TB + 128 * h:1024 * TB + 128 * (h + 1)],
                            PBF[:, 2048 * h + 128 * kt:2048 * h + 128 * (kt + 1)], ident[:, :])
                        if h == 3:
                            R.inc("tensor", fn, mmPT)
                        else:
                            R.op("tensor", fn)
                    R.wait("vector", mmPT)
                    R.inc("vector", lambda e, TB=TB, kt=kt: e.tensor_copy(
                        PTS[:, 512 * (kt % 2):512 * (kt % 2 + 1)], psTv[:, 1024 * TB:1024 * TB + 512]), ptE)

                def emit_AV(kt, t, gi):
                    pb = kt % 2
                    R.wait("tensor", ptE)
                    for pr in range(2):
                        for cg in range(2):
                            fn = lambda e, kt=kt, t=t, gi=gi, pr=pr, cg=cg, pb=pb: e.matmul(
                                ps_all[64 * cg:64 * (cg + 1), 2048 + 512 * pr:2048 + 512 * pr + 128],
                                VT[:, (kt * 8 + 2 * gi + pr) * 128 + 64 * cg:(kt * 8 + 2 * gi + pr) * 128 + 64 * (cg + 1)],
                                PTS[:, 512 * pb + 128 * (2 * pr + cg):512 * pb + 128 * (2 * pr + cg + 1)],
                                start=(kt == 0), stop=(kt == t), tile_position=(0, 64 * cg))
                            if pr == 1 and cg == 1 and kt == t:
                                R.inc("tensor", fn, mmAV)
                            else:
                                R.op("tensor", fn)

                for kt in range(t + 1):
                    emit_T(kt)
                    if kt > 0:
                        emit_AV(kt - 1, t, gi)
                emit_AV(t, t, gi)
                R.wait("vector", mmAV)
                R.op("vector", lambda e, gi=gi, t=t: e.tensor_copy(
                    AO[:, (2 * gi) * 2048 + 128 * t:(2 * gi) * 2048 + 128 * (t + 1)], ps_all[:, 2048:2176]))
                R.inc("vector", lambda e, gi=gi, t=t: e.tensor_copy(
                    AO[:, (2 * gi + 1) * 2048 + 128 * t:(2 * gi + 1) * 2048 + 128 * (t + 1)], ps_all[:, 2560:2688]), aoE)

        if dbg == "attn":
            dsem = R.sem("p1dbg")
            R.wait("gpsimd", aoE)
            R.inc("gpsimd", lambda e: e.dma_start(out=g["d_rq"][:, :], in_=AO[:, :]), dsem, 16)
            R.wait("gpsimd", dsem)
            return
        R.wait("tensor", aoE)
        R.wait("tensor", ptE)
        R.wait("gpsimd", mmQ)
        for tgi in range(4):
            for c in range(8):
                R.wait_rel("gpsimd", mmPc, 1)
                R.inc("gpsimd", lambda e, c=c: e.dma_start(
                    out=WOB[:, 1024 * (c % 2):1024 * (c % 2 + 1)], in_=g["p1_wo"][:, 1024 * c:1024 * (c + 1)]), wol, 16)
                R.wait("tensor", wol)
                if c == 0:
                    R.wait("tensor", esm["s_apply"])
                for tt in range(4):
                    for j in range(2):
                        fn = lambda e, c=c, tt=tt, j=j, tgi=tgi: e.matmul(
                            ps_all[:, 512 * (2 * tt + j):512 * (2 * tt + j + 1)],
                            AO[:, 2048 * c + 128 * (4 * tgi + tt):2048 * c + 128 * (4 * tgi + tt) + 128],
                            WOB[:, 1024 * (c % 2) + 512 * j:1024 * (c % 2) + 512 * (j + 1)],
                            start=(c == 0), stop=(c == 7))
                        if tt == 3 and j == 1:
                            R.inc("tensor", fn, mmPc)
                        else:
                            R.op("tensor", fn)
            R.wait("vector", mmPc)
            if tgi == 0:
                R.wait("sync", psm["s_app"])
            for tt in range(4):
                t_glob = tgi * 4 + tt
                if t_glob == 0:
                    epi_prefetch_x(R, ep, esm, x_src, seq, 0)
                emit_epilogue_tile(
                    R, ep, esm, seq, t_glob,
                    (lambda tt=tt: ps_all[:, 1024 * tt:1024 * (tt + 1)]),
                    x_dst, direct_sbuf=True)
                if t_glob < NT - 1:
                    epi_prefetch_x(R, ep, esm, x_src, seq, t_glob + 1)
    R.wait("gpsimd", esm["st"])




def emit_phase3(R, g, x_src, x_dst):
    R.init_epsc()
    cl = R.sem("p3cl")
    ident = R.sb("p3ident", [128, 128], BF16)
    MASKD = R.sb("p3MASKD", [128, 128], F32)
    W3Q = R.sb("p3W", [128, 192], BF16)
    B3Q = R.sb("p3B", [128, 3], F32)
    PROJB = R.sb("p3PJB", [128, 1024], F32)
    R.inc("sync", lambda e: e.dma_start(out=ident[:], in_=g["ident_bf"][:, :]), cl, 16)
    R.inc("sync", lambda e: e.dma_start(out=MASKD[:], in_=g["p1_mask"][:, 0:128]), cl, 16)
    R.inc("gpsimd", lambda e: e.dma_start(out=W3Q[:], in_=g["p3_wqkv"][:, :]), cl, 16)
    R.inc("sync", lambda e: e.dma_start(out=B3Q[:], in_=g["p3_bqkv"][:, :]), cl, 16)
    R.inc("sync", lambda e: e.dma_start(out=PROJB[:], in_=g["p3_projb"][:, :]), cl, 16)
    for e in ENGINES:
        R.wait(e, cl)

    ps_all = R.ps("p3ps", [128, 4096], F32)
    ps_bf = ps_all[:, 0:1024].bitcast(BF16)
    psTv = ps_all[:, 3072:4096].bitcast(BF16)
    U_fm = R.sb("p3U", [128, 8 * 2048], BF16)
    AO = U_fm
    Q_fm = R.sb("p3Q", [128, 8 * 2048], BF16)
    K_fm = R.sb("p3K", [128, 8 * 2048], BF16)
    VT = R.sb("p3VT", [128, 16 * 1024], BF16)
    SCO = R.sb("p3SCO", [128, 2 * 2048], F32)
    PBF = R.sb("p3PBF", [128, 2 * 2048], BF16)
    PTS = R.sb("p3PTS", [128, 2 * 256], BF16)
    VSB = R.sb("p3VSB", [128, 2048], BF16)
    WOB = R.sb("p3WOB", [128, 2 * 1024], BF16)
    SMT = R.sb("p3SMT", [128, 8], F32)
    pt = mk_preln_tensors(R, "p3pl")
    pt.update(ps_bf=ps_bf, ident=ident)
    psm = mk_preln_sems(R, "p3pl")
    ep = mk_epi_tensors(R, "p3ep", R.sb("p3DSB", [128, 1024], F32), pt["XT"][:, 0:1024])
    esm = mk_epi_sems(R, "p3ep")
    mmQ = R.sem("p3mmQ")
    qkE = R.sem("p3qkE")
    mmPT = R.sem("p3mmPT")
    ptE = R.sem("p3ptE")
    mmS = R.sem("p3mmS")
    sevE = R.sem("p3sevE")
    sf_d2a = R.sem("p3sfd2a")
    sf_a2d = R.sem("p3sfa2d")
    pN = R.sem("p3pN")
    mmAV = R.sem("p3mmAV")
    aoE = R.sem("p3aoE")
    wol = R.sem("p3wol")
    mmPc = R.sem("p3mmPc")

    for seq in range(NSEQ):
        if seq > 0:
            R.wait("sync", esm["st"])
            R.wait("vector", mmQ)
            R.wait("vector", mmPc)
            R.wait("tensor", esm["s_apply"])
            R.wait("tensor", esm["st"])
        emit_preln(R, pt, psm, x_src, seq, U_fm=U_fm)

        R.wait("tensor", psm["s_ev"])
        for pr in range(8):
            for p in range(4):
                pp = 4 * pr + p
                bq = 3 * (pp % 2)
                R.wait_rel("tensor", qkE, 3)
                for b in range(2):
                    base = 64 * b
                    for w in range(3):
                        fn = lambda e, pr=pr, p=p, b=b, w=w, bq=bq, base=base: e.matmul(
                            ps_all[base:base + 64, 512 * (bq + w):512 * (bq + w) + 512],
                            W3Q[base:base + 64, 64 * w:64 * (w + 1)],
                            U_fm[base:base + 64, pr * 2048 + 512 * p:pr * 2048 + 512 * (p + 1)],
                            start=True, stop=True, tile_position=(base, base))
                        if b == 1 and w == 2:
                            R.inc("tensor", fn, mmQ)
                        else:
                            R.op("tensor", fn)
                R.wait("scalar", mmQ)
                if p == 0:
                    R.wait("scalar", mmPT)
                R.op("scalar", lambda e, pr=pr, p=p, bq=bq: e.activation(
                    out=Q_fm[:, pr * 2048 + 512 * p:pr * 2048 + 512 * (p + 1)],
                    in_=ps_all[:, 512 * bq:512 * (bq + 1)], func=AF.Identity, bias=B3Q[:, 0:1]))
                R.op("scalar", lambda e, pr=pr, p=p, bq=bq: e.activation(
                    out=K_fm[:, pr * 2048 + 512 * p:pr * 2048 + 512 * (p + 1)],
                    in_=ps_all[:, 512 * (bq + 1):512 * (bq + 2)], func=AF.Identity, bias=B3Q[:, 1:2]))
                R.inc("scalar", lambda e, p=p, bq=bq: e.activation(
                    out=VSB[:, 512 * p:512 * (p + 1)],
                    in_=ps_all[:, 512 * (bq + 2):512 * (bq + 3)], func=AF.Identity, bias=B3Q[:, 2:3]), qkE, 3)
            R.wait("tensor", qkE)
            for j in range(2):
                TB = mmPT.n % 2
                R.wait_rel("tensor", ptE, 1)
                for i in range(8):
                    fn = lambda e, j=j, i=i, TB=TB: e.transpose(
                        psTv[:, 1024 * TB + 128 * i:1024 * TB + 128 * (i + 1)],
                        VSB[:, 128 * (8 * j + i):128 * (8 * j + i + 1)], ident[:, :])
                    if i == 7:
                        R.inc("tensor", fn, mmPT)
                    else:
                        R.op("tensor", fn)
                R.wait("vector", mmPT)
                R.inc("vector", lambda e, j=j, pr=pr, TB=TB: e.tensor_copy(
                    VT[:, :].rearrange("p (kt r) -> p kt r", kt=16)[:, 8 * j:8 * (j + 1), 128 * pr:128 * (pr + 1)],
                    psTv[:, 1024 * TB:1024 * (TB + 1)].rearrange("p (k r) -> p k r", k=8)), ptE)

        R.wait("tensor", qkE)
        R.wait("tensor", ptE)
        for pr in range(8):
            for t in range(NT):
                kpn = t // 4 + 1
                for kp in range(kpn):
                    width = min(512, 128 * (t + 1) - 512 * kp)
                    R.wait("tensor", sevE)
                    for b in range(2):
                        fn = lambda e, pr=pr, t=t, kp=kp, width=width, b=b: e.matmul(
                            ps_all[:, 512 * b:512 * b + width],
                            Q_fm[64 * b:64 * (b + 1), pr * 2048 + 128 * t:pr * 2048 + 128 * (t + 1)],
                            K_fm[64 * b:64 * (b + 1), pr * 2048 + 512 * kp:pr * 2048 + 512 * kp + width],
                            start=True, stop=True, tile_position=(64 * b, 0))
                        if b == 1:
                            R.inc("tensor", fn, mmS)
                        else:
                            R.op("tensor", fn)
                    R.wait("vector", mmS)
                    if kp == 0:
                        R.wait("vector", sf_a2d)
                    for b in range(2):
                        if kp < t // 4:
                            fn = lambda e, b=b, kp=kp: e.tensor_copy(
                                SCO[:, 2048 * b + 512 * kp:2048 * b + 512 * (kp + 1)], ps_all[:, 512 * b:512 * (b + 1)])
                        else:
                            vloc = t % 4
                            if vloc > 0:
                                R.op("vector", lambda e, b=b, kp=kp, vloc=vloc: e.tensor_copy(
                                    SCO[:, 2048 * b + 512 * kp:2048 * b + 512 * kp + 128 * vloc],
                                    ps_all[:, 512 * b:512 * b + 128 * vloc]))
                            fn = lambda e, b=b, kp=kp, vloc=vloc, width=width: e.tensor_tensor(
                                SCO[:, 2048 * b + 512 * kp + 128 * vloc:2048 * b + 512 * kp + width],
                                ps_all[:, 512 * b + 128 * vloc:512 * b + width],
                                MASKD[:, 0:width - 128 * vloc], op=ALU.add)
                        if b == 1:
                            R.inc("vector", fn, sevE)
                        else:
                            R.op("vector", fn)
                L = 128 * (t + 1)
                R.wait("scalar", mmPT)
                for b in range(2):
                    R.op("vector", lambda e, b=b, L=L: e.reduce_max(
                        SMT[:, b:b + 1], SCO[:, 2048 * b:2048 * b + L], axis=AX.X, negate=True))
                    R.op("vector", lambda e: e.drain())
                    R.inc("vector", lambda e, b=b: e.tensor_scalar(
                        SMT[:, 2 + b:3 + b], SMT[:, b:b + 1], 0.125, None, op0=ALU.mult), sf_d2a)
                    R.wait("scalar", sf_d2a)
                    R.inc("scalar", lambda e, b=b, L=L: e.activation(
                        out=PBF[:, 2048 * b:2048 * b + L], in_=SCO[:, 2048 * b:2048 * b + L],
                        func=AF.Exp, bias=SMT[:, 2 + b:3 + b], scale=0.125), sf_a2d)
                    R.wait("vector", sf_a2d)
                    R.op("vector", lambda e, b=b, L=L: e.reduce_sum(
                        SMT[:, 4 + b:5 + b], PBF[:, 2048 * b:2048 * b + L], axis=AX.X))
                    R.op("vector", lambda e: e.drain())
                    R.op("vector", lambda e, b=b: e.reciprocal(SMT[:, 6 + b:7 + b], SMT[:, 4 + b:5 + b]))
                    R.op("vector", lambda e: e.drain())
                    R.inc("vector", lambda e, b=b, L=L: e.tensor_scalar(
                        PBF[:, 2048 * b:2048 * b + L], PBF[:, 2048 * b:2048 * b + L],
                        SMT[:, 6 + b:7 + b], None, op0=ALU.mult), pN)
                R.wait("tensor", pN)
                R.wait("tensor", aoE)

                def emit_T3(kt):
                    TB = mmPT.n % 2
                    R.wait_rel("tensor", ptE, 1)
                    for b in range(2):
                        fn = lambda e, kt=kt, b=b, TB=TB: e.transpose(
                            psTv[:, 1024 * TB + 128 * b:1024 * TB + 128 * (b + 1)],
                            PBF[:, 2048 * b + 128 * kt:2048 * b + 128 * (kt + 1)], ident[:, :])
                        if b == 1:
                            R.inc("tensor", fn, mmPT)
                        else:
                            R.op("tensor", fn)
                    R.wait("vector", mmPT)
                    R.inc("vector", lambda e, TB=TB, kt=kt: e.tensor_copy(
                        PTS[:, 256 * (kt % 2):256 * (kt % 2 + 1)], psTv[:, 1024 * TB:1024 * TB + 256]), ptE)

                def emit_AV3(kt, t, pr):
                    pb = kt % 2
                    R.wait("tensor", ptE)
                    for b in range(2):
                        fn = lambda e, kt=kt, t=t, pr=pr, b=b, pb=pb: e.matmul(
                            ps_all[64 * b:64 * (b + 1), 2048:2176],
                            VT[:, (kt * 8 + pr) * 128 + 64 * b:(kt * 8 + pr) * 128 + 64 * (b + 1)],
                            PTS[:, 256 * pb + 128 * b:256 * pb + 128 * (b + 1)],
                            start=(kt == 0), stop=(kt == t), tile_position=(0, 64 * b))
                        if b == 1 and kt == t:
                            R.inc("tensor", fn, mmAV)
                        else:
                            R.op("tensor", fn)

                for kt in range(t + 1):
                    emit_T3(kt)
                    if kt > 0:
                        emit_AV3(kt - 1, t, pr)
                emit_AV3(t, t, pr)
                R.wait("vector", mmAV)
                R.inc("vector", lambda e, pr=pr, t=t: e.tensor_copy(
                    AO[:, pr * 2048 + 128 * t:pr * 2048 + 128 * (t + 1)], ps_all[:, 2048:2176]), aoE)

        R.wait("tensor", aoE)
        R.wait("tensor", ptE)
        for tgi in range(4):
            for c in range(8):
                R.wait_rel("gpsimd", mmPc, 1)
                R.inc("gpsimd", lambda e, c=c: e.dma_start(
                    out=WOB[:, 1024 * (c % 2):1024 * (c % 2 + 1)], in_=g["p3_proj"][:, 1024 * c:1024 * (c + 1)]), wol, 16)
                R.wait("tensor", wol)
                if c == 0:
                    R.wait("tensor", esm["s_apply"])
                for tt in range(4):
                    for j in range(2):
                        fn = lambda e, c=c, tt=tt, j=j, tgi=tgi: e.matmul(
                            ps_all[:, 512 * (2 * tt + j):512 * (2 * tt + j + 1)],
                            AO[:, 2048 * c + 128 * (4 * tgi + tt):2048 * c + 128 * (4 * tgi + tt) + 128],
                            WOB[:, 1024 * (c % 2) + 512 * j:1024 * (c % 2) + 512 * (j + 1)],
                            start=(c == 0), stop=(c == 7))
                        if tt == 3 and j == 1:
                            R.inc("tensor", fn, mmPc)
                        else:
                            R.op("tensor", fn)
            R.wait("vector", mmPc)
            if tgi == 0:
                R.wait("sync", psm["s_app"])
            for tt in range(4):
                t_glob = tgi * 4 + tt
                if t_glob == 0:
                    epi_prefetch_x(R, ep, esm, x_src, seq, 0)
                emit_epilogue_tile(
                    R, ep, esm, seq, t_glob,
                    (lambda tt=tt: ps_all[:, 1024 * tt:1024 * (tt + 1)]),
                    x_dst, bias_bcast=PROJB)
                if t_glob < NT - 1:
                    epi_prefetch_x(R, ep, esm, x_src, seq, t_glob + 1)
    R.wait("gpsimd", esm["st"])




def emit_phase4(R, g, x_src, x_dst):
    R.init_epsc()
    cl = R.sem("p4cl")
    IDENTF = R.sb("p4idf", [128, 128], F32)
    CENT = R.sb("p4cent", [128, 64], F32)
    CENTR = R.sb("p4centr", [128, 64], F32R)
    BETB = R.sb("p4betb", [128, 64], BF16)
    GAM = R.sb("p4gam", [128, 64], F32)
    B1IE = R.sb("p4b1i", [128, 32], F32)
    B1C = R.sb("p4b1c", [128, 32], F32)
    MA = R.sb("p4ma", [128, 128], F32)
    AFFS = R.sb("p4affs", [128, 8], F32)
    RMAL = R.sb("p4rmal", [128, 2], F32)
    ONESB = R.sb("p4ones", [1, 128], BF16)
    IDENTB = R.sb("p4idb", [128, 128], BF16)
    B2R = R.sb("p4b2r", [1, 512], BF16)
    R.inc("sync", lambda e: e.dma_start(out=IDENTF[:], in_=g["ident_f32"][:, :]), cl, 16)
    R.inc("sync", lambda e: e.dma_start(out=CENT[:], in_=g["p4_cent"][:, :]), cl, 16)
    R.inc("sync", lambda e: e.dma_start(out=GAM[:], in_=g["p4_gamma"][:, :]), cl, 16)

    R.inc("gpsimd", lambda e: e.dma_start(out=BETB[:], in_=g["p4_beta"][:, :]), cl, 16)
    R.inc("gpsimd", lambda e: e.dma_start(out=ONESB[:], in_=g["ones_row"][:, :]), cl, 16)
    R.inc("sync", lambda e: e.dma_start(out=IDENTB[:], in_=g["ident_bf"][:, :]), cl, 16)
    for e in ENGINES:
        R.wait(e, cl)
    R.op("vector", lambda e: e.tensor_copy(CENTR[:, :], CENT[:, :]))
    R.op("vector", lambda e: e.drain())

    ps_all = R.ps("p4ps", [128, 4096], F32)
    psXH = ps_all[:, 2048:2560].bitcast(BF16)
    u_acc = R.sb("p4ua", [128, 16 * 1024], F32)
    XHF = R.sb("p4xhf", [128, 8 * 2048], BF16)
    XS = R.sb("p4xs", [128, 16 * 1024], BF16)
    R_ = R.sb("p4r", [128, 32 * 1024], BF16)
    W1B = R.sb("p4w1b", [128, 2 * 1024], BF16)
    UF = R.sb("p4uf", [128, 1024], F32R)
    XH = R_[:, 2048:3072]
    pt = mk_preln_tensors(R, "p4pl", need_ufm=False, XT=XS[:, 0:4096].bitcast(F32))
    psm = mk_preln_sems(R, "p4pl")
    ep = mk_epi_tensors(R, "p4ep", None, pt["XT"][:, 0:1024], stats_from=pt)
    esm = mk_epi_sems(R, "p4ep")
    trU = R.sem("p4trU")
    trUe = R.sem("p4trUe")
    afM = R.sem("p4afM")
    roD = R.sem("p4roD")
    roA = R.sem("p4roA")
    trX = R.sem("p4trX")
    trXe = R.sem("p4trXe")
    w1l = R.sem("p4w1l")
    mmS1 = R.sem("p4mmS1")
    b1d = R.sem("p4b1d")
    relE = R.sem("p4relE")
    w2l = R.sem("p4w2l")
    b2l = R.sem("p4b2l")
    mmS2 = R.sem("p4mmS2")
    uaE = R.sem("p4uaE")

    for seq in range(NSEQ):
        if seq > 0:
            R.wait("sync", esm["st"])
            R.wait("sync", mmS1)
            R.wait("sync", mmS2)
            R.wait("sync", w2l)
            R.wait("tensor", uaE)
            R.wait("tensor", esm["st"])
            R.wait("vector", mmS2)
            R.wait("vector", esm["st"])
        emit_preln(R, pt, psm, x_src, seq, u_f32=u_acc)

        v = dict(BN6=pt["BN6"], MV=pt["MV"], SD=pt["SD"], RSTD=pt["RSTD"], NMR=pt["NMR"],
                 s_d2a=psm["s_d2a"], s_a2d=psm["s_a2d"])
        for t in range(NT):
            R.wait("vector", psm["s_app"], psm["s_app"].n - (NT - 1 - t))
            R.op("vector", lambda e, t=t: e.bn_stats(pt["BN6"][:, 12 * t:12 * t + 6], u_acc[:, 1024 * t:1024 * t + 512]))
            R.op("vector", lambda e, t=t: e.bn_stats(pt["BN6"][:, 12 * t + 6:12 * t + 12], u_acc[:, 1024 * t + 512:1024 * (t + 1)]))
            R.op("vector", lambda e: e.drain())
            ln_rstd_chain(R, v, t)
            R.wait("vector", trX)
            R.inc("vector", lambda e, t=t: e.tensor_scalar(
                XH[:, :], u_acc[:, 1024 * t:1024 * (t + 1)],
                pt["RSTD"][:, t:t + 1], pt["NMR"][:, t:t + 1], op0=ALU.mult, op1=ALU.add), psm["s_app"])
            par = t % 2
            R.wait("tensor", psm["s_app"], psm["s_app"].n - 1)
            R.wait_rel("tensor", trUe, 1)
            for d in range(8):
                fn = lambda e, t=t, d=d, par=par: e.transpose(
                    ps_all[:, 1024 * par + 128 * d:1024 * par + 128 * (d + 1)],
                    u_acc[:, 1024 * t + 128 * d:1024 * t + 128 * (d + 1)], IDENTF[:, :])
                if d == 7:
                    R.inc("tensor", fn, trU)
                else:
                    R.op("tensor", fn)
            R.wait("vector", trU)
            R.inc("vector", lambda e, par=par: e.tensor_copy(UF[:, :], ps_all[:, 1024 * par:1024 * (par + 1)]), trUe)
            R.wait("tensor", trUe)
            R.wait("tensor", roD)
            for d in range(8):
                fn = lambda e, d=d: e.matmul(
                    ps_all[:, 3584:3592], UF[:, 128 * d:128 * (d + 1)], CENTR[:, 8 * d:8 * (d + 1)],
                    start=(d == 0), stop=(d == 7))
                if d == 7:
                    R.inc("tensor", fn, afM)
                else:
                    R.op("tensor", fn)
            R.wait("vector", afM)
            R.op("vector", lambda e: e.tensor_copy(AFFS[:, :], ps_all[:, 3584:3592]))
            R.op("vector", lambda e: e.drain())
            R.op("vector", lambda e: e.reduce_max(RMAL[:, 0:1], AFFS[:, :], axis=AX.X))
            R.inc("vector", lambda e: e.drain(), roD)
            R.wait("scalar", roD)
            R.inc("scalar", lambda e: e.activation(out=RMAL[:, 1:2], in_=RMAL[:, 0:1], func=AF.Sigmoid), roA)
            R.wait("vector", roA)
            R.op("vector", lambda e, t=t: e.tensor_scalar(
                MA[:, 8 * t:8 * (t + 1)], AFFS[:, :], RMAL[:, 0:1], None, op0=ALU.is_ge))
            R.op("vector", lambda e: e.drain())
            R.op("vector", lambda e, t=t: e.tensor_scalar(
                MA[:, 8 * t:8 * (t + 1)], MA[:, 8 * t:8 * (t + 1)], RMAL[:, 1:2], None, op0=ALU.mult))
            R.wait("tensor", psm["s_app"])
            R.wait("tensor", trXe)
            for d in range(8):
                fn = lambda e, d=d: e.transpose(
                    psXH[:, 128 * d:128 * (d + 1)], XH[:, 128 * d:128 * (d + 1)], IDENTB[:, :])
                if d == 7:
                    R.inc("tensor", fn, trX)
                else:
                    R.op("tensor", fn)
            R.wait("vector", trX)
            R.inc("vector", lambda e, t=t: e.tensor_copy(
                XHF[:, :].rearrange("p (d s) -> p d s", d=8)[:, :, 128 * t:128 * (t + 1)],
                psXH[:, :].rearrange("p (d s) -> p d s", d=8)), trXe)

        R.wait("tensor", trXe)
        R.wait("tensor", trUe)
        R.wait("tensor", roD)
        for tc in range(2):
            for ex in range(8):
                R.wait("vector", mmS2)
                R.wait("vector", mmS1)
                for d in range(8):
                    fn = lambda e, d=d, ex=ex, tc=tc: e.tensor_scalar(
                        XS[:, 1024 * d:1024 * (d + 1)],
                        XHF[:, 2048 * d + 1024 * tc:2048 * d + 1024 * (tc + 1)],
                        GAM[:, 8 * ex + d:8 * ex + d + 1], None, op0=ALU.mult)
                    if d == 7:
                        R.inc("vector", fn, roD)
                    else:
                        R.op("vector", fn)
                R.wait("sync", b1d)
                R.inc("sync", lambda e, ex=ex: e.dma_start(out=B1IE[:, :], in_=g["p4_b1"][:, 32 * ex:32 * (ex + 1)]), cl, 16)
                R.wait("tensor", roD)
                R.wait("tensor", uaE)
                for c in range(32):
                    par = c % 2
                    R.wait_rel("gpsimd", mmS1, 1)
                    R.inc("gpsimd", lambda e, c=c, ex=ex, par=par: e.dma_start(
                        out=W1B[:, 1024 * par:1024 * (par + 1)],
                        in_=g["p4_w1"][:, 1024 * (32 * ex + c):1024 * (32 * ex + c + 1)]), w1l, 16)
                    R.wait("tensor", w1l)
                    bcol = 2048 if par == 0 else 3584
                    R.wait_rel("tensor", b1d, 1)
                    for d in range(8):
                        R.op("tensor", lambda e, d=d, ex=ex, par=par, bcol=bcol: e.matmul(
                            ps_all[:, bcol:bcol + 1], W1B[:, 1024 * par + 128 * d:1024 * par + 128 * (d + 1)],
                            BETB[:, 8 * ex + d:8 * ex + d + 1],
                            start=(d == 0), stop=(d == 7)))
                    R.wait_rel("tensor", relE, 1)
                    for pc in range(2):
                        for d in range(8):
                            fn = lambda e, c=c, pc=pc, d=d, par=par: e.matmul(
                                ps_all[:, 512 * (2 * par + pc):512 * (2 * par + pc + 1)],
                                W1B[:, 1024 * par + 128 * d:1024 * par + 128 * (d + 1)],
                                XS[:, 1024 * d + 512 * pc:1024 * d + 512 * (pc + 1)],
                                start=(d == 0), stop=(d == 7))
                            if pc == 1 and d == 7:
                                R.inc("tensor", fn, mmS1)
                            else:
                                R.op("tensor", fn)
                    R.wait("vector", mmS1)
                    if c == 0:
                        R.wait("vector", cl)
                    R.inc("vector", lambda e, c=c, bcol=bcol: e.tensor_tensor(
                        B1C[:, c:c + 1], ps_all[:, bcol:bcol + 1], B1IE[:, c:c + 1], op=ALU.add), b1d)
                    R.wait("scalar", b1d)
                    R.wait("scalar", mmS1)
                    for pc in range(2):
                        fn = lambda e, c=c, pc=pc, par=par: e.activation(
                            out=R_[:, 512 * (2 * c + pc):512 * (2 * c + pc + 1)],
                            in_=ps_all[:, 512 * (2 * par + pc):512 * (2 * par + pc + 1)],
                            func=AF.Relu, bias=B1C[:, c:c + 1])
                        if pc == 1:
                            R.inc("scalar", fn, relE)
                        else:
                            R.op("scalar", fn)
                R.wait("gpsimd", mmS1)
                for half in range(2):
                    if half == 1:
                        R.wait("gpsimd", mmS2)
                    R.inc("gpsimd", lambda e, ex=ex, half=half: e.dma_start(
                        out=B2R[:, :], in_=g["p4_b2"][0:1, 1024 * ex + 512 * half:1024 * ex + 512 * (half + 1)]), b2l, 16)
                    R.inc("gpsimd", lambda e, ex=ex, half=half: e.dma_start(
                        out=XS[:, :].rearrange("p (c s) -> p c s", c=32),
                        in_=g["p4_w2"][:, :].rearrange("p (E c h s) -> p E c h s", E=8, c=32, h=2)[:, ex, :, half, :]),
                        w2l, 16)
                    R.wait("tensor", w2l)
                    R.wait("tensor", relE)
                    R.wait("tensor", b2l)
                    if half == 0:
                        R.wait("tensor", b1d)
                    R.wait("tensor", uaE)
                    for c in range(32):
                        for tt in range(8):
                            R.op("tensor", lambda e, c=c, tt=tt: e.matmul(
                                ps_all[:, 512 * tt:512 * (tt + 1)],
                                R_[:, 512 * (2 * c + tt // 4) + 128 * (tt % 4):512 * (2 * c + tt // 4) + 128 * (tt % 4) + 128],
                                XS[:, 512 * c:512 * (c + 1)],
                                start=(c == 0), stop=False))
                    for tt in range(8):
                        fn = lambda e, tt=tt: e.matmul(
                            ps_all[:, 512 * tt:512 * (tt + 1)], ONESB[:, :], B2R[0:1, 0:512],
                            start=False, stop=True)
                        if tt == 7:
                            R.inc("tensor", fn, mmS2)
                        else:
                            R.op("tensor", fn)
                    R.wait("vector", mmS2)
                    for tt in range(8):
                        T = tc * 8 + tt
                        fn = lambda e, tt=tt, T=T, ex=ex, half=half: e.scalar_tensor_tensor(
                            u_acc[:, 1024 * T + 512 * half:1024 * T + 512 * (half + 1)],
                            ps_all[:, 512 * tt:512 * (tt + 1)],
                            MA[:, 8 * T + ex:8 * T + ex + 1],
                            u_acc[:, 1024 * T + 512 * half:1024 * T + 512 * (half + 1)],
                            op0=ALU.mult, op1=ALU.add)
                        if tt == 7:
                            R.inc("vector", fn, uaE)
                        else:
                            R.op("vector", fn)

        R.wait("vector", uaE)
        R.wait("sync", psm["s_app"])
        R.wait("sync", mmS2)
        R.wait("sync", w2l)
        for t in range(NT):
            if t == 0:
                epi_prefetch_x(R, ep, esm, x_src, seq, 0)
            emit_epilogue_tile(
                R, ep, esm, seq, t,
                (lambda t=t: u_acc[:, 1024 * t:1024 * (t + 1)]),
                x_dst, direct_sbuf=True)
            if t < NT - 1:
                epi_prefetch_x(R, ep, esm, x_src, seq, t + 1)
    R.wait("gpsimd", esm["st"])


PHASE_EMITTERS = {1: emit_phase1, 2: emit_phase2, 3: emit_phase3, 4: emit_phase4}


def _run(phase_list, imgs, x_full):
    nc = bass.Bass("TRN2", target_bir_lowering=False)
    g = {}
    for name, arr in imgs.items():
        dt = IMG_DTYPES.get(name, F32)
        g[name] = nc.dram_tensor(name, list(arr.shape), dt, kind="ExternalInput").ap()
    x_in = nc.dram_tensor("x_in", [NSEQ * S, D], F32, kind="ExternalInput").ap()
    x_out = nc.dram_tensor("x_out", [NSEQ * S, D], F32, kind="ExternalOutput").ap()
    n_ph = len(phase_list)
    scratch = [nc.dram_tensor(f"x_mid{i}", [NSEQ * S, D], F32, kind="Internal").ap()
               for i in range(max(0, n_ph - 1))]

    with ExitStack() as stack:
        R = Rec(nc, stack)
        for i, ph in enumerate(phase_list):
            src = x_in if i == 0 else scratch[i - 1]
            dst = x_out if i == n_ph - 1 else scratch[i]
            R.phase_begin()
            with ExitStack() as phase_stack:
                R.stack = phase_stack
                PHASE_EMITTERS[ph](R, g, src, dst)
                R.stack = stack
            if i < n_ph - 1:
                R.barrier(f"pb{i}")
        R.replay()
    nc.finalize()

    in_maps = []
    for core in range(NCORES):
        m = dict(imgs)
        m["x_in"] = np.ascontiguousarray(x_full[core * NSEQ * S:(core + 1) * NSEQ * S, :])
        in_maps.append(m)
    res = run_bass_kernel_spmd(nc, in_maps, core_ids=list(range(NCORES)))
    return np.concatenate([res.results[i]["x_out"] for i in range(NCORES)], axis=0)


def kernel(**inputs):
    inputs = {k: np.asarray(v) for k, v in inputs.items()}
    imgs = build_images(inputs)
    x = np.ascontiguousarray(inputs["L0_attn_x"].astype(np.float32).reshape(B * S, D))
    phase_groups = os.environ.get("MK_PHASES", "1,2,3,4").split("/")
    cur = x
    for grp in phase_groups:
        phl = [int(p) for p in grp.split(",")]
        cur = _run(phl, imgs, cur)
    return cur.reshape(B, S, D)


# version 05
# speedup vs baseline: 1.1575x; 1.1575x over previous
import os
import sys
import numpy as np
from contextlib import ExitStack

sys.path.insert(0, "/opt/trn_rl_repo")
import concourse.bass as bass
import concourse.mybir as mybir
from concourse.bass_utils import run_bass_kernel_spmd

F32 = mybir.dt.float32
F32R = mybir.dt.float32r
BF16 = mybir.dt.bfloat16
AF = mybir.ActivationFunctionType
ALU = mybir.AluOpType
AX = mybir.AxisListType

NCORES = 8
B, S, D = 32, 2048, 1024
NSEQ = B // NCORES
NT = S // 128
H, HD = 16, 64
DH = 4096
EPS = 1e-5

ENGINES = ("sync", "tensor", "vector", "scalar", "gpsimd")



def lhsT_image(W):
    O, I = W.shape
    Oc, Ic = O // 128, I // 128
    return np.ascontiguousarray(W.reshape(Oc, 128, Ic, 128).transpose(3, 0, 2, 1).reshape(128, Oc * Ic * 128))


def rhs_image(WT):
    K, O = WT.shape
    Kc = K // 128
    return np.ascontiguousarray(WT.reshape(Kc, 128, O).transpose(1, 0, 2).reshape(128, Kc * O))


def cols_image(v):
    n = v.shape[0] // 128
    return np.ascontiguousarray(v.reshape(n, 128).T)


def bcast_image(v):
    return np.ascontiguousarray(np.tile(np.asarray(v)[None, :], (128, 1)))


def build_images(inp):
    import ml_dtypes
    img = {}
    img["p2_w1"] = lhsT_image(inp["L0_ffn_w1_w"])
    img["p2_w2"] = lhsT_image(inp["L0_ffn_w2_w"])
    img["p2_h1"] = lhsT_image(inp["L0_ffn_h1_w"])
    img["p2_h2"] = lhsT_image(inp["L0_ffn_h2_w"])
    img["p2_w3"] = rhs_image(np.ascontiguousarray(inp["L0_ffn_w3_w"].T))
    img["p2_s12"] = np.concatenate([cols_image(inp["L0_ffn_w1_s"]),
                                    cols_image(inp["L0_ffn_w2_s"]),
                                    cols_image(inp["L0_ffn_h1_s"]),
                                    cols_image(inp["L0_ffn_h2_s"])], axis=1)
    img["p2_w3sb"] = bcast_image(inp["L0_ffn_w3_s"])

    perm_q = []
    for gi in range(4):
        for half in range(2):
            for hh in range(4):
                h = 4 * gi + hh
                for i in range(32):
                    perm_q.append(h * 64 + 2 * i + half)
    perm_q = np.array(perm_q)
    Wq = inp["L0_attn_W_qkv"][0:1024][perm_q]
    Wk = inp["L0_attn_W_qkv"][1024:2048][perm_q]
    Wv = inp["L0_attn_W_qkv"][2048:3072]
    img["p1_qkv"] = lhsT_image(np.concatenate([Wq, Wk, Wv], axis=0))
    img["p1_wo"] = rhs_image(np.ascontiguousarray(inp["L0_attn_W_o"].T))
    t = np.arange(S, dtype=np.float64)
    theta = 10000.0 ** (-2.0 * (np.arange(32, dtype=np.float64) / 64.0))
    ang = t[None, :] * theta[:, None]
    img["p1_cc"] = np.ascontiguousarray(np.tile(np.cos(ang).astype(np.float32), (4, 1)))
    img["p1_ss"] = np.ascontiguousarray(np.tile(np.sin(ang).astype(np.float32), (4, 1)))
    m = np.zeros((128, 4 * 512), dtype=np.float32)
    for v in range(4):
        blk = np.zeros((128, 512), dtype=np.float32)
        blk[:, v * 128:(v + 1) * 128] = np.triu(np.full((128, 128), -1e30, dtype=np.float32), 1)
        blk[:, (v + 1) * 128:] = -1e30
        m[:, v * 512:(v + 1) * 512] = blk
    img["p1_mask"] = m
    img["ident_bf"] = np.eye(128, dtype=ml_dtypes.bfloat16)
    img["ident_f32"] = np.eye(128, dtype=np.float32)

    w3q = np.concatenate(
        [np.ascontiguousarray(inp["L1_attn_Wq_w"].T),
         np.ascontiguousarray(inp["L1_attn_Wk_w"].T),
         np.ascontiguousarray(inp["L1_attn_Wv_w"].T)], axis=1)
    img["p3_wqkv"] = np.ascontiguousarray(np.tile(w3q, (2, 1)))
    b3q = np.stack([inp["L1_attn_Wq_b"], inp["L1_attn_Wk_b"], inp["L1_attn_Wv_b"]], axis=1)
    img["p3_bqkv"] = np.ascontiguousarray(np.tile(b3q, (2, 1)))
    img["p3_proj"] = rhs_image(np.ascontiguousarray(inp["L1_attn_proj_w"].T))
    img["p3_projb"] = bcast_image(inp["L1_attn_proj_b"])

    w1 = inp["L1_ffn_w1"]
    img["p4_w1"] = np.ascontiguousarray(
        w1.reshape(8, 8, 128, 32, 128).transpose(2, 0, 3, 1, 4).reshape(128, 8 * 32 * 8 * 128))
    w2 = inp["L1_ffn_w2"]
    img["p4_w2"] = np.ascontiguousarray(
        w2.reshape(8, 32, 128, 1024).transpose(2, 0, 1, 3).reshape(128, 8 * 32 * 1024))
    img["p4_b1"] = np.concatenate([cols_image(inp["L1_ffn_b1"][e]) for e in range(8)], axis=1)
    img["p4_beta"] = np.concatenate([cols_image(inp["L1_ffn_beta"][e]) for e in range(8)], axis=1)
    img["p4_gamma"] = np.concatenate([cols_image(inp["L1_ffn_gamma"][e]) for e in range(8)], axis=1)
    img["p4_b2"] = np.ascontiguousarray(inp["L1_ffn_b2"].reshape(1, 8 * 1024))
    cent = inp["L1_ffn_centroids"]
    img["p4_cent"] = np.ascontiguousarray(cent.T.reshape(8, 128, 8).transpose(1, 0, 2).reshape(128, 64))
    img["ones_row"] = np.ones((1, 128), dtype=np.float32)
    return img


IMG_DTYPES = {"ident_bf": BF16}



class Sem:
    __slots__ = ("name", "n", "h")

    def __init__(self, name):
        self.name = name
        self.n = 0
        self.h = None


class Rec:
    def __init__(self, nc, stack):
        self.nc = nc
        self.root_stack = stack
        self.stack = stack
        self.ops = {e: [] for e in ENGINES}
        self.nsems = 0
        self.epsc = None
        self._sem_pool = []
        self._sem_idx = 0

    def init_epsc(self):
        if self.epsc is None:
            old = self.stack
            self.stack = self.root_stack
            self.epsc = self.sb("epsc", [128, 1], F32)
            self.stack = old
            self.op("vector", lambda e: e.memset(self.epsc[:, :], float(EPS)))
            self.op("vector", lambda e: e.drain())

    def sem(self, name):
        if self._sem_idx < len(self._sem_pool):
            s = self._sem_pool[self._sem_idx]
        else:
            s = Sem(name)
            s.h = self.root_stack.enter_context(self.nc.semaphore(f"s{len(self._sem_pool)}"))
            self._sem_pool.append(s)
            self.nsems += 1
        self._sem_idx += 1
        return s

    def phase_begin(self):
        self._sem_idx = 0

    def sb(self, name, shape, dt=F32):
        return self.stack.enter_context(self.nc.sbuf_tensor(name, shape, dt))

    def ps(self, name, shape, dt=F32):
        return self.stack.enter_context(self.nc.psum_tensor(name, shape, dt))

    def op(self, eng, fn):
        self.ops[eng].append(fn)

    def inc(self, eng, fn, sem, k=1):
        sem.n += k
        self.ops[eng].append(lambda e, fn=fn, h=sem.h, k=k: fn(e).then_inc(h, k))

    def wait(self, eng, sem, val=None):
        v = sem.n if val is None else val
        if v <= 0:
            return
        self.ops[eng].append(lambda e, h=sem.h, v=v: e.wait_ge(h, v))

    def wait_rel(self, eng, sem, behind):
        self.wait(eng, sem, sem.n - behind)

    def attach_inc(self, eng, sem, k=1):
        sem.n += k
        old = self.ops[eng][-1]
        self.ops[eng][-1] = lambda e, old=old, h=sem.h, k=k: old(e).then_inc(h, k)

    def barrier(self, name):
        s = self.sem(name)
        for e in ENGINES:
            s.n += 1
            self.ops[e].append(lambda x, h=s.h: x.sem_inc(h, 1))
        tgt = s.n
        for e in ENGINES:
            self.wait(e, s, tgt)

    def replay(self):
        block = self.root_stack.enter_context(self.nc.Block())
        for ename in ENGINES:
            oplist = self.ops[ename]

            def body(eng, oplist=oplist):
                for fn in oplist:
                    fn(eng)
            getattr(block, ename)(body)



def ln_rstd_chain(R, v, t):
    R.op("vector", lambda e: e.drain())
    R.inc("vector", lambda e, t=t: e.bn_aggr(v["MV"][:, 2 * t:2 * t + 2], v["BN6"][:, 12 * t:12 * t + 12]),
          v["s_d2a"])
    R.wait("scalar", v["s_d2a"])
    R.op("scalar", lambda e, t=t: e.activation(out=v["SD"][:, t:t + 1], in_=v["MV"][:, 2 * t + 1:2 * t + 2],
                                               func=AF.Ln, bias=R.epsc[:, 0:1]))
    R.op("scalar", lambda e: e.drain())
    R.inc("scalar", lambda e, t=t: e.activation(out=v["RSTD"][:, t:t + 1], in_=v["SD"][:, t:t + 1],
                                                func=AF.Exp, scale=-0.5), v["s_a2d"])
    R.wait("vector", v["s_a2d"])


def mk_preln_sems(R, nm):
    return dict(xld=R.sem(nm + "xld"), s_d2a=R.sem(nm + "d2a"), s_a2d=R.sem(nm + "a2d"),
                s_app=R.sem(nm + "app"), s_tp=R.sem(nm + "tp"), s_ev=R.sem(nm + "ev"))


def mk_preln_tensors(R, nm, need_ufm=True, XT=None):
    d = dict(XT=XT if XT is not None else R.sb(nm + "XT", [128, 2 * 1024], F32),
             BN6=R.sb(nm + "BN6", [128, 12 * NT], F32), MV=R.sb(nm + "MV", [128, 2 * NT], F32),
             SD=R.sb(nm + "SD", [128, NT], F32), RSTD=R.sb(nm + "RSTD", [128, NT], F32),
             NMR=R.sb(nm + "NMR", [128, NT], F32))
    if need_ufm:
        d["UB"] = R.sb(nm + "UB", [128, 2 * 1024], BF16)
    return d


def emit_preln(R, pt, sm, x_src, seq, U_fm=None, u_f32=None):
    v = dict(BN6=pt["BN6"], MV=pt["MV"], SD=pt["SD"], RSTD=pt["RSTD"], NMR=pt["NMR"],
             s_d2a=sm["s_d2a"], s_a2d=sm["s_a2d"])
    for t in range(NT):
        sl = 1024 * (t % 2)
        R.wait_rel("sync", sm["s_app"], 1)
        R.inc("sync", lambda e, t=t, sl=sl: e.dma_start(out=pt["XT"][:, sl:sl + 1024],
                                                        in_=x_src[(seq * S + 128 * t):(seq * S + 128 * (t + 1)), :]),
              sm["xld"], 16)
        R.wait("vector", sm["xld"])
        R.op("vector", lambda e, t=t, sl=sl: e.bn_stats(pt["BN6"][:, 12 * t:12 * t + 6], pt["XT"][:, sl:sl + 512]))
        R.op("vector", lambda e, t=t, sl=sl: e.bn_stats(pt["BN6"][:, 12 * t + 6:12 * t + 12], pt["XT"][:, sl + 512:sl + 1024]))
        ln_rstd_chain(R, v, t)
        if u_f32 is not None:
            R.inc("vector", lambda e, t=t, sl=sl: e.tensor_scalar(
                u_f32[:, 1024 * t:1024 * (t + 1)], pt["XT"][:, sl:sl + 1024],
                pt["MV"][:, 2 * t:2 * t + 1], pt["RSTD"][:, t:t + 1], op0=ALU.subtract, op1=ALU.mult), sm["s_app"])
        if U_fm is not None:
            R.wait_rel("vector", sm["s_tp"], 1)
            R.inc("vector", lambda e, t=t, sl=sl: e.tensor_scalar(
                pt["UB"][:, sl:sl + 1024], pt["XT"][:, sl:sl + 1024],
                pt["MV"][:, 2 * t:2 * t + 1], pt["RSTD"][:, t:t + 1], op0=ALU.subtract, op1=ALU.mult), sm["s_app"])
            R.wait("tensor", sm["s_app"])
            R.wait_rel("tensor", sm["s_ev"], 1)
            for d in range(8):
                fn = lambda e, t=t, d=d, sl=sl: e.transpose(
                    pt["ps_bf"][:, sl + 128 * d:sl + 128 * (d + 1)],
                    pt["UB"][:, sl + 128 * d:sl + 128 * (d + 1)], pt["ident"][:, :])
                if d == 7:
                    R.inc("tensor", fn, sm["s_tp"])
                else:
                    R.op("tensor", fn)
            R.wait("vector", sm["s_tp"])
            R.inc("vector", lambda e, t=t, sl=sl: e.tensor_copy(
                U_fm[:, :].rearrange("p (d s) -> p d s", d=8)[:, :, 128 * t:128 * (t + 1)],
                pt["ps_bf"][:, sl:sl + 1024].rearrange("p (d s) -> p d s", d=8)),
                sm["s_ev"])


def mk_epi_sems(R, nm):
    return dict(xld=R.sem(nm + "xld"), st=R.sem(nm + "st"), s_apply=R.sem(nm + "ap"),
                s_d2a=R.sem(nm + "Ed2a"), s_a2d=R.sem(nm + "Ea2d"))


def mk_epi_tensors(R, nm, DSB, XT2, stats_from=None, TMPD=None):
    if stats_from is not None:
        return dict(DSB=DSB, XT2=XT2, TMPD=TMPD, BN6=stats_from["BN6"], MV=stats_from["MV"],
                    SD=stats_from["SD"], RSTD=stats_from["RSTD"], NMR=stats_from["NMR"])
    return dict(
        DSB=DSB,
        XT2=XT2,
        TMPD=TMPD,
        BN6=R.sb(nm + "BN6", [128, 12 * NT], F32),
        MV=R.sb(nm + "MV", [128, 2 * NT], F32),
        SD=R.sb(nm + "SD", [128, NT], F32),
        RSTD=R.sb(nm + "RSTD", [128, NT], F32),
        NMR=R.sb(nm + "NMR", [128, NT], F32),
    )


def epi_prefetch_x(R, ep, sm, x_src, seq, t):
    R.wait("sync", sm["st"])
    R.inc("sync", lambda e, t=t: e.dma_start(
        out=ep["XT2"][:, :],
        in_=x_src[(seq * S + 128 * t):(seq * S + 128 * (t + 1)), :]), sm["xld"], 16)


def emit_epilogue_tile(R, ep, sm, seq, t, delta_src_fn, x_dst, bias_bcast=None, scale_bcast=None,
                       direct_sbuf=False):
    v = dict(BN6=ep["BN6"], MV=ep["MV"], SD=ep["SD"], RSTD=ep["RSTD"], NMR=ep["NMR"],
             s_d2a=sm["s_d2a"], s_a2d=sm["s_a2d"])
    if direct_sbuf:
        src = delta_src_fn
    else:
        if scale_bcast is not None:
            R.op("vector", lambda e: e.tensor_tensor(ep["DSB"][:, :], delta_src_fn(), scale_bcast[:, :], op=ALU.mult))
        elif bias_bcast is not None:
            R.op("vector", lambda e: e.tensor_tensor(ep["DSB"][:, :], delta_src_fn(), bias_bcast[:, :], op=ALU.add))
        else:
            R.op("vector", lambda e: e.tensor_copy(ep["DSB"][:, :], delta_src_fn()))
        R.op("vector", lambda e: e.drain())
        src = lambda: ep["DSB"][:, :]
    R.op("vector", lambda e, t=t: e.bn_stats(ep["BN6"][:, 12 * t:12 * t + 6], src()[:, 0:512]))
    R.op("vector", lambda e, t=t: e.bn_stats(ep["BN6"][:, 12 * t + 6:12 * t + 12], src()[:, 512:1024]))
    ln_rstd_chain(R, v, t)
    R.wait("vector", sm["xld"])
    R.op("vector", lambda e, t=t: e.tensor_scalar(
        ep["TMPD"][:, :], src(), ep["MV"][:, 2 * t:2 * t + 1], ep["RSTD"][:, t:t + 1],
        op0=ALU.subtract, op1=ALU.mult))
    R.op("vector", lambda e: e.drain())
    R.inc("vector", lambda e, t=t: e.tensor_tensor(
        ep["XT2"][:, :], ep["TMPD"][:, :], ep["XT2"][:, :], op=ALU.add), sm["s_apply"])
    R.wait("gpsimd", sm["s_apply"])
    R.inc("gpsimd", lambda e, t=t: e.dma_start(
        out=x_dst[(seq * S + 128 * t):(seq * S + 128 * (t + 1)), :],
        in_=ep["XT2"][:, :]), sm["st"], 16)



def emit_phase2(R, g, x_src, x_dst):
    R.init_epsc()
    cl = R.sem("p2cl")
    ident = R.sb("p2ident", [128, 128], BF16)
    S12 = R.sb("p2S12", [128, 128], F32)
    W3SB = R.sb("p2w3sb", [128, 1024], F32)
    R.inc("sync", lambda e: e.dma_start(out=ident[:], in_=g["ident_bf"][:, :]), cl, 16)
    R.inc("sync", lambda e: e.dma_start(out=S12[:], in_=g["p2_s12"][:, :]), cl, 16)
    R.inc("sync", lambda e: e.dma_start(out=W3SB[:], in_=g["p2_w3sb"][:, :]), cl, 16)
    for e in ENGINES:
        R.wait(e, cl)

    ps_all = R.ps("p2ps", [128, 4096], F32)
    ps_bf = ps_all[:, 0:1024].bitcast(BF16)
    U_fm = R.sb("p2U", [128, 8 * 2048], BF16)
    A_ = R.sb("p2A", [128, 32 * 1024], BF16)
    B_ = R.sb("p2B", [128, 32 * 1024], BF16)
    W1B = R.sb("p2W1B", [128, 2 * 1024], BF16)
    W2B = R.sb("p2W2B", [128, 2 * 1024], BF16)
    HB = R.sb("p2HB", [128, 2 * 4096], BF16)
    W3B = W1B
    SCR = R.sb("p2SCR", [128, 1024], F32)
    SILU = SCR
    pt = mk_preln_tensors(R, "p2pl")
    pt.update(ps_bf=ps_bf, ident=ident)
    psm = mk_preln_sems(R, "p2pl")
    ep = mk_epi_tensors(R, "p2ep", SCR, pt["XT"][:, 0:1024], TMPD=pt["UB"][:, :].bitcast(F32))
    esm = mk_epi_sems(R, "p2ep")
    wl = R.sem("p2wl")
    mmA = R.sem("p2mmA")
    acA = R.sem("p2acA")
    dvA = R.sem("p2dvA")
    hl = R.sem("p2hl")
    mmH = R.sem("p2mmH")
    acH = R.sem("p2acH")
    w3l = R.sem("p2w3l")
    mmW3c = R.sem("p2mmW3c")

    for seq in range(NSEQ):
        if seq > 0:
            R.wait("sync", esm["st"])
            R.wait("vector", mmA)
            R.wait("tensor", esm["s_apply"])
            R.wait("tensor", esm["st"])
        emit_preln(R, pt, psm, x_src, seq, U_fm=U_fm)
        R.wait("sync", psm["s_app"])
        epi_prefetch_x(R, ep, esm, x_src, seq, 0)

        for tc in range(2):
            R.wait("tensor", psm["s_ev"])
            if tc == 1:
                R.wait("vector", mmW3c)
                R.wait("tensor", esm["s_apply"])
                R.wait("tensor", esm["st"])
            R.wait("gpsimd", mmW3c)
            for c in range(32):
                par = c % 2
                R.wait_rel("gpsimd", mmA, 1)
                R.inc("gpsimd", lambda e, c=c, par=par: e.dma_start(
                    out=W1B[:, 1024 * par:1024 * (par + 1)], in_=g["p2_w1"][:, 1024 * c:1024 * (c + 1)]), wl, 16)
                R.inc("gpsimd", lambda e, c=c, par=par: e.dma_start(
                    out=W2B[:, 1024 * par:1024 * (par + 1)], in_=g["p2_w2"][:, 1024 * c:1024 * (c + 1)]), wl, 16)
                R.wait("tensor", wl)
                R.wait_rel("tensor", dvA, 1)
                for pre in range(2):
                    WB = W1B if pre == 0 else W2B
                    for p in range(2):
                        for d in range(8):
                            fn = lambda e, p=p, d=d, par=par, tc=tc, WB=WB, pre=pre: e.matmul(
                                ps_all[:, 512 * (4 * par + 2 * pre + p):512 * (4 * par + 2 * pre + p + 1)],
                                WB[:, 1024 * par + 128 * d:1024 * par + 128 * (d + 1)],
                                U_fm[:, 2048 * d + 1024 * tc + 512 * p:2048 * d + 1024 * tc + 512 * (p + 1)],
                                start=(d == 0), stop=(d == 7))
                            if pre == 1 and p == 1 and d == 7:
                                R.inc("tensor", fn, mmA)
                            else:
                                R.op("tensor", fn)
                R.wait("scalar", mmA)
                R.wait("scalar", dvA)
                for p in range(2):
                    fn = lambda e, c=c, p=p, par=par: e.activation(
                        out=SILU[:, 512 * p:512 * (p + 1)],
                        in_=ps_all[:, 512 * (4 * par + p):512 * (4 * par + p + 1)],
                        func=AF.Silu, scale=S12[:, c:c + 1])
                    R.inc("scalar", fn, acA) if p == 1 else R.op("scalar", fn)
                R.wait("vector", acA)
                for p in range(2):
                    fn = lambda e, c=c, p=p, par=par: e.scalar_tensor_tensor(
                        A_[:, 1024 * c + 512 * p:1024 * c + 512 * (p + 1)],
                        ps_all[:, 512 * (4 * par + 2 + p):512 * (4 * par + 2 + p + 1)],
                        S12[:, 32 + c:32 + c + 1], SILU[:, 512 * p:512 * (p + 1)],
                        op0=ALU.mult, op1=ALU.mult)
                    R.inc("vector", fn, dvA) if p == 1 else R.op("vector", fn)

            def h_stage(img, srcT, dstT, scol_off, first):
                R.wait("tensor", dvA)
                for c in range(32):
                    par = c % 2
                    R.wait_rel("gpsimd", mmH, 1)
                    R.inc("gpsimd", lambda e, c=c, par=par, img=img: e.dma_start(
                        out=HB[:, 4096 * par:4096 * (par + 1)], in_=img[:, 4096 * c:4096 * (c + 1)]), hl, 16)
                    R.wait("tensor", hl)
                    R.wait_rel("tensor", acH, 1)
                    if first and c == 0:
                        R.wait("tensor", acA)
                        R.wait("tensor", dvA)
                    for p in range(2):
                        for k in range(32):
                            fn = lambda e, c=c, p=p, k=k, par=par, srcT=srcT: e.matmul(
                                ps_all[:, 512 * (2 * par + p):512 * (2 * par + p + 1)],
                                HB[:, 4096 * par + 128 * k:4096 * par + 128 * (k + 1)],
                                srcT[:, 1024 * k + 512 * p:1024 * k + 512 * (p + 1)],
                                start=(k == 0), stop=(k == 31))
                            if p == 1 and k == 31:
                                R.inc("tensor", fn, mmH)
                            else:
                                R.op("tensor", fn)
                    R.wait("scalar", mmH)
                    for p in range(2):
                        fn = lambda e, c=c, p=p, par=par, dstT=dstT, so=scol_off: e.activation(
                            out=dstT[:, 1024 * c + 512 * p:1024 * c + 512 * (p + 1)],
                            in_=ps_all[:, 512 * (2 * par + p):512 * (2 * par + p + 1)],
                            func=AF.Gelu, scale=S12[:, so + c:so + c + 1])
                        R.inc("scalar", fn, acH) if p == 1 else R.op("scalar", fn)

            h_stage(g["p2_h1"], A_, B_, 64, True)
            h_stage(g["p2_h2"], B_, A_, 96, False)

            for tg in range(2):
                R.wait("gpsimd", mmA)
                for c in range(32):
                    par = c % 2
                    R.wait_rel("gpsimd", mmW3c, 1)
                    R.inc("gpsimd", lambda e, c=c, par=par: e.dma_start(
                        out=W3B[:, 1024 * par:1024 * (par + 1)], in_=g["p2_w3"][:, 1024 * c:1024 * (c + 1)]), w3l, 16)
                    R.wait("tensor", w3l)
                    if c == 0:
                        R.wait("tensor", acH)
                        R.wait("tensor", esm["s_apply"])
                    for tt in range(4):
                        for j in range(2):
                            fn = lambda e, c=c, tt=tt, j=j, tc=tc, tg=tg, par=par: e.matmul(
                                ps_all[:, 512 * (2 * tt + j):512 * (2 * tt + j + 1)],
                                A_[:, 1024 * c + 128 * (tg * 4 + tt):1024 * c + 128 * (tg * 4 + tt) + 128],
                                W3B[:, 1024 * par + 512 * j:1024 * par + 512 * (j + 1)],
                                start=(c == 0), stop=(c == 31))
                            if tt == 3 and j == 1:
                                R.inc("tensor", fn, mmW3c)
                            else:
                                R.op("tensor", fn)
                R.wait("vector", mmW3c)
                for tt in range(4):
                    t_glob = tc * 8 + tg * 4 + tt
                    emit_epilogue_tile(
                        R, ep, esm, seq, t_glob,
                        (lambda tt=tt: ps_all[:, 1024 * tt:1024 * (tt + 1)]),
                        x_dst, scale_bcast=W3SB)
                    if t_glob < NT - 1:
                        epi_prefetch_x(R, ep, esm, x_src, seq, t_glob + 1)
    R.wait("gpsimd", esm["st"])





P1_DEBUG = {"stage": ""}


def emit_phase1(R, g, x_src, x_dst):
    R.init_epsc()
    dbg = P1_DEBUG["stage"]
    cl = R.sem("p1cl")
    ident = R.sb("p1ident", [128, 128], BF16)
    MASKD = R.sb("p1MASKD", [128, 128], F32)
    R.inc("sync", lambda e: e.dma_start(out=ident[:], in_=g["ident_bf"][:, :]), cl, 16)
    R.inc("sync", lambda e: e.dma_start(out=MASKD[:], in_=g["p1_mask"][:, 0:128]), cl, 16)
    for e in ENGINES:
        R.wait(e, cl)

    ps_all = R.ps("p1ps", [128, 4096], F32)
    ps_bf = ps_all[:, 0:1024].bitcast(BF16)
    psTv = ps_all[:, 3072:4096].bitcast(BF16)
    U_fm = R.sb("p1U", [128, 8 * 2048], BF16)
    AO = U_fm
    RQ = R.sb("p1RQ", [128, 8 * 2048], BF16)
    RK = R.sb("p1RK", [128, 8 * 2048], BF16)
    VT = R.sb("p1VT", [128, 16 * 1024], BF16)
    SCO = R.sb("p1SCO", [128, 4 * 2048], F32)
    CC = SCO[:, 0:2048]
    SS = SCO[:, 2048:4096]
    PBF = R.sb("p1PBF", [128, 4 * 2048], BF16)
    PTS = R.sb("p1PTS", [128, 2 * 512], BF16)
    RTMP = R.sb("p1RTMP", [128, 2 * 512], F32)
    VSB = RTMP[:, 0:1024].bitcast(BF16)
    QWB = R.sb("p1QWB", [128, 2 * 2048], BF16)
    WOB = QWB[:, 0:2048]
    SMT = R.sb("p1SMT", [128, 16], F32)
    pt = mk_preln_tensors(R, "p1pl")
    pt.update(ps_bf=ps_bf, ident=ident)
    psm = mk_preln_sems(R, "p1pl")
    ep = mk_epi_tensors(R, "p1ep", None, pt["XT"][:, 0:1024], TMPD=PBF[:, 0:2048].bitcast(F32))
    esm = mk_epi_sems(R, "p1ep")
    qw = R.sem("p1qw")
    mmQ = R.sem("p1mmQ")
    rpD = R.sem("p1rpD")
    vE = R.sem("p1vE")
    mmS = R.sem("p1mmS")
    sevE = R.sem("p1sevE")
    sf_d2a = R.sem("p1sfd2a")
    sf_a2d = R.sem("p1sfa2d")
    pN = R.sem("p1pN")
    mmPT = R.sem("p1mmPT")
    ptE = R.sem("p1ptE")
    mmAV = R.sem("p1mmAV")
    aoE = R.sem("p1aoE")
    wol = R.sem("p1wol")
    mmPc = R.sem("p1mmPc")
    ccl = R.sem("p1ccl")

    for seq in range(1 if dbg else NSEQ):
        if seq > 0:
            R.wait("sync", esm["st"])
            R.wait("vector", mmQ)
            R.wait("vector", mmPc)
            R.wait("tensor", esm["s_apply"])
            R.wait("tensor", esm["st"])
        emit_preln(R, pt, psm, x_src, seq, U_fm=U_fm)

        R.wait("sync", sf_a2d)
        R.wait("sync", sevE)
        R.inc("sync", lambda e: e.dma_start(out=CC, in_=g["p1_cc"][:, :]), ccl, 16)
        R.inc("sync", lambda e: e.dma_start(out=SS, in_=g["p1_ss"][:, :]), ccl, 16)
        R.wait("vector", ccl)
        R.wait("tensor", psm["s_ev"])
        for pi in range(8):
            slot = pi % 2
            dst = RQ if pi < 4 else RK
            gcol = (pi % 4) * 2 * 2048
            R.wait_rel("gpsimd", mmQ, 4)
            R.inc("gpsimd", lambda e, pi=pi, slot=slot: e.dma_start(
                out=QWB[:, 2048 * slot:2048 * slot + 1024], in_=g["p1_qkv"][:, 2048 * pi:2048 * pi + 1024]), qw, 16)
            R.inc("gpsimd", lambda e, pi=pi, slot=slot: e.dma_start(
                out=QWB[:, 2048 * slot + 1024:2048 * slot + 2048], in_=g["p1_qkv"][:, 2048 * pi + 1024:2048 * pi + 2048]), qw, 16)
            for p in range(4):
                pp = 4 * pi + p
                bA = 2 * (pp % 2)
                if p == 0:
                    R.wait("tensor", qw)
                R.wait_rel("tensor", rpD, 1)
                for ab in range(2):
                    for d in range(8):
                        fn = lambda e, p=p, d=d, ab=ab, slot=slot, bA=bA: e.matmul(
                            ps_all[:, 512 * (bA + ab):512 * (bA + ab + 1)],
                            QWB[:, 2048 * slot + 1024 * ab + 128 * d:2048 * slot + 1024 * ab + 128 * (d + 1)],
                            U_fm[:, 2048 * d + 512 * p:2048 * d + 512 * (p + 1)],
                            start=(d == 0), stop=(d == 7))
                        if ab == 1 and d == 7:
                            R.inc("tensor", fn, mmQ)
                        else:
                            R.op("tensor", fn)
                R.wait("vector", mmQ)
                psA = lambda bA=bA: ps_all[:, 512 * bA:512 * (bA + 1)]
                psB = lambda bA=bA: ps_all[:, 512 * (bA + 1):512 * (bA + 2)]
                ccp = lambda p=p: CC[:, 512 * p:512 * (p + 1)]
                ssp = lambda p=p: SS[:, 512 * p:512 * (p + 1)]
                R.op("vector", lambda e, psA=psA, ccp=ccp: e.tensor_tensor(RTMP[:, 0:512], ccp(), psA(), op=ALU.mult))
                R.op("vector", lambda e, psB=psB, ssp=ssp: e.tensor_tensor(RTMP[:, 512:1024], ssp(), psB(), op=ALU.mult))
                R.op("vector", lambda e: e.drain())
                R.op("vector", lambda e, p=p, gcol=gcol, dst=dst: e.tensor_tensor(
                    dst[:, gcol + 512 * p:gcol + 512 * (p + 1)], RTMP[:, 0:512], RTMP[:, 512:1024], op=ALU.subtract))
                R.op("vector", lambda e: e.drain())
                R.op("vector", lambda e, psA=psA, ssp=ssp: e.tensor_tensor(RTMP[:, 0:512], ssp(), psA(), op=ALU.mult))
                R.op("vector", lambda e, psB=psB, ccp=ccp: e.tensor_tensor(RTMP[:, 512:1024], ccp(), psB(), op=ALU.mult))
                R.op("vector", lambda e: e.drain())
                R.inc("vector", lambda e, p=p, gcol=gcol, dst=dst: e.tensor_tensor(
                    dst[:, gcol + 2048 + 512 * p:gcol + 2048 + 512 * (p + 1)], RTMP[:, 0:512], RTMP[:, 512:1024], op=ALU.add),
                    rpD)

        for vc in range(8):
            slot = vc % 2
            R.wait_rel("gpsimd", mmQ, 4)
            R.inc("gpsimd", lambda e, vc=vc, slot=slot: e.dma_start(
                out=QWB[:, 2048 * slot:2048 * slot + 1024], in_=g["p1_qkv"][:, 1024 * (16 + vc):1024 * (17 + vc)]), qw, 16)
            for p in range(4):
                pp2 = 4 * vc + p
                bk = 4 + (pp2 % 2)
                if p == 0:
                    R.wait("tensor", qw)
                    if vc == 0:
                        R.wait("tensor", rpD)
                R.wait_rel("tensor", vE, 1)
                for d in range(8):
                    fn = lambda e, p=p, d=d, slot=slot, bk=bk: e.matmul(
                        ps_all[:, 512 * bk:512 * (bk + 1)],
                        QWB[:, 2048 * slot + 128 * d:2048 * slot + 128 * (d + 1)],
                        U_fm[:, 2048 * d + 512 * p:2048 * d + 512 * (p + 1)],
                        start=(d == 0), stop=(d == 7))
                    if d == 7:
                        R.inc("tensor", fn, mmQ)
                    else:
                        R.op("tensor", fn)
                R.wait("scalar", mmQ)
                if p == 0:
                    R.wait("scalar", mmPT)
                if p == 0 and vc == 0:
                    R.wait("scalar", rpD)
                R.inc("scalar", lambda e, p=p, bk=bk: e.activation(
                    out=VSB[:, 512 * p:512 * (p + 1)], in_=ps_all[:, 512 * bk:512 * (bk + 1)], func=AF.Identity),
                    vE)
            R.wait("tensor", vE)
            for j in range(2):
                TB = mmPT.n % 2
                R.wait_rel("tensor", ptE, 1)
                for i in range(8):
                    fn = lambda e, j=j, i=i, TB=TB: e.transpose(
                        psTv[:, 1024 * TB + 128 * i:1024 * TB + 128 * (i + 1)],
                        VSB[:, 128 * (8 * j + i):128 * (8 * j + i + 1)], ident[:, :])
                    if i == 7:
                        R.inc("tensor", fn, mmPT)
                    else:
                        R.op("tensor", fn)
                R.wait("vector", mmPT)
                R.inc("vector", lambda e, j=j, vc=vc, TB=TB: e.tensor_copy(
                    VT[:, :].rearrange("p (kt r) -> p kt r", kt=16)[:, 8 * j:8 * (j + 1), 128 * vc:128 * (vc + 1)],
                    psTv[:, 1024 * TB:1024 * (TB + 1)].rearrange("p (k r) -> p k r", k=8)), ptE)

        if dbg == "qkv":
            dsem = R.sem("p1dbg")
            R.wait("gpsimd", rpD)
            R.wait("gpsimd", ptE)
            for nm2, buf in [("d_rq", RQ), ("d_rk", RK), ("d_vt", VT), ("d_u", U_fm)]:
                R.inc("gpsimd", lambda e, nm2=nm2, buf=buf: e.dma_start(out=g[nm2][:, :], in_=buf[:, :]), dsem, 16)
            R.wait("gpsimd", dsem)
            return
        R.wait("tensor", rpD)
        R.wait("tensor", ptE)
        R.wait("tensor", vE)
        for gi in range(4):
            for t in range(NT):
                kpn = t // 4 + 1
                for kp in range(kpn):
                    width = min(512, 128 * (t + 1) - 512 * kp)
                    R.wait("tensor", sevE)
                    for h in range(4):
                        for ab in range(2):
                            fn = lambda e, h=h, ab=ab, gi=gi, t=t, kp=kp, width=width: e.matmul(
                                ps_all[:, 512 * h:512 * h + width],
                                RQ[32 * h:32 * (h + 1), (2 * gi + ab) * 2048 + 128 * t:(2 * gi + ab) * 2048 + 128 * (t + 1)],
                                RK[32 * h:32 * (h + 1), (2 * gi + ab) * 2048 + 512 * kp:(2 * gi + ab) * 2048 + 512 * kp + width],
                                start=(ab == 0), stop=(ab == 1), tile_position=(32 * h, 0))
                            if h == 3 and ab == 1:
                                R.inc("tensor", fn, mmS)
                            else:
                                R.op("tensor", fn)
                    R.wait("vector", mmS)
                    if kp == 0:
                        R.wait("vector", sf_a2d)
                        R.wait("vector", rpD)
                    for h in range(4):
                        if kp < t // 4:
                            fn = lambda e, h=h, kp=kp: e.tensor_copy(
                                SCO[:, 2048 * h + 512 * kp:2048 * h + 512 * (kp + 1)], ps_all[:, 512 * h:512 * (h + 1)])
                        else:
                            vloc = t % 4
                            if vloc > 0:
                                R.op("vector", lambda e, h=h, kp=kp, vloc=vloc: e.tensor_copy(
                                    SCO[:, 2048 * h + 512 * kp:2048 * h + 512 * kp + 128 * vloc],
                                    ps_all[:, 512 * h:512 * h + 128 * vloc]))
                            fn = lambda e, h=h, kp=kp, vloc=vloc, width=width: e.tensor_tensor(
                                SCO[:, 2048 * h + 512 * kp + 128 * vloc:2048 * h + 512 * kp + width],
                                ps_all[:, 512 * h + 128 * vloc:512 * h + width],
                                MASKD[:, 0:width - 128 * vloc], op=ALU.add)
                        if h == 3:
                            R.inc("vector", fn, sevE)
                        else:
                            R.op("vector", fn)
                L = 128 * (t + 1)
                R.wait("scalar", mmPT)
                for h in range(4):
                    R.op("vector", lambda e, h=h, L=L: e.reduce_max(
                        SMT[:, h:h + 1], SCO[:, 2048 * h:2048 * h + L], axis=AX.X, negate=True))
                    R.op("vector", lambda e: e.drain())
                    R.inc("vector", lambda e, h=h: e.tensor_scalar(
                        SMT[:, 4 + h:5 + h], SMT[:, h:h + 1], 0.125, None, op0=ALU.mult), sf_d2a)
                    R.wait("scalar", sf_d2a)
                    R.inc("scalar", lambda e, h=h, L=L: e.activation(
                        out=PBF[:, 2048 * h:2048 * h + L], in_=SCO[:, 2048 * h:2048 * h + L],
                        func=AF.Exp, bias=SMT[:, 4 + h:5 + h], scale=0.125), sf_a2d)
                    R.wait("vector", sf_a2d)
                    R.op("vector", lambda e, h=h, L=L: e.reduce_sum(
                        SMT[:, 8 + h:9 + h], PBF[:, 2048 * h:2048 * h + L], axis=AX.X))
                    R.op("vector", lambda e: e.drain())
                    R.op("vector", lambda e, h=h: e.reciprocal(SMT[:, 12 + h:13 + h], SMT[:, 8 + h:9 + h]))
                    R.op("vector", lambda e: e.drain())
                    R.inc("vector", lambda e, h=h, L=L: e.tensor_scalar(
                        PBF[:, 2048 * h:2048 * h + L], PBF[:, 2048 * h:2048 * h + L],
                        SMT[:, 12 + h:13 + h], None, op0=ALU.mult), pN)
                R.wait("tensor", pN)
                R.wait("tensor", aoE)

                def emit_T(kt):
                    TB = mmPT.n % 2
                    R.wait_rel("tensor", ptE, 1)
                    for h in range(4):
                        fn = lambda e, kt=kt, h=h, TB=TB: e.transpose(
                            psTv[:, 1024 * TB + 128 * h:1024 * TB + 128 * (h + 1)],
                            PBF[:, 2048 * h + 128 * kt:2048 * h + 128 * (kt + 1)], ident[:, :])
                        if h == 3:
                            R.inc("tensor", fn, mmPT)
                        else:
                            R.op("tensor", fn)
                    R.wait("vector", mmPT)
                    R.inc("vector", lambda e, TB=TB, kt=kt: e.tensor_copy(
                        PTS[:, 512 * (kt % 2):512 * (kt % 2 + 1)], psTv[:, 1024 * TB:1024 * TB + 512]), ptE)

                def emit_AV(kt, t, gi):
                    pb = kt % 2
                    R.wait("tensor", ptE)
                    for pr in range(2):
                        for cg in range(2):
                            fn = lambda e, kt=kt, t=t, gi=gi, pr=pr, cg=cg, pb=pb: e.matmul(
                                ps_all[64 * cg:64 * (cg + 1), 2048 + 512 * pr:2048 + 512 * pr + 128],
                                VT[:, (kt * 8 + 2 * gi + pr) * 128 + 64 * cg:(kt * 8 + 2 * gi + pr) * 128 + 64 * (cg + 1)],
                                PTS[:, 512 * pb + 128 * (2 * pr + cg):512 * pb + 128 * (2 * pr + cg + 1)],
                                start=(kt == 0), stop=(kt == t), tile_position=(0, 64 * cg))
                            if pr == 1 and cg == 1 and kt == t:
                                R.inc("tensor", fn, mmAV)
                            else:
                                R.op("tensor", fn)

                for kt in range(t + 1):
                    emit_T(kt)
                    if kt > 0:
                        emit_AV(kt - 1, t, gi)
                emit_AV(t, t, gi)
                R.wait("vector", mmAV)
                R.op("vector", lambda e, gi=gi, t=t: e.tensor_copy(
                    AO[:, (2 * gi) * 2048 + 128 * t:(2 * gi) * 2048 + 128 * (t + 1)], ps_all[:, 2048:2176]))
                R.inc("vector", lambda e, gi=gi, t=t: e.tensor_copy(
                    AO[:, (2 * gi + 1) * 2048 + 128 * t:(2 * gi + 1) * 2048 + 128 * (t + 1)], ps_all[:, 2560:2688]), aoE)

        if dbg == "attn":
            dsem = R.sem("p1dbg")
            R.wait("gpsimd", aoE)
            R.inc("gpsimd", lambda e: e.dma_start(out=g["d_rq"][:, :], in_=AO[:, :]), dsem, 16)
            R.wait("gpsimd", dsem)
            return
        R.wait("tensor", aoE)
        R.wait("tensor", ptE)
        R.wait("gpsimd", mmQ)
        for tgi in range(4):
            for c in range(8):
                R.wait_rel("gpsimd", mmPc, 1)
                R.inc("gpsimd", lambda e, c=c: e.dma_start(
                    out=WOB[:, 1024 * (c % 2):1024 * (c % 2 + 1)], in_=g["p1_wo"][:, 1024 * c:1024 * (c + 1)]), wol, 16)
                R.wait("tensor", wol)
                if c == 0:
                    R.wait("tensor", esm["s_apply"])
                for tt in range(4):
                    for j in range(2):
                        fn = lambda e, c=c, tt=tt, j=j, tgi=tgi: e.matmul(
                            ps_all[:, 512 * (2 * tt + j):512 * (2 * tt + j + 1)],
                            AO[:, 2048 * c + 128 * (4 * tgi + tt):2048 * c + 128 * (4 * tgi + tt) + 128],
                            WOB[:, 1024 * (c % 2) + 512 * j:1024 * (c % 2) + 512 * (j + 1)],
                            start=(c == 0), stop=(c == 7))
                        if tt == 3 and j == 1:
                            R.inc("tensor", fn, mmPc)
                        else:
                            R.op("tensor", fn)
            R.wait("vector", mmPc)
            if tgi == 0:
                R.wait("sync", psm["s_app"])
            for tt in range(4):
                t_glob = tgi * 4 + tt
                if t_glob == 0:
                    epi_prefetch_x(R, ep, esm, x_src, seq, 0)
                emit_epilogue_tile(
                    R, ep, esm, seq, t_glob,
                    (lambda tt=tt: ps_all[:, 1024 * tt:1024 * (tt + 1)]),
                    x_dst, direct_sbuf=True)
                if t_glob < NT - 1:
                    epi_prefetch_x(R, ep, esm, x_src, seq, t_glob + 1)
    R.wait("gpsimd", esm["st"])




def emit_phase3(R, g, x_src, x_dst):
    R.init_epsc()
    cl = R.sem("p3cl")
    ident = R.sb("p3ident", [128, 128], BF16)
    MASKD = R.sb("p3MASKD", [128, 128], F32)
    W3Q = R.sb("p3W", [128, 192], BF16)
    B3Q = R.sb("p3B", [128, 3], F32)
    PROJB = R.sb("p3PJB", [128, 1024], F32)
    R.inc("sync", lambda e: e.dma_start(out=ident[:], in_=g["ident_bf"][:, :]), cl, 16)
    R.inc("sync", lambda e: e.dma_start(out=MASKD[:], in_=g["p1_mask"][:, 0:128]), cl, 16)
    R.inc("gpsimd", lambda e: e.dma_start(out=W3Q[:], in_=g["p3_wqkv"][:, :]), cl, 16)
    R.inc("sync", lambda e: e.dma_start(out=B3Q[:], in_=g["p3_bqkv"][:, :]), cl, 16)
    R.inc("sync", lambda e: e.dma_start(out=PROJB[:], in_=g["p3_projb"][:, :]), cl, 16)
    for e in ENGINES:
        R.wait(e, cl)

    ps_all = R.ps("p3ps", [128, 4096], F32)
    ps_bf = ps_all[:, 0:1024].bitcast(BF16)
    psTv = ps_all[:, 3072:4096].bitcast(BF16)
    U_fm = R.sb("p3U", [128, 8 * 2048], BF16)
    AO = U_fm
    Q_fm = R.sb("p3Q", [128, 8 * 2048], BF16)
    K_fm = R.sb("p3K", [128, 8 * 2048], BF16)
    VT = R.sb("p3VT", [128, 16 * 1024], BF16)
    SCO = R.sb("p3SCO", [128, 2 * 2048], F32)
    PBF = R.sb("p3PBF", [128, 2 * 2048], BF16)
    PTS = R.sb("p3PTS", [128, 2 * 256], BF16)
    VSB = R.sb("p3VSB", [128, 2048], BF16)
    WOB = R.sb("p3WOB", [128, 2 * 1024], BF16)
    SMT = R.sb("p3SMT", [128, 8], F32)
    pt = mk_preln_tensors(R, "p3pl")
    pt.update(ps_bf=ps_bf, ident=ident)
    psm = mk_preln_sems(R, "p3pl")
    ep = mk_epi_tensors(R, "p3ep", R.sb("p3DSB", [128, 1024], F32), pt["XT"][:, 0:1024],
                        TMPD=PBF[:, 0:2048].bitcast(F32))
    esm = mk_epi_sems(R, "p3ep")
    mmQ = R.sem("p3mmQ")
    qkE = R.sem("p3qkE")
    mmPT = R.sem("p3mmPT")
    ptE = R.sem("p3ptE")
    mmS = R.sem("p3mmS")
    sevE = R.sem("p3sevE")
    sf_d2a = R.sem("p3sfd2a")
    sf_a2d = R.sem("p3sfa2d")
    pN = R.sem("p3pN")
    mmAV = R.sem("p3mmAV")
    aoE = R.sem("p3aoE")
    wol = R.sem("p3wol")
    mmPc = R.sem("p3mmPc")

    for seq in range(NSEQ):
        if seq > 0:
            R.wait("sync", esm["st"])
            R.wait("vector", mmQ)
            R.wait("vector", mmPc)
            R.wait("tensor", esm["s_apply"])
            R.wait("tensor", esm["st"])
        emit_preln(R, pt, psm, x_src, seq, U_fm=U_fm)

        R.wait("tensor", psm["s_ev"])
        for pr in range(8):
            for p in range(4):
                pp = 4 * pr + p
                bq = 3 * (pp % 2)
                R.wait_rel("tensor", qkE, 3)
                for b in range(2):
                    base = 64 * b
                    for w in range(3):
                        fn = lambda e, pr=pr, p=p, b=b, w=w, bq=bq, base=base: e.matmul(
                            ps_all[base:base + 64, 512 * (bq + w):512 * (bq + w) + 512],
                            W3Q[base:base + 64, 64 * w:64 * (w + 1)],
                            U_fm[base:base + 64, pr * 2048 + 512 * p:pr * 2048 + 512 * (p + 1)],
                            start=True, stop=True, tile_position=(base, base))
                        if b == 1 and w == 2:
                            R.inc("tensor", fn, mmQ)
                        else:
                            R.op("tensor", fn)
                R.wait("scalar", mmQ)
                if p == 0:
                    R.wait("scalar", mmPT)
                R.op("scalar", lambda e, pr=pr, p=p, bq=bq: e.activation(
                    out=Q_fm[:, pr * 2048 + 512 * p:pr * 2048 + 512 * (p + 1)],
                    in_=ps_all[:, 512 * bq:512 * (bq + 1)], func=AF.Identity, bias=B3Q[:, 0:1]))
                R.op("scalar", lambda e, pr=pr, p=p, bq=bq: e.activation(
                    out=K_fm[:, pr * 2048 + 512 * p:pr * 2048 + 512 * (p + 1)],
                    in_=ps_all[:, 512 * (bq + 1):512 * (bq + 2)], func=AF.Identity, bias=B3Q[:, 1:2]))
                R.inc("scalar", lambda e, p=p, bq=bq: e.activation(
                    out=VSB[:, 512 * p:512 * (p + 1)],
                    in_=ps_all[:, 512 * (bq + 2):512 * (bq + 3)], func=AF.Identity, bias=B3Q[:, 2:3]), qkE, 3)
            R.wait("tensor", qkE)
            for j in range(2):
                TB = mmPT.n % 2
                R.wait_rel("tensor", ptE, 1)
                for i in range(8):
                    fn = lambda e, j=j, i=i, TB=TB: e.transpose(
                        psTv[:, 1024 * TB + 128 * i:1024 * TB + 128 * (i + 1)],
                        VSB[:, 128 * (8 * j + i):128 * (8 * j + i + 1)], ident[:, :])
                    if i == 7:
                        R.inc("tensor", fn, mmPT)
                    else:
                        R.op("tensor", fn)
                R.wait("vector", mmPT)
                R.inc("vector", lambda e, j=j, pr=pr, TB=TB: e.tensor_copy(
                    VT[:, :].rearrange("p (kt r) -> p kt r", kt=16)[:, 8 * j:8 * (j + 1), 128 * pr:128 * (pr + 1)],
                    psTv[:, 1024 * TB:1024 * (TB + 1)].rearrange("p (k r) -> p k r", k=8)), ptE)

        R.wait("tensor", qkE)
        R.wait("tensor", ptE)
        for pr in range(8):
            for t in range(NT):
                kpn = t // 4 + 1
                for kp in range(kpn):
                    width = min(512, 128 * (t + 1) - 512 * kp)
                    R.wait("tensor", sevE)
                    for b in range(2):
                        fn = lambda e, pr=pr, t=t, kp=kp, width=width, b=b: e.matmul(
                            ps_all[:, 512 * b:512 * b + width],
                            Q_fm[64 * b:64 * (b + 1), pr * 2048 + 128 * t:pr * 2048 + 128 * (t + 1)],
                            K_fm[64 * b:64 * (b + 1), pr * 2048 + 512 * kp:pr * 2048 + 512 * kp + width],
                            start=True, stop=True, tile_position=(64 * b, 0))
                        if b == 1:
                            R.inc("tensor", fn, mmS)
                        else:
                            R.op("tensor", fn)
                    R.wait("vector", mmS)
                    if kp == 0:
                        R.wait("vector", sf_a2d)
                    for b in range(2):
                        if kp < t // 4:
                            fn = lambda e, b=b, kp=kp: e.tensor_copy(
                                SCO[:, 2048 * b + 512 * kp:2048 * b + 512 * (kp + 1)], ps_all[:, 512 * b:512 * (b + 1)])
                        else:
                            vloc = t % 4
                            if vloc > 0:
                                R.op("vector", lambda e, b=b, kp=kp, vloc=vloc: e.tensor_copy(
                                    SCO[:, 2048 * b + 512 * kp:2048 * b + 512 * kp + 128 * vloc],
                                    ps_all[:, 512 * b:512 * b + 128 * vloc]))
                            fn = lambda e, b=b, kp=kp, vloc=vloc, width=width: e.tensor_tensor(
                                SCO[:, 2048 * b + 512 * kp + 128 * vloc:2048 * b + 512 * kp + width],
                                ps_all[:, 512 * b + 128 * vloc:512 * b + width],
                                MASKD[:, 0:width - 128 * vloc], op=ALU.add)
                        if b == 1:
                            R.inc("vector", fn, sevE)
                        else:
                            R.op("vector", fn)
                L = 128 * (t + 1)
                R.wait("scalar", mmPT)
                for b in range(2):
                    R.op("vector", lambda e, b=b, L=L: e.reduce_max(
                        SMT[:, b:b + 1], SCO[:, 2048 * b:2048 * b + L], axis=AX.X, negate=True))
                    R.op("vector", lambda e: e.drain())
                    R.inc("vector", lambda e, b=b: e.tensor_scalar(
                        SMT[:, 2 + b:3 + b], SMT[:, b:b + 1], 0.125, None, op0=ALU.mult), sf_d2a)
                    R.wait("scalar", sf_d2a)
                    R.inc("scalar", lambda e, b=b, L=L: e.activation(
                        out=PBF[:, 2048 * b:2048 * b + L], in_=SCO[:, 2048 * b:2048 * b + L],
                        func=AF.Exp, bias=SMT[:, 2 + b:3 + b], scale=0.125), sf_a2d)
                    R.wait("vector", sf_a2d)
                    R.op("vector", lambda e, b=b, L=L: e.reduce_sum(
                        SMT[:, 4 + b:5 + b], PBF[:, 2048 * b:2048 * b + L], axis=AX.X))
                    R.op("vector", lambda e: e.drain())
                    R.op("vector", lambda e, b=b: e.reciprocal(SMT[:, 6 + b:7 + b], SMT[:, 4 + b:5 + b]))
                    R.op("vector", lambda e: e.drain())
                    R.inc("vector", lambda e, b=b, L=L: e.tensor_scalar(
                        PBF[:, 2048 * b:2048 * b + L], PBF[:, 2048 * b:2048 * b + L],
                        SMT[:, 6 + b:7 + b], None, op0=ALU.mult), pN)
                R.wait("tensor", pN)
                R.wait("tensor", aoE)

                def emit_T3(kt):
                    TB = mmPT.n % 2
                    R.wait_rel("tensor", ptE, 1)
                    for b in range(2):
                        fn = lambda e, kt=kt, b=b, TB=TB: e.transpose(
                            psTv[:, 1024 * TB + 128 * b:1024 * TB + 128 * (b + 1)],
                            PBF[:, 2048 * b + 128 * kt:2048 * b + 128 * (kt + 1)], ident[:, :])
                        if b == 1:
                            R.inc("tensor", fn, mmPT)
                        else:
                            R.op("tensor", fn)
                    R.wait("vector", mmPT)
                    R.inc("vector", lambda e, TB=TB, kt=kt: e.tensor_copy(
                        PTS[:, 256 * (kt % 2):256 * (kt % 2 + 1)], psTv[:, 1024 * TB:1024 * TB + 256]), ptE)

                def emit_AV3(kt, t, pr):
                    pb = kt % 2
                    R.wait("tensor", ptE)
                    for b in range(2):
                        fn = lambda e, kt=kt, t=t, pr=pr, b=b, pb=pb: e.matmul(
                            ps_all[64 * b:64 * (b + 1), 2048:2176],
                            VT[:, (kt * 8 + pr) * 128 + 64 * b:(kt * 8 + pr) * 128 + 64 * (b + 1)],
                            PTS[:, 256 * pb + 128 * b:256 * pb + 128 * (b + 1)],
                            start=(kt == 0), stop=(kt == t), tile_position=(0, 64 * b))
                        if b == 1 and kt == t:
                            R.inc("tensor", fn, mmAV)
                        else:
                            R.op("tensor", fn)

                for kt in range(t + 1):
                    emit_T3(kt)
                    if kt > 0:
                        emit_AV3(kt - 1, t, pr)
                emit_AV3(t, t, pr)
                R.wait("vector", mmAV)
                R.inc("vector", lambda e, pr=pr, t=t: e.tensor_copy(
                    AO[:, pr * 2048 + 128 * t:pr * 2048 + 128 * (t + 1)], ps_all[:, 2048:2176]), aoE)

        R.wait("tensor", aoE)
        R.wait("tensor", ptE)
        for tgi in range(4):
            for c in range(8):
                R.wait_rel("gpsimd", mmPc, 1)
                R.inc("gpsimd", lambda e, c=c: e.dma_start(
                    out=WOB[:, 1024 * (c % 2):1024 * (c % 2 + 1)], in_=g["p3_proj"][:, 1024 * c:1024 * (c + 1)]), wol, 16)
                R.wait("tensor", wol)
                if c == 0:
                    R.wait("tensor", esm["s_apply"])
                for tt in range(4):
                    for j in range(2):
                        fn = lambda e, c=c, tt=tt, j=j, tgi=tgi: e.matmul(
                            ps_all[:, 512 * (2 * tt + j):512 * (2 * tt + j + 1)],
                            AO[:, 2048 * c + 128 * (4 * tgi + tt):2048 * c + 128 * (4 * tgi + tt) + 128],
                            WOB[:, 1024 * (c % 2) + 512 * j:1024 * (c % 2) + 512 * (j + 1)],
                            start=(c == 0), stop=(c == 7))
                        if tt == 3 and j == 1:
                            R.inc("tensor", fn, mmPc)
                        else:
                            R.op("tensor", fn)
            R.wait("vector", mmPc)
            if tgi == 0:
                R.wait("sync", psm["s_app"])
            for tt in range(4):
                t_glob = tgi * 4 + tt
                if t_glob == 0:
                    epi_prefetch_x(R, ep, esm, x_src, seq, 0)
                emit_epilogue_tile(
                    R, ep, esm, seq, t_glob,
                    (lambda tt=tt: ps_all[:, 1024 * tt:1024 * (tt + 1)]),
                    x_dst, bias_bcast=PROJB)
                if t_glob < NT - 1:
                    epi_prefetch_x(R, ep, esm, x_src, seq, t_glob + 1)
    R.wait("gpsimd", esm["st"])




def emit_phase4(R, g, x_src, x_dst):
    R.init_epsc()
    cl = R.sem("p4cl")
    IDENTF = R.sb("p4idf", [128, 128], F32)
    CENT = R.sb("p4cent", [128, 64], F32)
    CENTR = R.sb("p4centr", [128, 64], F32R)
    BETB = R.sb("p4betb", [128, 64], BF16)
    GAM = R.sb("p4gam", [128, 64], F32)
    B1IE = R.sb("p4b1i", [128, 32], F32)
    B1C = R.sb("p4b1c", [128, 32], F32)
    MA = R.sb("p4ma", [128, 128], F32)
    AFFS = R.sb("p4affs", [128, 8], F32)
    RMAL = R.sb("p4rmal", [128, 2], F32)
    ONESB = R.sb("p4ones", [1, 128], BF16)
    IDENTB = R.sb("p4idb", [128, 128], BF16)
    B2R = R.sb("p4b2r", [1, 512], BF16)
    R.inc("sync", lambda e: e.dma_start(out=IDENTF[:], in_=g["ident_f32"][:, :]), cl, 16)
    R.inc("sync", lambda e: e.dma_start(out=CENT[:], in_=g["p4_cent"][:, :]), cl, 16)
    R.inc("sync", lambda e: e.dma_start(out=GAM[:], in_=g["p4_gamma"][:, :]), cl, 16)

    R.inc("gpsimd", lambda e: e.dma_start(out=BETB[:], in_=g["p4_beta"][:, :]), cl, 16)
    R.inc("gpsimd", lambda e: e.dma_start(out=ONESB[:], in_=g["ones_row"][:, :]), cl, 16)
    R.inc("sync", lambda e: e.dma_start(out=IDENTB[:], in_=g["ident_bf"][:, :]), cl, 16)
    for e in ENGINES:
        R.wait(e, cl)
    R.op("vector", lambda e: e.tensor_copy(CENTR[:, :], CENT[:, :]))
    R.op("vector", lambda e: e.drain())

    ps_all = R.ps("p4ps", [128, 4096], F32)
    psXH = ps_all[:, 2048:2560].bitcast(BF16)
    u_acc = R.sb("p4ua", [128, 16 * 1024], F32)
    XHF = R.sb("p4xhf", [128, 8 * 2048], BF16)
    XS = R.sb("p4xs", [128, 16 * 1024], BF16)
    R_ = R.sb("p4r", [128, 32 * 1024], BF16)
    W1B = R.sb("p4w1b", [128, 2 * 1024], BF16)
    UF = R.sb("p4uf", [128, 1024], F32R)
    XH = R_[:, 2048:3072]
    pt = mk_preln_tensors(R, "p4pl", need_ufm=False, XT=XS[:, 0:4096].bitcast(F32))
    psm = mk_preln_sems(R, "p4pl")
    ep = mk_epi_tensors(R, "p4ep", None, pt["XT"][:, 0:1024], stats_from=pt,
                        TMPD=XS[:, 4096:6144].bitcast(F32))
    esm = mk_epi_sems(R, "p4ep")
    trU = R.sem("p4trU")
    trUe = R.sem("p4trUe")
    afM = R.sem("p4afM")
    roD = R.sem("p4roD")
    roA = R.sem("p4roA")
    trX = R.sem("p4trX")
    trXe = R.sem("p4trXe")
    w1l = R.sem("p4w1l")
    mmS1 = R.sem("p4mmS1")
    b1d = R.sem("p4b1d")
    relE = R.sem("p4relE")
    w2l = R.sem("p4w2l")
    b2l = R.sem("p4b2l")
    mmS2 = R.sem("p4mmS2")
    uaE = R.sem("p4uaE")

    for seq in range(NSEQ):
        if seq > 0:
            R.wait("sync", esm["st"])
            R.wait("sync", mmS1)
            R.wait("sync", mmS2)
            R.wait("sync", w2l)
            R.wait("tensor", uaE)
            R.wait("tensor", esm["st"])
            R.wait("vector", mmS2)
            R.wait("vector", esm["st"])
        emit_preln(R, pt, psm, x_src, seq, u_f32=u_acc)

        v = dict(BN6=pt["BN6"], MV=pt["MV"], SD=pt["SD"], RSTD=pt["RSTD"], NMR=pt["NMR"],
                 s_d2a=psm["s_d2a"], s_a2d=psm["s_a2d"])
        for t in range(NT):
            R.wait("vector", psm["s_app"], psm["s_app"].n - (NT - 1 - t))
            R.op("vector", lambda e, t=t: e.bn_stats(pt["BN6"][:, 12 * t:12 * t + 6], u_acc[:, 1024 * t:1024 * t + 512]))
            R.op("vector", lambda e, t=t: e.bn_stats(pt["BN6"][:, 12 * t + 6:12 * t + 12], u_acc[:, 1024 * t + 512:1024 * (t + 1)]))
            ln_rstd_chain(R, v, t)
            R.wait("vector", trX)
            R.inc("vector", lambda e, t=t: e.tensor_scalar(
                XH[:, :], u_acc[:, 1024 * t:1024 * (t + 1)],
                pt["MV"][:, 2 * t:2 * t + 1], pt["RSTD"][:, t:t + 1], op0=ALU.subtract, op1=ALU.mult), psm["s_app"])
            par = t % 2
            R.wait("tensor", psm["s_app"], psm["s_app"].n - 1)
            R.wait_rel("tensor", trUe, 1)
            for d in range(8):
                fn = lambda e, t=t, d=d, par=par: e.transpose(
                    ps_all[:, 1024 * par + 128 * d:1024 * par + 128 * (d + 1)],
                    u_acc[:, 1024 * t + 128 * d:1024 * t + 128 * (d + 1)], IDENTF[:, :])
                if d == 7:
                    R.inc("tensor", fn, trU)
                else:
                    R.op("tensor", fn)
            R.wait("vector", trU)
            R.inc("vector", lambda e, par=par: e.tensor_copy(UF[:, :], ps_all[:, 1024 * par:1024 * (par + 1)]), trUe)
            R.wait("tensor", trUe)
            R.wait("tensor", roD)
            for d in range(8):
                fn = lambda e, d=d: e.matmul(
                    ps_all[:, 3584:3592], UF[:, 128 * d:128 * (d + 1)], CENTR[:, 8 * d:8 * (d + 1)],
                    start=(d == 0), stop=(d == 7))
                if d == 7:
                    R.inc("tensor", fn, afM)
                else:
                    R.op("tensor", fn)
            R.wait("vector", afM)
            R.op("vector", lambda e: e.tensor_copy(AFFS[:, :], ps_all[:, 3584:3592]))
            R.op("vector", lambda e: e.drain())
            R.op("vector", lambda e: e.reduce_max(RMAL[:, 0:1], AFFS[:, :], axis=AX.X))
            R.inc("vector", lambda e: e.drain(), roD)
            R.wait("scalar", roD)
            R.inc("scalar", lambda e: e.activation(out=RMAL[:, 1:2], in_=RMAL[:, 0:1], func=AF.Sigmoid), roA)
            R.wait("vector", roA)
            R.op("vector", lambda e, t=t: e.tensor_scalar(
                MA[:, 8 * t:8 * (t + 1)], AFFS[:, :], RMAL[:, 0:1], None, op0=ALU.is_ge))
            R.op("vector", lambda e: e.drain())
            R.op("vector", lambda e, t=t: e.tensor_scalar(
                MA[:, 8 * t:8 * (t + 1)], MA[:, 8 * t:8 * (t + 1)], RMAL[:, 1:2], None, op0=ALU.mult))
            R.wait("tensor", psm["s_app"])
            R.wait("tensor", trXe)
            for d in range(8):
                fn = lambda e, d=d: e.transpose(
                    psXH[:, 128 * d:128 * (d + 1)], XH[:, 128 * d:128 * (d + 1)], IDENTB[:, :])
                if d == 7:
                    R.inc("tensor", fn, trX)
                else:
                    R.op("tensor", fn)
            R.wait("vector", trX)
            R.inc("vector", lambda e, t=t: e.tensor_copy(
                XHF[:, :].rearrange("p (d s) -> p d s", d=8)[:, :, 128 * t:128 * (t + 1)],
                psXH[:, :].rearrange("p (d s) -> p d s", d=8)), trXe)

        R.wait("tensor", trXe)
        R.wait("tensor", trUe)
        R.wait("tensor", roD)
        for tc in range(2):
            for ex in range(8):
                R.wait("vector", mmS2)
                R.wait("vector", mmS1)
                for d in range(8):
                    fn = lambda e, d=d, ex=ex, tc=tc: e.tensor_scalar(
                        XS[:, 1024 * d:1024 * (d + 1)],
                        XHF[:, 2048 * d + 1024 * tc:2048 * d + 1024 * (tc + 1)],
                        GAM[:, 8 * ex + d:8 * ex + d + 1], None, op0=ALU.mult)
                    if d == 7:
                        R.inc("vector", fn, roD)
                    else:
                        R.op("vector", fn)
                R.wait("sync", b1d)
                R.inc("sync", lambda e, ex=ex: e.dma_start(out=B1IE[:, :], in_=g["p4_b1"][:, 32 * ex:32 * (ex + 1)]), cl, 16)
                R.wait("tensor", roD)
                R.wait("tensor", uaE)
                for c in range(32):
                    par = c % 2
                    R.wait_rel("gpsimd", mmS1, 1)
                    R.inc("gpsimd", lambda e, c=c, ex=ex, par=par: e.dma_start(
                        out=W1B[:, 1024 * par:1024 * (par + 1)],
                        in_=g["p4_w1"][:, 1024 * (32 * ex + c):1024 * (32 * ex + c + 1)]), w1l, 16)
                    R.wait("tensor", w1l)
                    bcol = 2048 if par == 0 else 3584
                    R.wait_rel("tensor", b1d, 1)
                    for d in range(8):
                        R.op("tensor", lambda e, d=d, ex=ex, par=par, bcol=bcol: e.matmul(
                            ps_all[:, bcol:bcol + 1], W1B[:, 1024 * par + 128 * d:1024 * par + 128 * (d + 1)],
                            BETB[:, 8 * ex + d:8 * ex + d + 1],
                            start=(d == 0), stop=(d == 7)))
                    R.wait_rel("tensor", relE, 1)
                    for pc in range(2):
                        for d in range(8):
                            fn = lambda e, c=c, pc=pc, d=d, par=par: e.matmul(
                                ps_all[:, 512 * (2 * par + pc):512 * (2 * par + pc + 1)],
                                W1B[:, 1024 * par + 128 * d:1024 * par + 128 * (d + 1)],
                                XS[:, 1024 * d + 512 * pc:1024 * d + 512 * (pc + 1)],
                                start=(d == 0), stop=(d == 7))
                            if pc == 1 and d == 7:
                                R.inc("tensor", fn, mmS1)
                            else:
                                R.op("tensor", fn)
                    R.wait("vector", mmS1)
                    if c == 0:
                        R.wait("vector", cl)
                    R.inc("vector", lambda e, c=c, bcol=bcol: e.tensor_tensor(
                        B1C[:, c:c + 1], ps_all[:, bcol:bcol + 1], B1IE[:, c:c + 1], op=ALU.add), b1d)
                    R.wait("scalar", b1d)
                    R.wait("scalar", mmS1)
                    for pc in range(2):
                        fn = lambda e, c=c, pc=pc, par=par: e.activation(
                            out=R_[:, 512 * (2 * c + pc):512 * (2 * c + pc + 1)],
                            in_=ps_all[:, 512 * (2 * par + pc):512 * (2 * par + pc + 1)],
                            func=AF.Relu, bias=B1C[:, c:c + 1])
                        if pc == 1:
                            R.inc("scalar", fn, relE)
                        else:
                            R.op("scalar", fn)
                R.wait("gpsimd", mmS1)
                for half in range(2):
                    if half == 1:
                        R.wait("gpsimd", mmS2)
                    R.inc("gpsimd", lambda e, ex=ex, half=half: e.dma_start(
                        out=B2R[:, :], in_=g["p4_b2"][0:1, 1024 * ex + 512 * half:1024 * ex + 512 * (half + 1)]), b2l, 16)
                    R.inc("gpsimd", lambda e, ex=ex, half=half: e.dma_start(
                        out=XS[:, :].rearrange("p (c s) -> p c s", c=32),
                        in_=g["p4_w2"][:, :].rearrange("p (E c h s) -> p E c h s", E=8, c=32, h=2)[:, ex, :, half, :]),
                        w2l, 16)
                    R.wait("tensor", w2l)
                    R.wait("tensor", relE)
                    R.wait("tensor", b2l)
                    if half == 0:
                        R.wait("tensor", b1d)
                    R.wait("tensor", uaE)
                    for c in range(32):
                        for tt in range(8):
                            R.op("tensor", lambda e, c=c, tt=tt: e.matmul(
                                ps_all[:, 512 * tt:512 * (tt + 1)],
                                R_[:, 512 * (2 * c + tt // 4) + 128 * (tt % 4):512 * (2 * c + tt // 4) + 128 * (tt % 4) + 128],
                                XS[:, 512 * c:512 * (c + 1)],
                                start=(c == 0), stop=False))
                    for tt in range(8):
                        fn = lambda e, tt=tt: e.matmul(
                            ps_all[:, 512 * tt:512 * (tt + 1)], ONESB[:, :], B2R[0:1, 0:512],
                            start=False, stop=True)
                        if tt == 7:
                            R.inc("tensor", fn, mmS2)
                        else:
                            R.op("tensor", fn)
                    R.wait("vector", mmS2)
                    for tt in range(8):
                        T = tc * 8 + tt
                        fn = lambda e, tt=tt, T=T, ex=ex, half=half: e.scalar_tensor_tensor(
                            u_acc[:, 1024 * T + 512 * half:1024 * T + 512 * (half + 1)],
                            ps_all[:, 512 * tt:512 * (tt + 1)],
                            MA[:, 8 * T + ex:8 * T + ex + 1],
                            u_acc[:, 1024 * T + 512 * half:1024 * T + 512 * (half + 1)],
                            op0=ALU.mult, op1=ALU.add)
                        if tt == 7:
                            R.inc("vector", fn, uaE)
                        else:
                            R.op("vector", fn)

        R.wait("vector", uaE)
        R.wait("sync", psm["s_app"])
        R.wait("sync", mmS2)
        R.wait("sync", w2l)
        for t in range(NT):
            if t == 0:
                epi_prefetch_x(R, ep, esm, x_src, seq, 0)
            emit_epilogue_tile(
                R, ep, esm, seq, t,
                (lambda t=t: u_acc[:, 1024 * t:1024 * (t + 1)]),
                x_dst, direct_sbuf=True)
            if t < NT - 1:
                epi_prefetch_x(R, ep, esm, x_src, seq, t + 1)
    R.wait("gpsimd", esm["st"])


PHASE_EMITTERS = {1: emit_phase1, 2: emit_phase2, 3: emit_phase3, 4: emit_phase4}


def _run(phase_list, imgs, x_full):
    nc = bass.Bass("TRN2", target_bir_lowering=False)
    g = {}
    for name, arr in imgs.items():
        dt = IMG_DTYPES.get(name, F32)
        g[name] = nc.dram_tensor(name, list(arr.shape), dt, kind="ExternalInput").ap()
    x_in = nc.dram_tensor("x_in", [NSEQ * S, D], F32, kind="ExternalInput").ap()
    x_out = nc.dram_tensor("x_out", [NSEQ * S, D], F32, kind="ExternalOutput").ap()
    n_ph = len(phase_list)
    scratch = [nc.dram_tensor(f"x_mid{i}", [NSEQ * S, D], F32, kind="Internal").ap()
               for i in range(max(0, n_ph - 1))]

    with ExitStack() as stack:
        R = Rec(nc, stack)
        for i, ph in enumerate(phase_list):
            src = x_in if i == 0 else scratch[i - 1]
            dst = x_out if i == n_ph - 1 else scratch[i]
            R.phase_begin()
            with ExitStack() as phase_stack:
                R.stack = phase_stack
                PHASE_EMITTERS[ph](R, g, src, dst)
                R.stack = stack
            if i < n_ph - 1:
                R.barrier(f"pb{i}")
        R.replay()
    nc.finalize()

    in_maps = []
    for core in range(NCORES):
        m = dict(imgs)
        m["x_in"] = np.ascontiguousarray(x_full[core * NSEQ * S:(core + 1) * NSEQ * S, :])
        in_maps.append(m)
    res = run_bass_kernel_spmd(nc, in_maps, core_ids=list(range(NCORES)))
    return np.concatenate([res.results[i]["x_out"] for i in range(NCORES)], axis=0)


def kernel(**inputs):
    inputs = {k: np.asarray(v) for k, v in inputs.items()}
    imgs = build_images(inputs)
    x = np.ascontiguousarray(inputs["L0_attn_x"].astype(np.float32).reshape(B * S, D))
    phase_groups = os.environ.get("MK_PHASES", "1,2,3,4").split("/")
    cur = x
    for grp in phase_groups:
        phl = [int(p) for p in grp.split(",")]
        cur = _run(phl, imgs, cur)
    return cur.reshape(B, S, D)


# version 06
# speedup vs baseline: 1.1852x; 1.0239x over previous
import os
import sys
import numpy as np
from contextlib import ExitStack

sys.path.insert(0, "/opt/trn_rl_repo")
import concourse.bass as bass
import concourse.mybir as mybir
from concourse.bass_utils import run_bass_kernel_spmd

F32 = mybir.dt.float32
F32R = mybir.dt.float32r
BF16 = mybir.dt.bfloat16
AF = mybir.ActivationFunctionType
ALU = mybir.AluOpType
AX = mybir.AxisListType

NCORES = 8
B, S, D = 32, 2048, 1024
NSEQ = B // NCORES
NT = S // 128
H, HD = 16, 64
DH = 4096
EPS = 1e-5

ENGINES = ("sync", "tensor", "vector", "scalar", "gpsimd")



def lhsT_image(W):
    O, I = W.shape
    Oc, Ic = O // 128, I // 128
    return np.ascontiguousarray(W.reshape(Oc, 128, Ic, 128).transpose(3, 0, 2, 1).reshape(128, Oc * Ic * 128))


def rhs_image(WT):
    K, O = WT.shape
    Kc = K // 128
    return np.ascontiguousarray(WT.reshape(Kc, 128, O).transpose(1, 0, 2).reshape(128, Kc * O))


def cols_image(v):
    n = v.shape[0] // 128
    return np.ascontiguousarray(v.reshape(n, 128).T)


def bcast_image(v):
    return np.ascontiguousarray(np.tile(np.asarray(v)[None, :], (128, 1)))


def build_images(inp):
    import ml_dtypes
    img = {}
    img["p2_w1"] = lhsT_image(inp["L0_ffn_w1_w"])
    img["p2_w2"] = lhsT_image(inp["L0_ffn_w2_w"])
    img["p2_h1"] = lhsT_image(inp["L0_ffn_h1_w"])
    img["p2_h2"] = lhsT_image(inp["L0_ffn_h2_w"])
    img["p2_w3"] = rhs_image(np.ascontiguousarray(inp["L0_ffn_w3_w"].T))
    img["p2_s12"] = np.concatenate([cols_image(inp["L0_ffn_w1_s"]),
                                    cols_image(inp["L0_ffn_w2_s"]),
                                    cols_image(inp["L0_ffn_h1_s"]),
                                    cols_image(inp["L0_ffn_h2_s"])], axis=1)
    img["p2_w3sb"] = bcast_image(inp["L0_ffn_w3_s"])

    perm_q = []
    for gi in range(4):
        for half in range(2):
            for hh in range(4):
                h = 4 * gi + hh
                for i in range(32):
                    perm_q.append(h * 64 + 2 * i + half)
    perm_q = np.array(perm_q)
    Wq = inp["L0_attn_W_qkv"][0:1024][perm_q]
    Wk = inp["L0_attn_W_qkv"][1024:2048][perm_q]
    Wv = inp["L0_attn_W_qkv"][2048:3072]
    img["p1_qkv"] = lhsT_image(np.concatenate([Wq, Wk, Wv], axis=0))
    img["p1_wo"] = rhs_image(np.ascontiguousarray(inp["L0_attn_W_o"].T))
    t = np.arange(S, dtype=np.float64)
    theta = 10000.0 ** (-2.0 * (np.arange(32, dtype=np.float64) / 64.0))
    ang = t[None, :] * theta[:, None]
    img["p1_cc"] = np.ascontiguousarray(np.tile(np.cos(ang).astype(np.float32), (4, 1)))
    img["p1_ss"] = np.ascontiguousarray(np.tile(np.sin(ang).astype(np.float32), (4, 1)))
    m = np.zeros((128, 4 * 512), dtype=np.float32)
    for v in range(4):
        blk = np.zeros((128, 512), dtype=np.float32)
        blk[:, v * 128:(v + 1) * 128] = np.triu(np.full((128, 128), -1e30, dtype=np.float32), 1)
        blk[:, (v + 1) * 128:] = -1e30
        m[:, v * 512:(v + 1) * 512] = blk
    img["p1_mask"] = m
    img["ident_bf"] = np.eye(128, dtype=ml_dtypes.bfloat16)
    img["ident_f32"] = np.eye(128, dtype=np.float32)

    w3q = np.concatenate(
        [np.ascontiguousarray(inp["L1_attn_Wq_w"].T),
         np.ascontiguousarray(inp["L1_attn_Wk_w"].T),
         np.ascontiguousarray(inp["L1_attn_Wv_w"].T)], axis=1)
    img["p3_wqkv"] = np.ascontiguousarray(np.tile(w3q, (2, 1)))
    b3q = np.stack([inp["L1_attn_Wq_b"], inp["L1_attn_Wk_b"], inp["L1_attn_Wv_b"]], axis=1)
    img["p3_bqkv"] = np.ascontiguousarray(np.tile(b3q, (2, 1)))
    img["p3_proj"] = rhs_image(np.ascontiguousarray(inp["L1_attn_proj_w"].T))
    img["p3_projb"] = bcast_image(inp["L1_attn_proj_b"])

    w1 = inp["L1_ffn_w1"]
    img["p4_w1"] = np.ascontiguousarray(
        w1.reshape(8, 8, 128, 32, 128).transpose(2, 0, 3, 1, 4).reshape(128, 8 * 32 * 8 * 128))
    w2 = inp["L1_ffn_w2"]
    img["p4_w2"] = np.ascontiguousarray(
        w2.reshape(8, 32, 128, 1024).transpose(2, 0, 1, 3).reshape(128, 8 * 32 * 1024))
    img["p4_b1"] = np.concatenate([cols_image(inp["L1_ffn_b1"][e]) for e in range(8)], axis=1)
    img["p4_beta"] = np.concatenate([cols_image(inp["L1_ffn_beta"][e]) for e in range(8)], axis=1)
    img["p4_gamma"] = np.concatenate([cols_image(inp["L1_ffn_gamma"][e]) for e in range(8)], axis=1)
    img["p4_b2"] = np.ascontiguousarray(inp["L1_ffn_b2"].reshape(1, 8 * 1024))
    cent = inp["L1_ffn_centroids"]
    img["p4_cent"] = np.ascontiguousarray(cent.T.reshape(8, 128, 8).transpose(1, 0, 2).reshape(128, 64))
    img["ones_row"] = np.ones((1, 128), dtype=np.float32)
    return img


IMG_DTYPES = {"ident_bf": BF16}



class Sem:
    __slots__ = ("name", "n", "h")

    def __init__(self, name):
        self.name = name
        self.n = 0
        self.h = None


class Rec:
    def __init__(self, nc, stack):
        self.nc = nc
        self.root_stack = stack
        self.stack = stack
        self.ops = {e: [] for e in ENGINES}
        self.nsems = 0
        self.epsc = None
        self._sem_pool = []
        self._sem_idx = 0

    def init_epsc(self):
        if self.epsc is None:
            old = self.stack
            self.stack = self.root_stack
            self.epsc = self.sb("epsc", [128, 1], F32)
            self.stack = old
            self.op("vector", lambda e: e.memset(self.epsc[:, :], float(EPS)))
            self.op("vector", lambda e: e.drain())

    def sem(self, name):
        if self._sem_idx < len(self._sem_pool):
            s = self._sem_pool[self._sem_idx]
        else:
            s = Sem(name)
            s.h = self.root_stack.enter_context(self.nc.semaphore(f"s{len(self._sem_pool)}"))
            self._sem_pool.append(s)
            self.nsems += 1
        self._sem_idx += 1
        return s

    def phase_begin(self):
        self._sem_idx = 0

    def sb(self, name, shape, dt=F32):
        return self.stack.enter_context(self.nc.sbuf_tensor(name, shape, dt))

    def ps(self, name, shape, dt=F32):
        return self.stack.enter_context(self.nc.psum_tensor(name, shape, dt))

    def op(self, eng, fn):
        self.ops[eng].append(fn)

    def inc(self, eng, fn, sem, k=1):
        sem.n += k
        self.ops[eng].append(lambda e, fn=fn, h=sem.h, k=k: fn(e).then_inc(h, k))

    def wait(self, eng, sem, val=None):
        v = sem.n if val is None else val
        if v <= 0:
            return
        self.ops[eng].append(lambda e, h=sem.h, v=v: e.wait_ge(h, v))

    def wait_rel(self, eng, sem, behind):
        self.wait(eng, sem, sem.n - behind)

    def attach_inc(self, eng, sem, k=1):
        sem.n += k
        old = self.ops[eng][-1]
        self.ops[eng][-1] = lambda e, old=old, h=sem.h, k=k: old(e).then_inc(h, k)

    def barrier(self, name):
        s = self.sem(name)
        for e in ENGINES:
            s.n += 1
            self.ops[e].append(lambda x, h=s.h: x.sem_inc(h, 1))
        tgt = s.n
        for e in ENGINES:
            self.wait(e, s, tgt)

    def replay(self):
        block = self.root_stack.enter_context(self.nc.Block())
        for ename in ENGINES:
            oplist = self.ops[ename]

            def body(eng, oplist=oplist):
                for fn in oplist:
                    fn(eng)
            getattr(block, ename)(body)



def ln_rstd_chain(R, v, t):
    R.op("vector", lambda e: e.drain())
    R.inc("vector", lambda e, t=t: e.bn_aggr(v["MV"][:, 2 * t:2 * t + 2], v["BN6"][:, 12 * t:12 * t + 12]),
          v["s_d2a"])
    R.wait("scalar", v["s_d2a"])
    R.op("scalar", lambda e, t=t: e.activation(out=v["SD"][:, t:t + 1], in_=v["MV"][:, 2 * t + 1:2 * t + 2],
                                               func=AF.Ln, bias=R.epsc[:, 0:1]))
    R.op("scalar", lambda e: e.drain())
    R.inc("scalar", lambda e, t=t: e.activation(out=v["RSTD"][:, t:t + 1], in_=v["SD"][:, t:t + 1],
                                                func=AF.Exp, scale=-0.5), v["s_a2d"])
    R.wait("vector", v["s_a2d"])


def mk_preln_sems(R, nm):
    return dict(xld=R.sem(nm + "xld"), s_d2a=R.sem(nm + "d2a"), s_a2d=R.sem(nm + "a2d"),
                s_app=R.sem(nm + "app"), s_tp=R.sem(nm + "tp"), s_ev=R.sem(nm + "ev"))


def mk_preln_tensors(R, nm, need_ufm=True, XT=None):
    d = dict(XT=XT if XT is not None else R.sb(nm + "XT", [128, 2 * 1024], F32),
             BN6=R.sb(nm + "BN6", [128, 12 * NT], F32), MV=R.sb(nm + "MV", [128, 2 * NT], F32),
             SD=R.sb(nm + "SD", [128, NT], F32), RSTD=R.sb(nm + "RSTD", [128, NT], F32),
             NMR=R.sb(nm + "NMR", [128, NT], F32))
    if need_ufm:
        d["UB"] = R.sb(nm + "UB", [128, 2 * 1024], BF16)
    return d


def emit_preln(R, pt, sm, x_src, seq, U_fm=None, u_f32=None):
    v = dict(BN6=pt["BN6"], MV=pt["MV"], SD=pt["SD"], RSTD=pt["RSTD"], NMR=pt["NMR"],
             s_d2a=sm["s_d2a"], s_a2d=sm["s_a2d"])
    for t in range(NT):
        sl = 1024 * (t % 2)
        R.wait_rel("sync", sm["s_app"], 1)
        R.inc("sync", lambda e, t=t, sl=sl: e.dma_start(out=pt["XT"][:, sl:sl + 1024],
                                                        in_=x_src[(seq * S + 128 * t):(seq * S + 128 * (t + 1)), :]),
              sm["xld"], 16)
        R.wait("vector", sm["xld"])
        R.op("vector", lambda e, t=t, sl=sl: e.bn_stats(pt["BN6"][:, 12 * t:12 * t + 6], pt["XT"][:, sl:sl + 512]))
        R.op("vector", lambda e, t=t, sl=sl: e.bn_stats(pt["BN6"][:, 12 * t + 6:12 * t + 12], pt["XT"][:, sl + 512:sl + 1024]))
        ln_rstd_chain(R, v, t)
        if u_f32 is not None:
            R.inc("vector", lambda e, t=t, sl=sl: e.tensor_scalar(
                u_f32[:, 1024 * t:1024 * (t + 1)], pt["XT"][:, sl:sl + 1024],
                pt["MV"][:, 2 * t:2 * t + 1], pt["RSTD"][:, t:t + 1], op0=ALU.subtract, op1=ALU.mult), sm["s_app"])
        if U_fm is not None:
            R.wait_rel("vector", sm["s_tp"], 1)
            R.inc("vector", lambda e, t=t, sl=sl: e.tensor_scalar(
                pt["UB"][:, sl:sl + 1024], pt["XT"][:, sl:sl + 1024],
                pt["MV"][:, 2 * t:2 * t + 1], pt["RSTD"][:, t:t + 1], op0=ALU.subtract, op1=ALU.mult), sm["s_app"])
            R.wait("tensor", sm["s_app"])
            R.wait_rel("tensor", sm["s_ev"], 1)
            for d in range(8):
                fn = lambda e, t=t, d=d, sl=sl: e.transpose(
                    pt["ps_bf"][:, sl + 128 * d:sl + 128 * (d + 1)],
                    pt["UB"][:, sl + 128 * d:sl + 128 * (d + 1)], pt["ident"][:, :])
                if d == 7:
                    R.inc("tensor", fn, sm["s_tp"])
                else:
                    R.op("tensor", fn)
            R.wait("vector", sm["s_tp"])
            R.inc("vector", lambda e, t=t, sl=sl: e.tensor_copy(
                U_fm[:, :].rearrange("p (d s) -> p d s", d=8)[:, :, 128 * t:128 * (t + 1)],
                pt["ps_bf"][:, sl:sl + 1024].rearrange("p (d s) -> p d s", d=8)),
                sm["s_ev"])


def mk_epi_sems(R, nm):
    return dict(xld=R.sem(nm + "xld"), st=R.sem(nm + "st"), s_apply=R.sem(nm + "ap"),
                s_d2a=R.sem(nm + "Ed2a"), s_a2d=R.sem(nm + "Ea2d"))


def mk_epi_tensors(R, nm, DSB, XT2, stats_from=None, TMPD=None):
    if stats_from is not None:
        return dict(DSB=DSB, XT2=XT2, TMPD=TMPD, BN6=stats_from["BN6"], MV=stats_from["MV"],
                    SD=stats_from["SD"], RSTD=stats_from["RSTD"], NMR=stats_from["NMR"])
    return dict(
        DSB=DSB,
        XT2=XT2,
        TMPD=TMPD,
        BN6=R.sb(nm + "BN6", [128, 12 * NT], F32),
        MV=R.sb(nm + "MV", [128, 2 * NT], F32),
        SD=R.sb(nm + "SD", [128, NT], F32),
        RSTD=R.sb(nm + "RSTD", [128, NT], F32),
        NMR=R.sb(nm + "NMR", [128, NT], F32),
    )


def epi_prefetch_x(R, ep, sm, x_src, seq, t):
    R.wait("sync", sm["st"])
    R.inc("sync", lambda e, t=t: e.dma_start(
        out=ep["XT2"][:, :],
        in_=x_src[(seq * S + 128 * t):(seq * S + 128 * (t + 1)), :]), sm["xld"], 16)


def emit_epilogue_tile(R, ep, sm, seq, t, delta_src_fn, x_dst, bias_bcast=None, scale_bcast=None,
                       direct_sbuf=False):
    v = dict(BN6=ep["BN6"], MV=ep["MV"], SD=ep["SD"], RSTD=ep["RSTD"], NMR=ep["NMR"],
             s_d2a=sm["s_d2a"], s_a2d=sm["s_a2d"])
    if direct_sbuf:
        src = delta_src_fn
    else:
        if scale_bcast is not None:
            R.op("vector", lambda e: e.tensor_tensor(ep["DSB"][:, :], delta_src_fn(), scale_bcast[:, :], op=ALU.mult))
        elif bias_bcast is not None:
            R.op("vector", lambda e: e.tensor_tensor(ep["DSB"][:, :], delta_src_fn(), bias_bcast[:, :], op=ALU.add))
        else:
            R.op("vector", lambda e: e.tensor_copy(ep["DSB"][:, :], delta_src_fn()))
        R.op("vector", lambda e: e.drain())
        src = lambda: ep["DSB"][:, :]
    R.op("vector", lambda e, t=t: e.bn_stats(ep["BN6"][:, 12 * t:12 * t + 6], src()[:, 0:512]))
    R.op("vector", lambda e, t=t: e.bn_stats(ep["BN6"][:, 12 * t + 6:12 * t + 12], src()[:, 512:1024]))
    ln_rstd_chain(R, v, t)
    R.wait("vector", sm["xld"])
    R.op("vector", lambda e, t=t: e.tensor_scalar(
        ep["TMPD"][:, :], src(), ep["MV"][:, 2 * t:2 * t + 1], ep["RSTD"][:, t:t + 1],
        op0=ALU.subtract, op1=ALU.mult))
    R.op("vector", lambda e: e.drain())
    R.inc("vector", lambda e, t=t: e.tensor_tensor(
        ep["XT2"][:, :], ep["TMPD"][:, :], ep["XT2"][:, :], op=ALU.add), sm["s_apply"])
    R.wait("gpsimd", sm["s_apply"])
    R.inc("gpsimd", lambda e, t=t: e.dma_start(
        out=x_dst[(seq * S + 128 * t):(seq * S + 128 * (t + 1)), :],
        in_=ep["XT2"][:, :]), sm["st"], 16)



def emit_phase2(R, g, x_src, x_dst):
    R.init_epsc()
    cl = R.sem("p2cl")
    ident = R.sb("p2ident", [128, 128], BF16)
    S12 = R.sb("p2S12", [128, 128], F32)
    W3SB = R.sb("p2w3sb", [128, 1024], F32)
    R.inc("sync", lambda e: e.dma_start(out=ident[:], in_=g["ident_bf"][:, :]), cl, 16)
    R.inc("sync", lambda e: e.dma_start(out=S12[:], in_=g["p2_s12"][:, :]), cl, 16)
    R.inc("sync", lambda e: e.dma_start(out=W3SB[:], in_=g["p2_w3sb"][:, :]), cl, 16)
    for e in ENGINES:
        R.wait(e, cl)

    ps_all = R.ps("p2ps", [128, 4096], F32)
    ps_bf = ps_all[:, 0:1024].bitcast(BF16)
    U_fm = R.sb("p2U", [128, 8 * 2048], BF16)
    A_ = R.sb("p2A", [128, 32 * 1024], BF16)
    B_ = R.sb("p2B", [128, 32 * 1024], BF16)
    W1B = R.sb("p2W1B", [128, 2 * 1024], BF16)
    W2B = R.sb("p2W2B", [128, 2 * 1024], BF16)
    HB = R.sb("p2HB", [128, 2 * 4096], BF16)
    W3B = W1B
    SCR = R.sb("p2SCR", [128, 1024], F32)
    SILU = SCR
    pt = mk_preln_tensors(R, "p2pl")
    pt.update(ps_bf=ps_bf, ident=ident)
    psm = mk_preln_sems(R, "p2pl")
    ep = mk_epi_tensors(R, "p2ep", SCR, pt["XT"][:, 0:1024], TMPD=pt["UB"][:, :].bitcast(F32))
    esm = mk_epi_sems(R, "p2ep")
    wl = R.sem("p2wl")
    mmA = R.sem("p2mmA")
    acA = R.sem("p2acA")
    dvA = R.sem("p2dvA")
    hl = R.sem("p2hl")
    mmH = R.sem("p2mmH")
    acH = R.sem("p2acH")
    w3l = R.sem("p2w3l")
    mmW3c = R.sem("p2mmW3c")

    for seq in range(NSEQ):
        if seq > 0:
            R.wait("sync", esm["st"])
            R.wait("vector", mmA)
            R.wait("tensor", esm["s_apply"])
            R.wait("tensor", esm["st"])
        emit_preln(R, pt, psm, x_src, seq, U_fm=U_fm)
        R.wait("sync", psm["s_app"])
        epi_prefetch_x(R, ep, esm, x_src, seq, 0)

        for tc in range(2):
            R.wait("tensor", psm["s_ev"])
            if tc == 1:
                R.wait("vector", mmW3c)
                R.wait("tensor", esm["s_apply"])
                R.wait("tensor", esm["st"])
            R.wait("gpsimd", mmW3c)
            for c in range(32):
                par = c % 2
                R.wait_rel("gpsimd", mmA, 1)
                R.inc("gpsimd", lambda e, c=c, par=par: e.dma_start(
                    out=W1B[:, 1024 * par:1024 * (par + 1)], in_=g["p2_w1"][:, 1024 * c:1024 * (c + 1)]), wl, 16)
                R.inc("gpsimd", lambda e, c=c, par=par: e.dma_start(
                    out=W2B[:, 1024 * par:1024 * (par + 1)], in_=g["p2_w2"][:, 1024 * c:1024 * (c + 1)]), wl, 16)
                R.wait("tensor", wl)
                R.wait_rel("tensor", dvA, 1)
                for pre in range(2):
                    WB = W1B if pre == 0 else W2B
                    for p in range(2):
                        for d in range(8):
                            fn = lambda e, p=p, d=d, par=par, tc=tc, WB=WB, pre=pre: e.matmul(
                                ps_all[:, 512 * (4 * par + 2 * pre + p):512 * (4 * par + 2 * pre + p + 1)],
                                WB[:, 1024 * par + 128 * d:1024 * par + 128 * (d + 1)],
                                U_fm[:, 2048 * d + 1024 * tc + 512 * p:2048 * d + 1024 * tc + 512 * (p + 1)],
                                start=(d == 0), stop=(d == 7))
                            if pre == 1 and p == 1 and d == 7:
                                R.inc("tensor", fn, mmA)
                            else:
                                R.op("tensor", fn)
                R.wait("scalar", mmA)
                R.wait("scalar", dvA)
                for p in range(2):
                    fn = lambda e, c=c, p=p, par=par: e.activation(
                        out=SILU[:, 512 * p:512 * (p + 1)],
                        in_=ps_all[:, 512 * (4 * par + p):512 * (4 * par + p + 1)],
                        func=AF.Silu, scale=S12[:, c:c + 1])
                    R.inc("scalar", fn, acA) if p == 1 else R.op("scalar", fn)
                R.wait("vector", acA)
                for p in range(2):
                    fn = lambda e, c=c, p=p, par=par: e.scalar_tensor_tensor(
                        A_[:, 1024 * c + 512 * p:1024 * c + 512 * (p + 1)],
                        ps_all[:, 512 * (4 * par + 2 + p):512 * (4 * par + 2 + p + 1)],
                        S12[:, 32 + c:32 + c + 1], SILU[:, 512 * p:512 * (p + 1)],
                        op0=ALU.mult, op1=ALU.mult)
                    R.inc("vector", fn, dvA) if p == 1 else R.op("vector", fn)

            def h_stage(img, srcT, dstT, scol_off, first):
                R.wait("tensor", dvA)
                for c in range(32):
                    par = c % 2
                    R.wait_rel("gpsimd", mmH, 1)
                    R.inc("gpsimd", lambda e, c=c, par=par, img=img: e.dma_start(
                        out=HB[:, 4096 * par:4096 * (par + 1)], in_=img[:, 4096 * c:4096 * (c + 1)]), hl, 16)
                    R.wait("tensor", hl)
                    R.wait_rel("tensor", acH, 1)
                    if first and c == 0:
                        R.wait("tensor", acA)
                        R.wait("tensor", dvA)
                    for p in range(2):
                        for k in range(32):
                            fn = lambda e, c=c, p=p, k=k, par=par, srcT=srcT: e.matmul(
                                ps_all[:, 512 * (2 * par + p):512 * (2 * par + p + 1)],
                                HB[:, 4096 * par + 128 * k:4096 * par + 128 * (k + 1)],
                                srcT[:, 1024 * k + 512 * p:1024 * k + 512 * (p + 1)],
                                start=(k == 0), stop=(k == 31))
                            if p == 1 and k == 31:
                                R.inc("tensor", fn, mmH)
                            else:
                                R.op("tensor", fn)
                    R.wait("scalar", mmH)
                    for p in range(2):
                        fn = lambda e, c=c, p=p, par=par, dstT=dstT, so=scol_off: e.activation(
                            out=dstT[:, 1024 * c + 512 * p:1024 * c + 512 * (p + 1)],
                            in_=ps_all[:, 512 * (2 * par + p):512 * (2 * par + p + 1)],
                            func=AF.Gelu, scale=S12[:, so + c:so + c + 1])
                        R.inc("scalar", fn, acH) if p == 1 else R.op("scalar", fn)

            h_stage(g["p2_h1"], A_, B_, 64, True)
            h_stage(g["p2_h2"], B_, A_, 96, False)

            for tg in range(2):
                R.wait("gpsimd", mmA)
                for c in range(32):
                    par = c % 2
                    R.wait_rel("gpsimd", mmW3c, 1)
                    R.inc("gpsimd", lambda e, c=c, par=par: e.dma_start(
                        out=W3B[:, 1024 * par:1024 * (par + 1)], in_=g["p2_w3"][:, 1024 * c:1024 * (c + 1)]), w3l, 16)
                    R.wait("tensor", w3l)
                    if c == 0:
                        R.wait("tensor", acH)
                        R.wait("tensor", esm["s_apply"])
                    for tt in range(4):
                        for j in range(2):
                            fn = lambda e, c=c, tt=tt, j=j, tc=tc, tg=tg, par=par: e.matmul(
                                ps_all[:, 512 * (2 * tt + j):512 * (2 * tt + j + 1)],
                                A_[:, 1024 * c + 128 * (tg * 4 + tt):1024 * c + 128 * (tg * 4 + tt) + 128],
                                W3B[:, 1024 * par + 512 * j:1024 * par + 512 * (j + 1)],
                                start=(c == 0), stop=(c == 31))
                            if tt == 3 and j == 1:
                                R.inc("tensor", fn, mmW3c)
                            else:
                                R.op("tensor", fn)
                R.wait("vector", mmW3c)
                for tt in range(4):
                    t_glob = tc * 8 + tg * 4 + tt
                    emit_epilogue_tile(
                        R, ep, esm, seq, t_glob,
                        (lambda tt=tt: ps_all[:, 1024 * tt:1024 * (tt + 1)]),
                        x_dst, scale_bcast=W3SB)
                    if t_glob < NT - 1:
                        epi_prefetch_x(R, ep, esm, x_src, seq, t_glob + 1)
    R.wait("gpsimd", esm["st"])





P1_DEBUG = {"stage": ""}


def emit_phase1(R, g, x_src, x_dst):
    R.init_epsc()
    dbg = P1_DEBUG["stage"]
    cl = R.sem("p1cl")
    ident = R.sb("p1ident", [128, 128], BF16)
    MASKD = R.sb("p1MASKD", [128, 128], F32)
    R.inc("sync", lambda e: e.dma_start(out=ident[:], in_=g["ident_bf"][:, :]), cl, 16)
    R.inc("sync", lambda e: e.dma_start(out=MASKD[:], in_=g["p1_mask"][:, 0:128]), cl, 16)
    for e in ENGINES:
        R.wait(e, cl)

    ps_all = R.ps("p1ps", [128, 4096], F32)
    ps_bf = ps_all[:, 0:1024].bitcast(BF16)
    psTv = ps_all[:, 3072:4096].bitcast(BF16)
    U_fm = R.sb("p1U", [128, 8 * 2048], BF16)
    AO = U_fm
    RQ = R.sb("p1RQ", [128, 8 * 2048], BF16)
    RK = R.sb("p1RK", [128, 8 * 2048], BF16)
    VT = R.sb("p1VT", [128, 16 * 1024], BF16)
    SCO = R.sb("p1SCO", [128, 2 * 4 * 2048], BF16)
    CC = SCO[:, 0:4096].bitcast(F32)
    SS = SCO[:, 4096:8192].bitcast(F32)
    PBF = R.sb("p1PBF", [128, 4 * 2048], BF16)
    PTS = R.sb("p1PTS", [128, 2 * 512], BF16)
    RTMP = R.sb("p1RTMP", [128, 2 * 512], F32)
    VSB = RTMP[:, 0:1024].bitcast(BF16)
    QWB = R.sb("p1QWB", [128, 2 * 2048], BF16)
    WOB = QWB[:, 0:2048]
    SMT = R.sb("p1SMT", [128, 16], F32)
    pt = mk_preln_tensors(R, "p1pl")
    pt.update(ps_bf=ps_bf, ident=ident)
    psm = mk_preln_sems(R, "p1pl")
    ep = mk_epi_tensors(R, "p1ep", None, pt["XT"][:, 0:1024], TMPD=PBF[:, 0:2048].bitcast(F32))
    esm = mk_epi_sems(R, "p1ep")
    qw = R.sem("p1qw")
    mmQ = R.sem("p1mmQ")
    rpD = R.sem("p1rpD")
    vE = R.sem("p1vE")
    mmS = R.sem("p1mmS")
    sevE = R.sem("p1sevE")
    sf_d2a = R.sem("p1sfd2a")
    sf_a2d = R.sem("p1sfa2d")
    pN = R.sem("p1pN")
    mmPT = R.sem("p1mmPT")
    ptE = R.sem("p1ptE")
    mmAV = R.sem("p1mmAV")
    aoE = R.sem("p1aoE")
    wol = R.sem("p1wol")
    mmPc = R.sem("p1mmPc")
    ccl = R.sem("p1ccl")

    for seq in range(1 if dbg else NSEQ):
        if seq > 0:
            R.wait("sync", esm["st"])
            R.wait("vector", mmQ)
            R.wait("vector", mmPc)
            R.wait("tensor", esm["s_apply"])
            R.wait("tensor", esm["st"])
        emit_preln(R, pt, psm, x_src, seq, U_fm=U_fm)

        R.wait("sync", sf_a2d)
        R.wait("sync", sevE)
        R.inc("sync", lambda e: e.dma_start(out=CC, in_=g["p1_cc"][:, :]), ccl, 16)
        R.inc("sync", lambda e: e.dma_start(out=SS, in_=g["p1_ss"][:, :]), ccl, 16)
        R.wait("vector", ccl)
        R.wait("tensor", psm["s_ev"])
        for pi in range(8):
            slot = pi % 2
            dst = RQ if pi < 4 else RK
            gcol = (pi % 4) * 2 * 2048
            R.wait_rel("gpsimd", mmQ, 4)
            R.inc("gpsimd", lambda e, pi=pi, slot=slot: e.dma_start(
                out=QWB[:, 2048 * slot:2048 * slot + 1024], in_=g["p1_qkv"][:, 2048 * pi:2048 * pi + 1024]), qw, 16)
            R.inc("gpsimd", lambda e, pi=pi, slot=slot: e.dma_start(
                out=QWB[:, 2048 * slot + 1024:2048 * slot + 2048], in_=g["p1_qkv"][:, 2048 * pi + 1024:2048 * pi + 2048]), qw, 16)
            for p in range(4):
                pp = 4 * pi + p
                bA = 2 * (pp % 2)
                if p == 0:
                    R.wait("tensor", qw)
                R.wait_rel("tensor", rpD, 1)
                for ab in range(2):
                    for d in range(8):
                        fn = lambda e, p=p, d=d, ab=ab, slot=slot, bA=bA: e.matmul(
                            ps_all[:, 512 * (bA + ab):512 * (bA + ab + 1)],
                            QWB[:, 2048 * slot + 1024 * ab + 128 * d:2048 * slot + 1024 * ab + 128 * (d + 1)],
                            U_fm[:, 2048 * d + 512 * p:2048 * d + 512 * (p + 1)],
                            start=(d == 0), stop=(d == 7))
                        if ab == 1 and d == 7:
                            R.inc("tensor", fn, mmQ)
                        else:
                            R.op("tensor", fn)
                R.wait("vector", mmQ)
                psA = lambda bA=bA: ps_all[:, 512 * bA:512 * (bA + 1)]
                psB = lambda bA=bA: ps_all[:, 512 * (bA + 1):512 * (bA + 2)]
                ccp = lambda p=p: CC[:, 512 * p:512 * (p + 1)]
                ssp = lambda p=p: SS[:, 512 * p:512 * (p + 1)]
                R.op("vector", lambda e, psA=psA, ccp=ccp: e.tensor_tensor(RTMP[:, 0:512], ccp(), psA(), op=ALU.mult))
                R.op("vector", lambda e, psB=psB, ssp=ssp: e.tensor_tensor(RTMP[:, 512:1024], ssp(), psB(), op=ALU.mult))
                R.op("vector", lambda e: e.drain())
                R.op("vector", lambda e, p=p, gcol=gcol, dst=dst: e.tensor_tensor(
                    dst[:, gcol + 512 * p:gcol + 512 * (p + 1)], RTMP[:, 0:512], RTMP[:, 512:1024], op=ALU.subtract))
                R.op("vector", lambda e: e.drain())
                R.op("vector", lambda e, psA=psA, ssp=ssp: e.tensor_tensor(RTMP[:, 0:512], ssp(), psA(), op=ALU.mult))
                R.op("vector", lambda e, psB=psB, ccp=ccp: e.tensor_tensor(RTMP[:, 512:1024], ccp(), psB(), op=ALU.mult))
                R.op("vector", lambda e: e.drain())
                R.inc("vector", lambda e, p=p, gcol=gcol, dst=dst: e.tensor_tensor(
                    dst[:, gcol + 2048 + 512 * p:gcol + 2048 + 512 * (p + 1)], RTMP[:, 0:512], RTMP[:, 512:1024], op=ALU.add),
                    rpD)

        for vc in range(8):
            slot = vc % 2
            R.wait_rel("gpsimd", mmQ, 4)
            R.inc("gpsimd", lambda e, vc=vc, slot=slot: e.dma_start(
                out=QWB[:, 2048 * slot:2048 * slot + 1024], in_=g["p1_qkv"][:, 1024 * (16 + vc):1024 * (17 + vc)]), qw, 16)
            for p in range(4):
                pp2 = 4 * vc + p
                bk = 4 + (pp2 % 2)
                if p == 0:
                    R.wait("tensor", qw)
                    if vc == 0:
                        R.wait("tensor", rpD)
                R.wait_rel("tensor", vE, 1)
                for d in range(8):
                    fn = lambda e, p=p, d=d, slot=slot, bk=bk: e.matmul(
                        ps_all[:, 512 * bk:512 * (bk + 1)],
                        QWB[:, 2048 * slot + 128 * d:2048 * slot + 128 * (d + 1)],
                        U_fm[:, 2048 * d + 512 * p:2048 * d + 512 * (p + 1)],
                        start=(d == 0), stop=(d == 7))
                    if d == 7:
                        R.inc("tensor", fn, mmQ)
                    else:
                        R.op("tensor", fn)
                R.wait("scalar", mmQ)
                if p == 0:
                    R.wait("scalar", mmPT)
                if p == 0 and vc == 0:
                    R.wait("scalar", rpD)
                R.inc("scalar", lambda e, p=p, bk=bk: e.activation(
                    out=VSB[:, 512 * p:512 * (p + 1)], in_=ps_all[:, 512 * bk:512 * (bk + 1)], func=AF.Identity),
                    vE)
            R.wait("tensor", vE)
            for j in range(2):
                TB = mmPT.n % 2
                R.wait_rel("tensor", ptE, 1)
                for i in range(8):
                    fn = lambda e, j=j, i=i, TB=TB: e.transpose(
                        psTv[:, 1024 * TB + 128 * i:1024 * TB + 128 * (i + 1)],
                        VSB[:, 128 * (8 * j + i):128 * (8 * j + i + 1)], ident[:, :])
                    if i == 7:
                        R.inc("tensor", fn, mmPT)
                    else:
                        R.op("tensor", fn)
                R.wait("vector", mmPT)
                R.inc("vector", lambda e, j=j, vc=vc, TB=TB: e.tensor_copy(
                    VT[:, :].rearrange("p (kt r) -> p kt r", kt=16)[:, 8 * j:8 * (j + 1), 128 * vc:128 * (vc + 1)],
                    psTv[:, 1024 * TB:1024 * (TB + 1)].rearrange("p (k r) -> p k r", k=8)), ptE)

        if dbg == "qkv":
            dsem = R.sem("p1dbg")
            R.wait("gpsimd", rpD)
            R.wait("gpsimd", ptE)
            for nm2, buf in [("d_rq", RQ), ("d_rk", RK), ("d_vt", VT), ("d_u", U_fm)]:
                R.inc("gpsimd", lambda e, nm2=nm2, buf=buf: e.dma_start(out=g[nm2][:, :], in_=buf[:, :]), dsem, 16)
            R.wait("gpsimd", dsem)
            return
        R.wait("tensor", rpD)
        R.wait("tensor", ptE)
        R.wait("tensor", vE)

        def p1_scores(ci, gi, t):
            sl = (ci % 2) * 8192
            kpn = t // 4 + 1
            vloc = t % 4
            for kp in range(kpn):
                width = min(512, 128 * (t + 1) - 512 * kp)
                R.wait("tensor", sevE)
                for h in range(4):
                    for ab in range(2):
                        fn = lambda e, h=h, ab=ab, gi=gi, t=t, kp=kp, width=width: e.matmul(
                            ps_all[:, 512 * h:512 * h + width],
                            RQ[32 * h:32 * (h + 1), (2 * gi + ab) * 2048 + 128 * t:(2 * gi + ab) * 2048 + 128 * (t + 1)],
                            RK[32 * h:32 * (h + 1), (2 * gi + ab) * 2048 + 512 * kp:(2 * gi + ab) * 2048 + 512 * kp + width],
                            start=(ab == 0), stop=(ab == 1), tile_position=(32 * h, 0))
                        if h == 3 and ab == 1:
                            R.inc("tensor", fn, mmS)
                        else:
                            R.op("tensor", fn)
                diag = (kp == kpn - 1)
                if not diag:
                    R.wait("scalar", mmS)
                    if kp == 0:
                        R.wait("scalar", sf_a2d)
                    for h in range(4):
                        dst0 = sl + 2048 * h + 512 * kp
                        fn = lambda e, h=h, dst0=dst0: e.activation(
                            out=SCO[:, dst0:dst0 + 512], in_=ps_all[:, 512 * h:512 * (h + 1)], func=AF.Identity)
                        R.inc("scalar", fn, sevE) if h == 3 else R.op("scalar", fn)
                else:
                    R.wait("vector", mmS)
                    R.wait("vector", sf_a2d)
                    for h in range(4):
                        dst0 = sl + 2048 * h + 512 * kp
                        if vloc > 0:
                            R.op("vector", lambda e, h=h, dst0=dst0, vloc=vloc: e.tensor_copy(
                                SCO[:, dst0:dst0 + 128 * vloc], ps_all[:, 512 * h:512 * h + 128 * vloc]))
                        fn = lambda e, h=h, dst0=dst0, vloc=vloc, width=width: e.tensor_tensor(
                            SCO[:, dst0 + 128 * vloc:dst0 + width],
                            ps_all[:, 512 * h + 128 * vloc:512 * h + width],
                            MASKD[:, 0:width - 128 * vloc], op=ALU.add)
                        R.inc("vector", fn, sevE) if h == 3 else R.op("vector", fn)

        def p1_softmax(ci, t):
            sl = (ci % 2) * 8192
            L = 128 * (t + 1)
            R.wait("vector", sevE)
            for h in range(4):
                R.op("vector", lambda e, h=h, sl=sl, L=L: e.reduce_max(
                    SMT[:, h:h + 1], SCO[:, sl + 2048 * h:sl + 2048 * h + L], axis=AX.X, negate=True))
            R.op("vector", lambda e: e.drain())
            R.inc("vector", lambda e: e.tensor_scalar(
                SMT[:, 4:8], SMT[:, 0:4], 0.125, None, op0=ALU.mult), sf_d2a)
            R.wait("scalar", sf_d2a)
            R.wait("scalar", mmPT)
            for h in range(4):
                fn = lambda e, h=h, sl=sl, L=L: e.activation(
                    out=PBF[:, 2048 * h:2048 * h + L], in_=SCO[:, sl + 2048 * h:sl + 2048 * h + L],
                    func=AF.Exp, bias=SMT[:, 4 + h:5 + h], scale=0.125, accum_out=SMT[:, 8 + h:9 + h])
                R.inc("scalar", fn, sf_a2d) if h == 3 else R.op("scalar", fn)
            R.wait("vector", sf_a2d)
            R.op("vector", lambda e: e.reciprocal(SMT[:, 12:16], SMT[:, 8:12]))
            R.op("vector", lambda e: e.drain())
            for h in range(4):
                fn = lambda e, h=h, L=L: e.tensor_scalar(
                    PBF[:, 2048 * h:2048 * h + L], PBF[:, 2048 * h:2048 * h + L],
                    SMT[:, 12 + h:13 + h], None, op0=ALU.mult)
                R.inc("vector", fn, pN) if h == 3 else R.op("vector", fn)

        def p1_tav(gi, t):
            R.wait("tensor", pN)
            R.wait("tensor", aoE)

            def emit_T(kt):
                TB = mmPT.n % 2
                R.wait_rel("tensor", ptE, 1)
                for h in range(4):
                    fn = lambda e, kt=kt, h=h, TB=TB: e.transpose(
                        psTv[:, 1024 * TB + 128 * h:1024 * TB + 128 * (h + 1)],
                        PBF[:, 2048 * h + 128 * kt:2048 * h + 128 * (kt + 1)], ident[:, :])
                    if h == 3:
                        R.inc("tensor", fn, mmPT)
                    else:
                        R.op("tensor", fn)
                R.wait("vector", mmPT)
                R.inc("vector", lambda e, TB=TB, kt=kt: e.tensor_copy(
                    PTS[:, 512 * (kt % 2):512 * (kt % 2 + 1)], psTv[:, 1024 * TB:1024 * TB + 512]), ptE)

            def emit_AV(kt, t, gi):
                pb = kt % 2
                R.wait("tensor", ptE)
                for pr in range(2):
                    for cg in range(2):
                        fn = lambda e, kt=kt, t=t, gi=gi, pr=pr, cg=cg, pb=pb: e.matmul(
                            ps_all[64 * cg:64 * (cg + 1), 2048 + 512 * pr:2048 + 512 * pr + 128],
                            VT[:, (kt * 8 + 2 * gi + pr) * 128 + 64 * cg:(kt * 8 + 2 * gi + pr) * 128 + 64 * (cg + 1)],
                            PTS[:, 512 * pb + 128 * (2 * pr + cg):512 * pb + 128 * (2 * pr + cg + 1)],
                            start=(kt == 0), stop=(kt == t), tile_position=(0, 64 * cg))
                        if pr == 1 and cg == 1 and kt == t:
                            R.inc("tensor", fn, mmAV)
                        else:
                            R.op("tensor", fn)

            for kt in range(t + 1):
                emit_T(kt)
                if kt > 0:
                    emit_AV(kt - 1, t, gi)
            emit_AV(t, t, gi)
            R.wait("vector", mmAV)
            R.op("vector", lambda e, gi=gi, t=t: e.tensor_copy(
                AO[:, (2 * gi) * 2048 + 128 * t:(2 * gi) * 2048 + 128 * (t + 1)], ps_all[:, 2048:2176]))
            R.inc("vector", lambda e, gi=gi, t=t: e.tensor_copy(
                AO[:, (2 * gi + 1) * 2048 + 128 * t:(2 * gi + 1) * 2048 + 128 * (t + 1)], ps_all[:, 2560:2688]), aoE)

        combos = [(gi, t) for gi in range(4) for t in range(NT)]
        PIPE = int(os.environ.get("MK_P1PIPE", "1"))
        if PIPE:
            p1_scores(0, *combos[0])
            for ci, (gi, t) in enumerate(combos):
                if ci + 1 < len(combos):
                    p1_scores(ci + 1, *combos[ci + 1])
                p1_softmax(ci, t)
                p1_tav(gi, t)
        else:
            for ci, (gi, t) in enumerate(combos):
                p1_scores(ci, gi, t)
                p1_softmax(ci, t)
                p1_tav(gi, t)

        R.wait("tensor", aoE)
        R.wait("tensor", ptE)
        R.wait("gpsimd", mmQ)
        for tgi in range(4):
            for c in range(8):
                R.wait_rel("gpsimd", mmPc, 1)
                R.inc("gpsimd", lambda e, c=c: e.dma_start(
                    out=WOB[:, 1024 * (c % 2):1024 * (c % 2 + 1)], in_=g["p1_wo"][:, 1024 * c:1024 * (c + 1)]), wol, 16)
                R.wait("tensor", wol)
                if c == 0:
                    R.wait("tensor", esm["s_apply"])
                for tt in range(4):
                    for j in range(2):
                        fn = lambda e, c=c, tt=tt, j=j, tgi=tgi: e.matmul(
                            ps_all[:, 512 * (2 * tt + j):512 * (2 * tt + j + 1)],
                            AO[:, 2048 * c + 128 * (4 * tgi + tt):2048 * c + 128 * (4 * tgi + tt) + 128],
                            WOB[:, 1024 * (c % 2) + 512 * j:1024 * (c % 2) + 512 * (j + 1)],
                            start=(c == 0), stop=(c == 7))
                        if tt == 3 and j == 1:
                            R.inc("tensor", fn, mmPc)
                        else:
                            R.op("tensor", fn)
            R.wait("vector", mmPc)
            if tgi == 0:
                R.wait("sync", psm["s_app"])
            for tt in range(4):
                t_glob = tgi * 4 + tt
                if t_glob == 0:
                    epi_prefetch_x(R, ep, esm, x_src, seq, 0)
                emit_epilogue_tile(
                    R, ep, esm, seq, t_glob,
                    (lambda tt=tt: ps_all[:, 1024 * tt:1024 * (tt + 1)]),
                    x_dst, direct_sbuf=True)
                if t_glob < NT - 1:
                    epi_prefetch_x(R, ep, esm, x_src, seq, t_glob + 1)
    R.wait("gpsimd", esm["st"])




def emit_phase3(R, g, x_src, x_dst):
    R.init_epsc()
    cl = R.sem("p3cl")
    ident = R.sb("p3ident", [128, 128], BF16)
    MASKD = R.sb("p3MASKD", [128, 128], F32)
    W3Q = R.sb("p3W", [128, 192], BF16)
    B3Q = R.sb("p3B", [128, 3], F32)
    PROJB = R.sb("p3PJB", [128, 1024], F32)
    R.inc("sync", lambda e: e.dma_start(out=ident[:], in_=g["ident_bf"][:, :]), cl, 16)
    R.inc("sync", lambda e: e.dma_start(out=MASKD[:], in_=g["p1_mask"][:, 0:128]), cl, 16)
    R.inc("gpsimd", lambda e: e.dma_start(out=W3Q[:], in_=g["p3_wqkv"][:, :]), cl, 16)
    R.inc("sync", lambda e: e.dma_start(out=B3Q[:], in_=g["p3_bqkv"][:, :]), cl, 16)
    R.inc("sync", lambda e: e.dma_start(out=PROJB[:], in_=g["p3_projb"][:, :]), cl, 16)
    for e in ENGINES:
        R.wait(e, cl)

    ps_all = R.ps("p3ps", [128, 4096], F32)
    ps_bf = ps_all[:, 0:1024].bitcast(BF16)
    psTv = ps_all[:, 3072:4096].bitcast(BF16)
    U_fm = R.sb("p3U", [128, 8 * 2048], BF16)
    AO = U_fm
    Q_fm = R.sb("p3Q", [128, 8 * 2048], BF16)
    K_fm = R.sb("p3K", [128, 8 * 2048], BF16)
    VT = R.sb("p3VT", [128, 16 * 1024], BF16)
    SCO = R.sb("p3SCO", [128, 2 * 2 * 2048], BF16)
    PBF = R.sb("p3PBF", [128, 2 * 2048], BF16)
    PTS = R.sb("p3PTS", [128, 2 * 256], BF16)
    VSB = R.sb("p3VSB", [128, 2048], BF16)
    WOB = R.sb("p3WOB", [128, 2 * 1024], BF16)
    SMT = R.sb("p3SMT", [128, 8], F32)
    pt = mk_preln_tensors(R, "p3pl")
    pt.update(ps_bf=ps_bf, ident=ident)
    psm = mk_preln_sems(R, "p3pl")
    ep = mk_epi_tensors(R, "p3ep", R.sb("p3DSB", [128, 1024], F32), pt["XT"][:, 0:1024],
                        TMPD=PBF[:, 0:2048].bitcast(F32))
    esm = mk_epi_sems(R, "p3ep")
    mmQ = R.sem("p3mmQ")
    qkE = R.sem("p3qkE")
    mmPT = R.sem("p3mmPT")
    ptE = R.sem("p3ptE")
    mmS = R.sem("p3mmS")
    sevE = R.sem("p3sevE")
    sf_d2a = R.sem("p3sfd2a")
    sf_a2d = R.sem("p3sfa2d")
    pN = R.sem("p3pN")
    mmAV = R.sem("p3mmAV")
    aoE = R.sem("p3aoE")
    wol = R.sem("p3wol")
    mmPc = R.sem("p3mmPc")

    for seq in range(NSEQ):
        if seq > 0:
            R.wait("sync", esm["st"])
            R.wait("vector", mmQ)
            R.wait("vector", mmPc)
            R.wait("tensor", esm["s_apply"])
            R.wait("tensor", esm["st"])
        emit_preln(R, pt, psm, x_src, seq, U_fm=U_fm)

        R.wait("tensor", psm["s_ev"])
        for pr in range(8):
            for p in range(4):
                pp = 4 * pr + p
                bq = 3 * (pp % 2)
                R.wait_rel("tensor", qkE, 3)
                for b in range(2):
                    base = 64 * b
                    for w in range(3):
                        fn = lambda e, pr=pr, p=p, b=b, w=w, bq=bq, base=base: e.matmul(
                            ps_all[base:base + 64, 512 * (bq + w):512 * (bq + w) + 512],
                            W3Q[base:base + 64, 64 * w:64 * (w + 1)],
                            U_fm[base:base + 64, pr * 2048 + 512 * p:pr * 2048 + 512 * (p + 1)],
                            start=True, stop=True, tile_position=(base, base))
                        if b == 1 and w == 2:
                            R.inc("tensor", fn, mmQ)
                        else:
                            R.op("tensor", fn)
                R.wait("scalar", mmQ)
                if p == 0:
                    R.wait("scalar", mmPT)
                R.op("scalar", lambda e, pr=pr, p=p, bq=bq: e.activation(
                    out=Q_fm[:, pr * 2048 + 512 * p:pr * 2048 + 512 * (p + 1)],
                    in_=ps_all[:, 512 * bq:512 * (bq + 1)], func=AF.Identity, bias=B3Q[:, 0:1]))
                R.op("scalar", lambda e, pr=pr, p=p, bq=bq: e.activation(
                    out=K_fm[:, pr * 2048 + 512 * p:pr * 2048 + 512 * (p + 1)],
                    in_=ps_all[:, 512 * (bq + 1):512 * (bq + 2)], func=AF.Identity, bias=B3Q[:, 1:2]))
                R.inc("scalar", lambda e, p=p, bq=bq: e.activation(
                    out=VSB[:, 512 * p:512 * (p + 1)],
                    in_=ps_all[:, 512 * (bq + 2):512 * (bq + 3)], func=AF.Identity, bias=B3Q[:, 2:3]), qkE, 3)
            R.wait("tensor", qkE)
            for j in range(2):
                TB = mmPT.n % 2
                R.wait_rel("tensor", ptE, 1)
                for i in range(8):
                    fn = lambda e, j=j, i=i, TB=TB: e.transpose(
                        psTv[:, 1024 * TB + 128 * i:1024 * TB + 128 * (i + 1)],
                        VSB[:, 128 * (8 * j + i):128 * (8 * j + i + 1)], ident[:, :])
                    if i == 7:
                        R.inc("tensor", fn, mmPT)
                    else:
                        R.op("tensor", fn)
                R.wait("vector", mmPT)
                R.inc("vector", lambda e, j=j, pr=pr, TB=TB: e.tensor_copy(
                    VT[:, :].rearrange("p (kt r) -> p kt r", kt=16)[:, 8 * j:8 * (j + 1), 128 * pr:128 * (pr + 1)],
                    psTv[:, 1024 * TB:1024 * (TB + 1)].rearrange("p (k r) -> p k r", k=8)), ptE)

        R.wait("tensor", qkE)
        R.wait("tensor", ptE)

        def p3_scores(ci, pr, t):
            sl = (ci % 2) * 4096
            kpn = t // 4 + 1
            vloc = t % 4
            for kp in range(kpn):
                width = min(512, 128 * (t + 1) - 512 * kp)
                R.wait("tensor", sevE)
                for b in range(2):
                    fn = lambda e, pr=pr, t=t, kp=kp, width=width, b=b: e.matmul(
                        ps_all[:, 512 * b:512 * b + width],
                        Q_fm[64 * b:64 * (b + 1), pr * 2048 + 128 * t:pr * 2048 + 128 * (t + 1)],
                        K_fm[64 * b:64 * (b + 1), pr * 2048 + 512 * kp:pr * 2048 + 512 * kp + width],
                        start=True, stop=True, tile_position=(64 * b, 0))
                    if b == 1:
                        R.inc("tensor", fn, mmS)
                    else:
                        R.op("tensor", fn)
                diag = (kp == kpn - 1)
                if not diag:
                    R.wait("scalar", mmS)
                    if kp == 0:
                        R.wait("scalar", sf_a2d)
                    for b in range(2):
                        dst0 = sl + 2048 * b + 512 * kp
                        fn = lambda e, b=b, dst0=dst0: e.activation(
                            out=SCO[:, dst0:dst0 + 512], in_=ps_all[:, 512 * b:512 * (b + 1)], func=AF.Identity)
                        R.inc("scalar", fn, sevE) if b == 1 else R.op("scalar", fn)
                else:
                    R.wait("vector", mmS)
                    R.wait("vector", sf_a2d)
                    for b in range(2):
                        dst0 = sl + 2048 * b + 512 * kp
                        if vloc > 0:
                            R.op("vector", lambda e, b=b, dst0=dst0, vloc=vloc: e.tensor_copy(
                                SCO[:, dst0:dst0 + 128 * vloc], ps_all[:, 512 * b:512 * b + 128 * vloc]))
                        fn = lambda e, b=b, dst0=dst0, vloc=vloc, width=width: e.tensor_tensor(
                            SCO[:, dst0 + 128 * vloc:dst0 + width],
                            ps_all[:, 512 * b + 128 * vloc:512 * b + width],
                            MASKD[:, 0:width - 128 * vloc], op=ALU.add)
                        R.inc("vector", fn, sevE) if b == 1 else R.op("vector", fn)

        def p3_softmax(ci, t):
            sl = (ci % 2) * 4096
            L = 128 * (t + 1)
            R.wait("vector", sevE)
            for b in range(2):
                R.op("vector", lambda e, b=b, sl=sl, L=L: e.reduce_max(
                    SMT[:, b:b + 1], SCO[:, sl + 2048 * b:sl + 2048 * b + L], axis=AX.X, negate=True))
            R.op("vector", lambda e: e.drain())
            R.inc("vector", lambda e: e.tensor_scalar(
                SMT[:, 2:4], SMT[:, 0:2], 0.125, None, op0=ALU.mult), sf_d2a)
            R.wait("scalar", sf_d2a)
            R.wait("scalar", mmPT)
            for b in range(2):
                fn = lambda e, b=b, sl=sl, L=L: e.activation(
                    out=PBF[:, 2048 * b:2048 * b + L], in_=SCO[:, sl + 2048 * b:sl + 2048 * b + L],
                    func=AF.Exp, bias=SMT[:, 2 + b:3 + b], scale=0.125, accum_out=SMT[:, 4 + b:5 + b])
                R.inc("scalar", fn, sf_a2d) if b == 1 else R.op("scalar", fn)
            R.wait("vector", sf_a2d)
            R.op("vector", lambda e: e.reciprocal(SMT[:, 6:8], SMT[:, 4:6]))
            R.op("vector", lambda e: e.drain())
            for b in range(2):
                fn = lambda e, b=b, L=L: e.tensor_scalar(
                    PBF[:, 2048 * b:2048 * b + L], PBF[:, 2048 * b:2048 * b + L],
                    SMT[:, 6 + b:7 + b], None, op0=ALU.mult)
                R.inc("vector", fn, pN) if b == 1 else R.op("vector", fn)

        def p3_tav(pr, t):
            R.wait("tensor", pN)
            R.wait("tensor", aoE)

            def emit_T3(kt):
                TB = mmPT.n % 2
                R.wait_rel("tensor", ptE, 1)
                for b in range(2):
                    fn = lambda e, kt=kt, b=b, TB=TB: e.transpose(
                        psTv[:, 1024 * TB + 128 * b:1024 * TB + 128 * (b + 1)],
                        PBF[:, 2048 * b + 128 * kt:2048 * b + 128 * (kt + 1)], ident[:, :])
                    if b == 1:
                        R.inc("tensor", fn, mmPT)
                    else:
                        R.op("tensor", fn)
                R.wait("vector", mmPT)
                R.inc("vector", lambda e, TB=TB, kt=kt: e.tensor_copy(
                    PTS[:, 256 * (kt % 2):256 * (kt % 2 + 1)], psTv[:, 1024 * TB:1024 * TB + 256]), ptE)

            def emit_AV3(kt, t, pr):
                pb = kt % 2
                R.wait("tensor", ptE)
                for b in range(2):
                    fn = lambda e, kt=kt, t=t, pr=pr, b=b, pb=pb: e.matmul(
                        ps_all[64 * b:64 * (b + 1), 2048:2176],
                        VT[:, (kt * 8 + pr) * 128 + 64 * b:(kt * 8 + pr) * 128 + 64 * (b + 1)],
                        PTS[:, 256 * pb + 128 * b:256 * pb + 128 * (b + 1)],
                        start=(kt == 0), stop=(kt == t), tile_position=(0, 64 * b))
                    if b == 1 and kt == t:
                        R.inc("tensor", fn, mmAV)
                    else:
                        R.op("tensor", fn)

            for kt in range(t + 1):
                emit_T3(kt)
                if kt > 0:
                    emit_AV3(kt - 1, t, pr)
            emit_AV3(t, t, pr)
            R.wait("vector", mmAV)
            R.inc("vector", lambda e, pr=pr, t=t: e.tensor_copy(
                AO[:, pr * 2048 + 128 * t:pr * 2048 + 128 * (t + 1)], ps_all[:, 2048:2176]), aoE)

        combos3 = [(pr, t) for pr in range(8) for t in range(NT)]
        p3_scores(0, *combos3[0])
        for ci, (pr, t) in enumerate(combos3):
            if ci + 1 < len(combos3):
                p3_scores(ci + 1, *combos3[ci + 1])
            p3_softmax(ci, t)
            p3_tav(pr, t)

        R.wait("tensor", aoE)
        R.wait("tensor", ptE)
        for tgi in range(4):
            for c in range(8):
                R.wait_rel("gpsimd", mmPc, 1)
                R.inc("gpsimd", lambda e, c=c: e.dma_start(
                    out=WOB[:, 1024 * (c % 2):1024 * (c % 2 + 1)], in_=g["p3_proj"][:, 1024 * c:1024 * (c + 1)]), wol, 16)
                R.wait("tensor", wol)
                if c == 0:
                    R.wait("tensor", esm["s_apply"])
                for tt in range(4):
                    for j in range(2):
                        fn = lambda e, c=c, tt=tt, j=j, tgi=tgi: e.matmul(
                            ps_all[:, 512 * (2 * tt + j):512 * (2 * tt + j + 1)],
                            AO[:, 2048 * c + 128 * (4 * tgi + tt):2048 * c + 128 * (4 * tgi + tt) + 128],
                            WOB[:, 1024 * (c % 2) + 512 * j:1024 * (c % 2) + 512 * (j + 1)],
                            start=(c == 0), stop=(c == 7))
                        if tt == 3 and j == 1:
                            R.inc("tensor", fn, mmPc)
                        else:
                            R.op("tensor", fn)
            R.wait("vector", mmPc)
            if tgi == 0:
                R.wait("sync", psm["s_app"])
            for tt in range(4):
                t_glob = tgi * 4 + tt
                if t_glob == 0:
                    epi_prefetch_x(R, ep, esm, x_src, seq, 0)
                emit_epilogue_tile(
                    R, ep, esm, seq, t_glob,
                    (lambda tt=tt: ps_all[:, 1024 * tt:1024 * (tt + 1)]),
                    x_dst, bias_bcast=PROJB)
                if t_glob < NT - 1:
                    epi_prefetch_x(R, ep, esm, x_src, seq, t_glob + 1)
    R.wait("gpsimd", esm["st"])




def emit_phase4(R, g, x_src, x_dst):
    R.init_epsc()
    cl = R.sem("p4cl")
    IDENTF = R.sb("p4idf", [128, 128], F32)
    CENT = R.sb("p4cent", [128, 64], F32)
    CENTR = R.sb("p4centr", [128, 64], F32R)
    BETB = R.sb("p4betb", [128, 64], BF16)
    GAM = R.sb("p4gam", [128, 64], F32)
    B1IE = R.sb("p4b1i", [128, 32], F32)
    B1C = R.sb("p4b1c", [128, 32], F32)
    MA = R.sb("p4ma", [128, 128], F32)
    AFFS = R.sb("p4affs", [128, 8], F32)
    RMAL = R.sb("p4rmal", [128, 2], F32)
    ONESB = R.sb("p4ones", [1, 128], BF16)
    IDENTB = R.sb("p4idb", [128, 128], BF16)
    B2R = R.sb("p4b2r", [1, 512], BF16)
    R.inc("sync", lambda e: e.dma_start(out=IDENTF[:], in_=g["ident_f32"][:, :]), cl, 16)
    R.inc("sync", lambda e: e.dma_start(out=CENT[:], in_=g["p4_cent"][:, :]), cl, 16)
    R.inc("sync", lambda e: e.dma_start(out=GAM[:], in_=g["p4_gamma"][:, :]), cl, 16)

    R.inc("gpsimd", lambda e: e.dma_start(out=BETB[:], in_=g["p4_beta"][:, :]), cl, 16)
    R.inc("gpsimd", lambda e: e.dma_start(out=ONESB[:], in_=g["ones_row"][:, :]), cl, 16)
    R.inc("sync", lambda e: e.dma_start(out=IDENTB[:], in_=g["ident_bf"][:, :]), cl, 16)
    for e in ENGINES:
        R.wait(e, cl)
    R.op("vector", lambda e: e.tensor_copy(CENTR[:, :], CENT[:, :]))
    R.op("vector", lambda e: e.drain())

    ps_all = R.ps("p4ps", [128, 4096], F32)
    psXH = ps_all[:, 2048:2560].bitcast(BF16)
    u_acc = R.sb("p4ua", [128, 16 * 1024], F32)
    XHF = R.sb("p4xhf", [128, 8 * 2048], BF16)
    XS = R.sb("p4xs", [128, 16 * 1024], BF16)
    R_ = R.sb("p4r", [128, 32 * 1024], BF16)
    W1B = R.sb("p4w1b", [128, 2 * 1024], BF16)
    UF = R.sb("p4uf", [128, 1024], F32R)
    XH = R_[:, 2048:3072]
    pt = mk_preln_tensors(R, "p4pl", need_ufm=False, XT=XS[:, 0:4096].bitcast(F32))
    psm = mk_preln_sems(R, "p4pl")
    ep = mk_epi_tensors(R, "p4ep", None, pt["XT"][:, 0:1024], stats_from=pt,
                        TMPD=XS[:, 4096:6144].bitcast(F32))
    esm = mk_epi_sems(R, "p4ep")
    trU = R.sem("p4trU")
    trUe = R.sem("p4trUe")
    afM = R.sem("p4afM")
    roD = R.sem("p4roD")
    roA = R.sem("p4roA")
    trX = R.sem("p4trX")
    trXe = R.sem("p4trXe")
    w1l = R.sem("p4w1l")
    mmS1 = R.sem("p4mmS1")
    b1d = R.sem("p4b1d")
    relE = R.sem("p4relE")
    w2l = R.sem("p4w2l")
    b2l = R.sem("p4b2l")
    mmS2 = R.sem("p4mmS2")
    uaE = R.sem("p4uaE")

    for seq in range(NSEQ):
        if seq > 0:
            R.wait("sync", esm["st"])
            R.wait("sync", mmS1)
            R.wait("sync", mmS2)
            R.wait("sync", w2l)
            R.wait("tensor", uaE)
            R.wait("tensor", esm["st"])
            R.wait("vector", mmS2)
            R.wait("vector", esm["st"])
        emit_preln(R, pt, psm, x_src, seq, u_f32=u_acc)

        v = dict(BN6=pt["BN6"], MV=pt["MV"], SD=pt["SD"], RSTD=pt["RSTD"], NMR=pt["NMR"],
                 s_d2a=psm["s_d2a"], s_a2d=psm["s_a2d"])
        for t in range(NT):
            R.wait("vector", psm["s_app"], psm["s_app"].n - (NT - 1 - t))
            R.op("vector", lambda e, t=t: e.bn_stats(pt["BN6"][:, 12 * t:12 * t + 6], u_acc[:, 1024 * t:1024 * t + 512]))
            R.op("vector", lambda e, t=t: e.bn_stats(pt["BN6"][:, 12 * t + 6:12 * t + 12], u_acc[:, 1024 * t + 512:1024 * (t + 1)]))
            ln_rstd_chain(R, v, t)
            R.wait("vector", trX)
            R.inc("vector", lambda e, t=t: e.tensor_scalar(
                XH[:, :], u_acc[:, 1024 * t:1024 * (t + 1)],
                pt["MV"][:, 2 * t:2 * t + 1], pt["RSTD"][:, t:t + 1], op0=ALU.subtract, op1=ALU.mult), psm["s_app"])
            par = t % 2
            R.wait("tensor", psm["s_app"], psm["s_app"].n - 1)
            R.wait_rel("tensor", trUe, 1)
            for d in range(8):
                fn = lambda e, t=t, d=d, par=par: e.transpose(
                    ps_all[:, 1024 * par + 128 * d:1024 * par + 128 * (d + 1)],
                    u_acc[:, 1024 * t + 128 * d:1024 * t + 128 * (d + 1)], IDENTF[:, :])
                if d == 7:
                    R.inc("tensor", fn, trU)
                else:
                    R.op("tensor", fn)
            R.wait("vector", trU)
            R.inc("vector", lambda e, par=par: e.tensor_copy(UF[:, :], ps_all[:, 1024 * par:1024 * (par + 1)]), trUe)
            R.wait("tensor", trUe)
            R.wait("tensor", roD)
            for d in range(8):
                fn = lambda e, d=d: e.matmul(
                    ps_all[:, 3584:3592], UF[:, 128 * d:128 * (d + 1)], CENTR[:, 8 * d:8 * (d + 1)],
                    start=(d == 0), stop=(d == 7))
                if d == 7:
                    R.inc("tensor", fn, afM)
                else:
                    R.op("tensor", fn)
            R.wait("vector", afM)
            R.op("vector", lambda e: e.tensor_copy(AFFS[:, :], ps_all[:, 3584:3592]))
            R.op("vector", lambda e: e.drain())
            R.op("vector", lambda e: e.reduce_max(RMAL[:, 0:1], AFFS[:, :], axis=AX.X))
            R.inc("vector", lambda e: e.drain(), roD)
            R.wait("scalar", roD)
            R.inc("scalar", lambda e: e.activation(out=RMAL[:, 1:2], in_=RMAL[:, 0:1], func=AF.Sigmoid), roA)
            R.wait("vector", roA)
            R.op("vector", lambda e, t=t: e.tensor_scalar(
                MA[:, 8 * t:8 * (t + 1)], AFFS[:, :], RMAL[:, 0:1], None, op0=ALU.is_ge))
            R.op("vector", lambda e: e.drain())
            R.op("vector", lambda e, t=t: e.tensor_scalar(
                MA[:, 8 * t:8 * (t + 1)], MA[:, 8 * t:8 * (t + 1)], RMAL[:, 1:2], None, op0=ALU.mult))
            R.wait("tensor", psm["s_app"])
            R.wait("tensor", trXe)
            for d in range(8):
                fn = lambda e, d=d: e.transpose(
                    psXH[:, 128 * d:128 * (d + 1)], XH[:, 128 * d:128 * (d + 1)], IDENTB[:, :])
                if d == 7:
                    R.inc("tensor", fn, trX)
                else:
                    R.op("tensor", fn)
            R.wait("vector", trX)
            R.inc("vector", lambda e, t=t: e.tensor_copy(
                XHF[:, :].rearrange("p (d s) -> p d s", d=8)[:, :, 128 * t:128 * (t + 1)],
                psXH[:, :].rearrange("p (d s) -> p d s", d=8)), trXe)

        R.wait("tensor", trXe)
        R.wait("tensor", trUe)
        R.wait("tensor", roD)
        for tc in range(2):
            for ex in range(8):
                R.wait("vector", mmS2)
                R.wait("vector", mmS1)
                for d in range(8):
                    fn = lambda e, d=d, ex=ex, tc=tc: e.tensor_scalar(
                        XS[:, 1024 * d:1024 * (d + 1)],
                        XHF[:, 2048 * d + 1024 * tc:2048 * d + 1024 * (tc + 1)],
                        GAM[:, 8 * ex + d:8 * ex + d + 1], None, op0=ALU.mult)
                    if d == 7:
                        R.inc("vector", fn, roD)
                    else:
                        R.op("vector", fn)
                R.wait("sync", b1d)
                R.inc("sync", lambda e, ex=ex: e.dma_start(out=B1IE[:, :], in_=g["p4_b1"][:, 32 * ex:32 * (ex + 1)]), cl, 16)
                R.wait("tensor", roD)
                R.wait("tensor", uaE)
                for c in range(32):
                    par = c % 2
                    R.wait_rel("gpsimd", mmS1, 1)
                    R.inc("gpsimd", lambda e, c=c, ex=ex, par=par: e.dma_start(
                        out=W1B[:, 1024 * par:1024 * (par + 1)],
                        in_=g["p4_w1"][:, 1024 * (32 * ex + c):1024 * (32 * ex + c + 1)]), w1l, 16)
                    R.wait("tensor", w1l)
                    bcol = 2048 if par == 0 else 3584
                    R.wait_rel("tensor", b1d, 1)
                    for d in range(8):
                        R.op("tensor", lambda e, d=d, ex=ex, par=par, bcol=bcol: e.matmul(
                            ps_all[:, bcol:bcol + 1], W1B[:, 1024 * par + 128 * d:1024 * par + 128 * (d + 1)],
                            BETB[:, 8 * ex + d:8 * ex + d + 1],
                            start=(d == 0), stop=(d == 7)))
                    R.wait_rel("tensor", relE, 1)
                    for pc in range(2):
                        for d in range(8):
                            fn = lambda e, c=c, pc=pc, d=d, par=par: e.matmul(
                                ps_all[:, 512 * (2 * par + pc):512 * (2 * par + pc + 1)],
                                W1B[:, 1024 * par + 128 * d:1024 * par + 128 * (d + 1)],
                                XS[:, 1024 * d + 512 * pc:1024 * d + 512 * (pc + 1)],
                                start=(d == 0), stop=(d == 7))
                            if pc == 1 and d == 7:
                                R.inc("tensor", fn, mmS1)
                            else:
                                R.op("tensor", fn)
                    R.wait("vector", mmS1)
                    if c == 0:
                        R.wait("vector", cl)
                    R.inc("vector", lambda e, c=c, bcol=bcol: e.tensor_tensor(
                        B1C[:, c:c + 1], ps_all[:, bcol:bcol + 1], B1IE[:, c:c + 1], op=ALU.add), b1d)
                    R.wait("scalar", b1d)
                    R.wait("scalar", mmS1)
                    for pc in range(2):
                        fn = lambda e, c=c, pc=pc, par=par: e.activation(
                            out=R_[:, 512 * (2 * c + pc):512 * (2 * c + pc + 1)],
                            in_=ps_all[:, 512 * (2 * par + pc):512 * (2 * par + pc + 1)],
                            func=AF.Relu, bias=B1C[:, c:c + 1])
                        if pc == 1:
                            R.inc("scalar", fn, relE)
                        else:
                            R.op("scalar", fn)
                R.wait("gpsimd", mmS1)
                for half in range(2):
                    if half == 1:
                        R.wait("gpsimd", mmS2)
                    R.inc("gpsimd", lambda e, ex=ex, half=half: e.dma_start(
                        out=B2R[:, :], in_=g["p4_b2"][0:1, 1024 * ex + 512 * half:1024 * ex + 512 * (half + 1)]), b2l, 16)
                    R.inc("gpsimd", lambda e, ex=ex, half=half: e.dma_start(
                        out=XS[:, :].rearrange("p (c s) -> p c s", c=32),
                        in_=g["p4_w2"][:, :].rearrange("p (E c h s) -> p E c h s", E=8, c=32, h=2)[:, ex, :, half, :]),
                        w2l, 16)
                    R.wait("tensor", w2l)
                    R.wait("tensor", relE)
                    R.wait("tensor", b2l)
                    if half == 0:
                        R.wait("tensor", b1d)
                    R.wait("tensor", uaE)
                    for c in range(32):
                        for tt in range(8):
                            R.op("tensor", lambda e, c=c, tt=tt: e.matmul(
                                ps_all[:, 512 * tt:512 * (tt + 1)],
                                R_[:, 512 * (2 * c + tt // 4) + 128 * (tt % 4):512 * (2 * c + tt // 4) + 128 * (tt % 4) + 128],
                                XS[:, 512 * c:512 * (c + 1)],
                                start=(c == 0), stop=False))
                    for tt in range(8):
                        fn = lambda e, tt=tt: e.matmul(
                            ps_all[:, 512 * tt:512 * (tt + 1)], ONESB[:, :], B2R[0:1, 0:512],
                            start=False, stop=True)
                        if tt == 7:
                            R.inc("tensor", fn, mmS2)
                        else:
                            R.op("tensor", fn)
                    R.wait("vector", mmS2)
                    for tt in range(8):
                        T = tc * 8 + tt
                        fn = lambda e, tt=tt, T=T, ex=ex, half=half: e.scalar_tensor_tensor(
                            u_acc[:, 1024 * T + 512 * half:1024 * T + 512 * (half + 1)],
                            ps_all[:, 512 * tt:512 * (tt + 1)],
                            MA[:, 8 * T + ex:8 * T + ex + 1],
                            u_acc[:, 1024 * T + 512 * half:1024 * T + 512 * (half + 1)],
                            op0=ALU.mult, op1=ALU.add)
                        if tt == 7:
                            R.inc("vector", fn, uaE)
                        else:
                            R.op("vector", fn)

        R.wait("vector", uaE)
        R.wait("sync", psm["s_app"])
        R.wait("sync", mmS2)
        R.wait("sync", w2l)
        for t in range(NT):
            if t == 0:
                epi_prefetch_x(R, ep, esm, x_src, seq, 0)
            emit_epilogue_tile(
                R, ep, esm, seq, t,
                (lambda t=t: u_acc[:, 1024 * t:1024 * (t + 1)]),
                x_dst, direct_sbuf=True)
            if t < NT - 1:
                epi_prefetch_x(R, ep, esm, x_src, seq, t + 1)
    R.wait("gpsimd", esm["st"])


PHASE_EMITTERS = {1: emit_phase1, 2: emit_phase2, 3: emit_phase3, 4: emit_phase4}


def _run(phase_list, imgs, x_full):
    nc = bass.Bass("TRN2", target_bir_lowering=False)
    g = {}
    for name, arr in imgs.items():
        dt = IMG_DTYPES.get(name, F32)
        g[name] = nc.dram_tensor(name, list(arr.shape), dt, kind="ExternalInput").ap()
    x_in = nc.dram_tensor("x_in", [NSEQ * S, D], F32, kind="ExternalInput").ap()
    x_out = nc.dram_tensor("x_out", [NSEQ * S, D], F32, kind="ExternalOutput").ap()
    n_ph = len(phase_list)
    scratch = [nc.dram_tensor(f"x_mid{i}", [NSEQ * S, D], F32, kind="Internal").ap()
               for i in range(max(0, n_ph - 1))]

    with ExitStack() as stack:
        R = Rec(nc, stack)
        for i, ph in enumerate(phase_list):
            src = x_in if i == 0 else scratch[i - 1]
            dst = x_out if i == n_ph - 1 else scratch[i]
            R.phase_begin()
            with ExitStack() as phase_stack:
                R.stack = phase_stack
                PHASE_EMITTERS[ph](R, g, src, dst)
                R.stack = stack
            if i < n_ph - 1:
                R.barrier(f"pb{i}")
        R.replay()
    nc.finalize()

    in_maps = []
    for core in range(NCORES):
        m = dict(imgs)
        m["x_in"] = np.ascontiguousarray(x_full[core * NSEQ * S:(core + 1) * NSEQ * S, :])
        in_maps.append(m)
    res = run_bass_kernel_spmd(nc, in_maps, core_ids=list(range(NCORES)))
    return np.concatenate([res.results[i]["x_out"] for i in range(NCORES)], axis=0)


def kernel(**inputs):
    inputs = {k: np.asarray(v) for k, v in inputs.items()}
    imgs = build_images(inputs)
    x = np.ascontiguousarray(inputs["L0_attn_x"].astype(np.float32).reshape(B * S, D))
    phase_groups = os.environ.get("MK_PHASES", "1,2,3,4").split("/")
    cur = x
    for grp in phase_groups:
        phl = [int(p) for p in grp.split(",")]
        cur = _run(phl, imgs, cur)
    return cur.reshape(B, S, D)


# version 10 (final)
# speedup vs baseline: 1.3469x; 1.1364x over previous
import os
import sys
import numpy as np
from contextlib import ExitStack

sys.path.insert(0, "/opt/trn_rl_repo")
import concourse.bass as bass
import concourse.mybir as mybir
from concourse.bass_utils import run_bass_kernel_spmd

F32 = mybir.dt.float32
F32R = mybir.dt.float32r
BF16 = mybir.dt.bfloat16
AF = mybir.ActivationFunctionType
ALU = mybir.AluOpType
AX = mybir.AxisListType

NCORES = 8
B, S, D = 32, 2048, 1024
NSEQ = B // NCORES
NT = S // 128
H, HD = 16, 64
DH = 4096
EPS = 1e-5

ENGINES = ("sync", "tensor", "vector", "scalar", "gpsimd")



def lhsT_image(W):
    O, I = W.shape
    Oc, Ic = O // 128, I // 128
    return np.ascontiguousarray(W.reshape(Oc, 128, Ic, 128).transpose(3, 0, 2, 1).reshape(128, Oc * Ic * 128))


def rhs_image(WT):
    K, O = WT.shape
    Kc = K // 128
    return np.ascontiguousarray(WT.reshape(Kc, 128, O).transpose(1, 0, 2).reshape(128, Kc * O))


def cols_image(v):
    n = v.shape[0] // 128
    return np.ascontiguousarray(v.reshape(n, 128).T)


def bcast_image(v):
    return np.ascontiguousarray(np.tile(np.asarray(v)[None, :], (128, 1)))


def build_images(inp):
    import ml_dtypes
    img = {}
    img["p2_w1"] = lhsT_image(inp["L0_ffn_w1_w"])
    img["p2_w2"] = lhsT_image(inp["L0_ffn_w2_w"])
    img["p2_h1"] = lhsT_image(inp["L0_ffn_h1_w"])
    img["p2_h2"] = lhsT_image(inp["L0_ffn_h2_w"])
    img["p2_w3"] = rhs_image(np.ascontiguousarray(inp["L0_ffn_w3_w"].T))
    img["p2_s12"] = np.concatenate([cols_image(inp["L0_ffn_w1_s"]),
                                    cols_image(inp["L0_ffn_w2_s"]),
                                    cols_image(inp["L0_ffn_h1_s"]),
                                    cols_image(inp["L0_ffn_h2_s"])], axis=1)
    img["p2_w3sb"] = bcast_image(inp["L0_ffn_w3_s"])

    perm_q = []
    for gi in range(4):
        for half in range(2):
            for hh in range(4):
                h = 4 * gi + hh
                for i in range(32):
                    perm_q.append(h * 64 + 2 * i + half)
    perm_q = np.array(perm_q)
    Wq = inp["L0_attn_W_qkv"][0:1024][perm_q]
    Wk = inp["L0_attn_W_qkv"][1024:2048][perm_q]
    Wv = inp["L0_attn_W_qkv"][2048:3072]
    img["p1_qkv"] = lhsT_image(np.concatenate([Wq, Wk, Wv], axis=0))
    img["p1_wo"] = rhs_image(np.ascontiguousarray(inp["L0_attn_W_o"].T))
    t = np.arange(S, dtype=np.float64)
    theta = 10000.0 ** (-2.0 * (np.arange(32, dtype=np.float64) / 64.0))
    ang = t[None, :] * theta[:, None]
    img["p1_cc"] = np.ascontiguousarray(np.tile(np.cos(ang).astype(np.float32), (4, 1)))
    img["p1_ss"] = np.ascontiguousarray(np.tile(np.sin(ang).astype(np.float32), (4, 1)))
    m = np.zeros((128, 4 * 512), dtype=np.float32)
    for v in range(4):
        blk = np.zeros((128, 512), dtype=np.float32)
        blk[:, v * 128:(v + 1) * 128] = np.triu(np.full((128, 128), -1e30, dtype=np.float32), 1)
        blk[:, (v + 1) * 128:] = -1e30
        m[:, v * 512:(v + 1) * 512] = blk
    img["p1_mask"] = m
    img["ident_bf"] = np.eye(128, dtype=ml_dtypes.bfloat16)
    img["ident_f32"] = np.eye(128, dtype=np.float32)

    w3q = np.concatenate(
        [np.ascontiguousarray(inp["L1_attn_Wq_w"].T),
         np.ascontiguousarray(inp["L1_attn_Wk_w"].T),
         np.ascontiguousarray(inp["L1_attn_Wv_w"].T)], axis=1)
    img["p3_wqkv"] = np.ascontiguousarray(np.tile(w3q, (2, 1)))
    b3q = np.stack([inp["L1_attn_Wq_b"], inp["L1_attn_Wk_b"], inp["L1_attn_Wv_b"]], axis=1)
    img["p3_bqkv"] = np.ascontiguousarray(np.tile(b3q, (2, 1)))
    img["p3_proj"] = rhs_image(np.ascontiguousarray(inp["L1_attn_proj_w"].T))
    img["p3_projb"] = bcast_image(inp["L1_attn_proj_b"])

    w1 = inp["L1_ffn_w1"]
    img["p4_w1"] = np.ascontiguousarray(
        w1.reshape(8, 8, 128, 32, 128).transpose(2, 0, 3, 1, 4).reshape(128, 8 * 32 * 8 * 128))
    w2 = inp["L1_ffn_w2"]
    img["p4_w2"] = np.ascontiguousarray(
        w2.reshape(8, 32, 128, 1024).transpose(2, 0, 1, 3).reshape(128, 8 * 32 * 1024))
    img["p4_b1"] = np.concatenate([cols_image(inp["L1_ffn_b1"][e]) for e in range(8)], axis=1)
    img["p4_beta"] = np.concatenate([cols_image(inp["L1_ffn_beta"][e]) for e in range(8)], axis=1)
    img["p4_gamma"] = np.concatenate([cols_image(inp["L1_ffn_gamma"][e]) for e in range(8)], axis=1)
    img["p4_b2"] = np.ascontiguousarray(inp["L1_ffn_b2"].reshape(1, 8 * 1024))
    cent = inp["L1_ffn_centroids"]
    img["p4_cent"] = np.ascontiguousarray(cent.T.reshape(8, 128, 8).transpose(1, 0, 2).reshape(128, 64))
    img["ones_row"] = np.ones((1, 128), dtype=np.float32)
    return img


IMG_DTYPES = {"ident_bf": BF16}



class Sem:
    __slots__ = ("name", "n", "h")

    def __init__(self, name):
        self.name = name
        self.n = 0
        self.h = None


class Rec:
    def __init__(self, nc, stack):
        self.nc = nc
        self.root_stack = stack
        self.stack = stack
        self.ops = {e: [] for e in ENGINES}
        self.nsems = 0
        self.epsc = None
        self._sem_pool = []
        self._sem_idx = 0

    def init_epsc(self):
        if self.epsc is None:
            old = self.stack
            self.stack = self.root_stack
            self.epsc = self.sb("epsc", [128, 1], F32)
            self.stack = old
            self.op("vector", lambda e: e.memset(self.epsc[:, :], float(EPS)))
            self.op("vector", lambda e: e.drain())

    def sem(self, name):
        if self._sem_idx < len(self._sem_pool):
            s = self._sem_pool[self._sem_idx]
        else:
            s = Sem(name)
            s.h = self.root_stack.enter_context(self.nc.semaphore(f"s{len(self._sem_pool)}"))
            self._sem_pool.append(s)
            self.nsems += 1
        self._sem_idx += 1
        return s

    def phase_begin(self):
        self._sem_idx = 0

    def sb(self, name, shape, dt=F32):
        return self.stack.enter_context(self.nc.sbuf_tensor(name, shape, dt))

    def ps(self, name, shape, dt=F32):
        return self.stack.enter_context(self.nc.psum_tensor(name, shape, dt))

    def op(self, eng, fn):
        self.ops[eng].append(fn)

    def inc(self, eng, fn, sem, k=1):
        sem.n += k
        self.ops[eng].append(lambda e, fn=fn, h=sem.h, k=k: fn(e).then_inc(h, k))

    def wait(self, eng, sem, val=None):
        v = sem.n if val is None else val
        if v <= 0:
            return
        self.ops[eng].append(lambda e, h=sem.h, v=v: e.wait_ge(h, v))

    def wait_rel(self, eng, sem, behind):
        self.wait(eng, sem, sem.n - behind)

    def attach_inc(self, eng, sem, k=1):
        sem.n += k
        old = self.ops[eng][-1]
        self.ops[eng][-1] = lambda e, old=old, h=sem.h, k=k: old(e).then_inc(h, k)

    def barrier(self, name):
        s = self.sem(name)
        for e in ENGINES:
            s.n += 1
            self.ops[e].append(lambda x, h=s.h: x.sem_inc(h, 1))
        tgt = s.n
        for e in ENGINES:
            self.wait(e, s, tgt)

    def replay(self):
        block = self.root_stack.enter_context(self.nc.Block())
        for ename in ENGINES:
            oplist = self.ops[ename]

            def body(eng, oplist=oplist):
                for fn in oplist:
                    fn(eng)
            getattr(block, ename)(body)



def ln_rstd_chain(R, v, t):
    R.op("vector", lambda e: e.drain())
    R.inc("vector", lambda e, t=t: e.bn_aggr(v["MV"][:, 2 * t:2 * t + 2], v["BN6"][:, 12 * t:12 * t + 12]),
          v["s_d2a"])
    R.wait("scalar", v["s_d2a"])
    R.op("scalar", lambda e, t=t: e.activation(out=v["SD"][:, t:t + 1], in_=v["MV"][:, 2 * t + 1:2 * t + 2],
                                               func=AF.Ln, bias=R.epsc[:, 0:1]))
    R.op("scalar", lambda e: e.drain())
    R.inc("scalar", lambda e, t=t: e.activation(out=v["RSTD"][:, t:t + 1], in_=v["SD"][:, t:t + 1],
                                                func=AF.Exp, scale=-0.5), v["s_a2d"])
    R.wait("vector", v["s_a2d"])


def mk_preln_sems(R, nm):
    return dict(xld=R.sem(nm + "xld"), s_d2a=R.sem(nm + "d2a"), s_a2d=R.sem(nm + "a2d"),
                s_app=R.sem(nm + "app"), s_tp=R.sem(nm + "tp"), s_ev=R.sem(nm + "ev"))


def mk_preln_tensors(R, nm, need_ufm=True, XT=None):
    d = dict(XT=XT if XT is not None else R.sb(nm + "XT", [128, 2 * 1024], F32),
             BN6=R.sb(nm + "BN6", [128, 12 * NT], F32), MV=R.sb(nm + "MV", [128, 2 * NT], F32),
             SD=R.sb(nm + "SD", [128, NT], F32), RSTD=R.sb(nm + "RSTD", [128, NT], F32),
             NMR=R.sb(nm + "NMR", [128, NT], F32))
    if need_ufm:
        d["UB"] = R.sb(nm + "UB", [128, 2 * 1024], BF16)
    return d


def emit_preln(R, pt, sm, x_src, seq, U_fm=None, u_f32=None):
    v = dict(BN6=pt["BN6"], MV=pt["MV"], SD=pt["SD"], RSTD=pt["RSTD"], NMR=pt["NMR"],
             s_d2a=sm["s_d2a"], s_a2d=sm["s_a2d"])
    for t in range(NT):
        sl = 1024 * (t % 2)
        R.wait_rel("sync", sm["s_app"], 1)
        R.inc("sync", lambda e, t=t, sl=sl: e.dma_start(out=pt["XT"][:, sl:sl + 1024],
                                                        in_=x_src[(seq * S + 128 * t):(seq * S + 128 * (t + 1)), :]),
              sm["xld"], 16)
        R.wait("vector", sm["xld"])
        R.op("vector", lambda e, t=t, sl=sl: e.bn_stats(pt["BN6"][:, 12 * t:12 * t + 6], pt["XT"][:, sl:sl + 512]))
        R.op("vector", lambda e, t=t, sl=sl: e.bn_stats(pt["BN6"][:, 12 * t + 6:12 * t + 12], pt["XT"][:, sl + 512:sl + 1024]))
        ln_rstd_chain(R, v, t)
        if u_f32 is not None:
            R.inc("vector", lambda e, t=t, sl=sl: e.tensor_scalar(
                u_f32[:, 1024 * t:1024 * (t + 1)], pt["XT"][:, sl:sl + 1024],
                pt["MV"][:, 2 * t:2 * t + 1], pt["RSTD"][:, t:t + 1], op0=ALU.subtract, op1=ALU.mult), sm["s_app"])
        if U_fm is not None:
            R.wait_rel("vector", sm["s_tp"], 1)
            R.inc("vector", lambda e, t=t, sl=sl: e.tensor_scalar(
                pt["UB"][:, sl:sl + 1024], pt["XT"][:, sl:sl + 1024],
                pt["MV"][:, 2 * t:2 * t + 1], pt["RSTD"][:, t:t + 1], op0=ALU.subtract, op1=ALU.mult), sm["s_app"])
            R.wait("tensor", sm["s_app"])
            R.wait_rel("tensor", sm["s_ev"], 1)
            for d in range(8):
                fn = lambda e, t=t, d=d, sl=sl: e.transpose(
                    pt["ps_bf"][:, sl + 128 * d:sl + 128 * (d + 1)],
                    pt["UB"][:, sl + 128 * d:sl + 128 * (d + 1)], pt["ident"][:, :])
                if d == 7:
                    R.inc("tensor", fn, sm["s_tp"])
                else:
                    R.op("tensor", fn)
            R.wait("vector", sm["s_tp"])
            R.inc("vector", lambda e, t=t, sl=sl: e.tensor_copy(
                U_fm[:, :].rearrange("p (d s) -> p d s", d=8)[:, :, 128 * t:128 * (t + 1)],
                pt["ps_bf"][:, sl:sl + 1024].rearrange("p (d s) -> p d s", d=8)),
                sm["s_ev"])


def mk_epi_sems(R, nm):
    return dict(xld=R.sem(nm + "xld"), st=R.sem(nm + "st"), s_apply=R.sem(nm + "ap"),
                s_d2a=R.sem(nm + "Ed2a"), s_a2d=R.sem(nm + "Ea2d"))


def mk_epi_tensors(R, nm, DSB, XT2, stats_from=None, TMPD=None):
    if stats_from is not None:
        return dict(DSB=DSB, XT2=XT2, TMPD=TMPD, BN6=stats_from["BN6"], MV=stats_from["MV"],
                    SD=stats_from["SD"], RSTD=stats_from["RSTD"], NMR=stats_from["NMR"])
    return dict(
        DSB=DSB,
        XT2=XT2,
        TMPD=TMPD,
        BN6=R.sb(nm + "BN6", [128, 12 * NT], F32),
        MV=R.sb(nm + "MV", [128, 2 * NT], F32),
        SD=R.sb(nm + "SD", [128, NT], F32),
        RSTD=R.sb(nm + "RSTD", [128, NT], F32),
        NMR=R.sb(nm + "NMR", [128, NT], F32),
    )


def epi_prefetch_x(R, ep, sm, x_src, seq, t):
    R.wait("sync", sm["st"])
    R.inc("sync", lambda e, t=t: e.dma_start(
        out=ep["XT2"][:, :],
        in_=x_src[(seq * S + 128 * t):(seq * S + 128 * (t + 1)), :]), sm["xld"], 16)


def emit_epilogue_tile(R, ep, sm, seq, t, delta_src_fn, x_dst, bias_bcast=None, scale_bcast=None,
                       direct_sbuf=False):
    v = dict(BN6=ep["BN6"], MV=ep["MV"], SD=ep["SD"], RSTD=ep["RSTD"], NMR=ep["NMR"],
             s_d2a=sm["s_d2a"], s_a2d=sm["s_a2d"])
    if direct_sbuf:
        src = delta_src_fn
    else:
        if scale_bcast is not None:
            R.op("vector", lambda e: e.tensor_tensor(ep["DSB"][:, :], delta_src_fn(), scale_bcast[:, :], op=ALU.mult))
        elif bias_bcast is not None:
            R.op("vector", lambda e: e.tensor_tensor(ep["DSB"][:, :], delta_src_fn(), bias_bcast[:, :], op=ALU.add))
        else:
            R.op("vector", lambda e: e.tensor_copy(ep["DSB"][:, :], delta_src_fn()))
        R.op("vector", lambda e: e.drain())
        src = lambda: ep["DSB"][:, :]
    R.op("vector", lambda e, t=t: e.bn_stats(ep["BN6"][:, 12 * t:12 * t + 6], src()[:, 0:512]))
    R.op("vector", lambda e, t=t: e.bn_stats(ep["BN6"][:, 12 * t + 6:12 * t + 12], src()[:, 512:1024]))
    ln_rstd_chain(R, v, t)
    R.wait("vector", sm["xld"])
    R.op("vector", lambda e, t=t: e.tensor_scalar(
        ep["TMPD"][:, :], src(), ep["MV"][:, 2 * t:2 * t + 1], ep["RSTD"][:, t:t + 1],
        op0=ALU.subtract, op1=ALU.mult))
    R.op("vector", lambda e: e.drain())
    R.inc("vector", lambda e, t=t: e.tensor_tensor(
        ep["XT2"][:, :], ep["TMPD"][:, :], ep["XT2"][:, :], op=ALU.add), sm["s_apply"])
    R.wait("gpsimd", sm["s_apply"])
    R.inc("gpsimd", lambda e, t=t: e.dma_start(
        out=x_dst[(seq * S + 128 * t):(seq * S + 128 * (t + 1)), :],
        in_=ep["XT2"][:, :]), sm["st"], 16)



def emit_phase2(R, g, x_src, x_dst):
    R.init_epsc()
    cl = R.sem("p2cl")
    ident = R.sb("p2ident", [128, 128], BF16)
    S12 = R.sb("p2S12", [128, 128], F32)
    W3SB = R.sb("p2w3sb", [128, 1024], F32)
    R.inc("sync", lambda e: e.dma_start(out=ident[:], in_=g["ident_bf"][:, :]), cl, 16)
    R.inc("sync", lambda e: e.dma_start(out=S12[:], in_=g["p2_s12"][:, :]), cl, 16)
    R.inc("sync", lambda e: e.dma_start(out=W3SB[:], in_=g["p2_w3sb"][:, :]), cl, 16)
    for e in ENGINES:
        R.wait(e, cl)

    ps_all = R.ps("p2ps", [128, 4096], F32)
    ps_bf = ps_all[:, 0:1024].bitcast(BF16)
    U_fm = R.sb("p2U", [128, 8 * 2048], BF16)
    A_ = R.sb("p2A", [128, 32 * 1024], BF16)
    B_ = R.sb("p2B", [128, 32 * 1024], BF16)
    W1B = R.sb("p2W1B", [128, 2 * 1024], BF16)
    W2B = R.sb("p2W2B", [128, 2 * 1024], BF16)
    HB = R.sb("p2HB", [128, 2 * 4096], BF16)
    W3B = W1B
    SCR = R.sb("p2SCR", [128, 1024], F32)
    SILU = SCR
    pt = mk_preln_tensors(R, "p2pl")
    pt.update(ps_bf=ps_bf, ident=ident)
    psm = mk_preln_sems(R, "p2pl")
    ep = mk_epi_tensors(R, "p2ep", SCR, pt["XT"][:, 0:1024], TMPD=pt["UB"][:, :].bitcast(F32))
    esm = mk_epi_sems(R, "p2ep")
    wl = R.sem("p2wl")
    mmA = R.sem("p2mmA")
    acA = R.sem("p2acA")
    dvA = R.sem("p2dvA")
    hl = R.sem("p2hl")
    mmH = R.sem("p2mmH")
    acH = R.sem("p2acH")
    w3l = R.sem("p2w3l")
    mmW3c = R.sem("p2mmW3c")

    for seq in range(NSEQ):
        if seq > 0:
            R.wait("sync", esm["st"])
            R.wait("vector", mmA)
            R.wait("tensor", esm["s_apply"])
            R.wait("tensor", esm["st"])
        emit_preln(R, pt, psm, x_src, seq, U_fm=U_fm)
        R.wait("sync", psm["s_app"])
        epi_prefetch_x(R, ep, esm, x_src, seq, 0)

        for tc in range(2):
            R.wait("tensor", psm["s_ev"])
            if tc == 1:
                R.wait("vector", mmW3c)
                R.wait("tensor", esm["s_apply"])
                R.wait("tensor", esm["st"])
            R.wait("gpsimd", mmW3c)
            for c in range(32):
                par = c % 2
                R.wait_rel("gpsimd", mmA, 1)
                R.inc("gpsimd", lambda e, c=c, par=par: e.dma_start(
                    out=W1B[:, 1024 * par:1024 * (par + 1)], in_=g["p2_w1"][:, 1024 * c:1024 * (c + 1)]), wl, 16)
                R.inc("gpsimd", lambda e, c=c, par=par: e.dma_start(
                    out=W2B[:, 1024 * par:1024 * (par + 1)], in_=g["p2_w2"][:, 1024 * c:1024 * (c + 1)]), wl, 16)
                R.wait("tensor", wl)
                R.wait_rel("tensor", dvA, 1)
                for pre in range(2):
                    WB = W1B if pre == 0 else W2B
                    for p in range(2):
                        for d in range(8):
                            fn = lambda e, p=p, d=d, par=par, tc=tc, WB=WB, pre=pre: e.matmul(
                                ps_all[:, 512 * (4 * par + 2 * pre + p):512 * (4 * par + 2 * pre + p + 1)],
                                WB[:, 1024 * par + 128 * d:1024 * par + 128 * (d + 1)],
                                U_fm[:, 2048 * d + 1024 * tc + 512 * p:2048 * d + 1024 * tc + 512 * (p + 1)],
                                start=(d == 0), stop=(d == 7))
                            if pre == 1 and p == 1 and d == 7:
                                R.inc("tensor", fn, mmA)
                            else:
                                R.op("tensor", fn)
                R.wait("scalar", mmA)
                R.wait("scalar", dvA)
                for p in range(2):
                    fn = lambda e, c=c, p=p, par=par: e.activation(
                        out=SILU[:, 512 * p:512 * (p + 1)],
                        in_=ps_all[:, 512 * (4 * par + p):512 * (4 * par + p + 1)],
                        func=AF.Silu, scale=S12[:, c:c + 1])
                    R.inc("scalar", fn, acA) if p == 1 else R.op("scalar", fn)
                R.wait("vector", acA)
                for p in range(2):
                    fn = lambda e, c=c, p=p, par=par: e.scalar_tensor_tensor(
                        A_[:, 1024 * c + 512 * p:1024 * c + 512 * (p + 1)],
                        ps_all[:, 512 * (4 * par + 2 + p):512 * (4 * par + 2 + p + 1)],
                        S12[:, 32 + c:32 + c + 1], SILU[:, 512 * p:512 * (p + 1)],
                        op0=ALU.mult, op1=ALU.mult)
                    R.inc("vector", fn, dvA) if p == 1 else R.op("vector", fn)

            def h_stage(img, srcT, dstT, scol_off, first):
                R.wait("tensor", dvA)
                for c in range(32):
                    par = c % 2
                    R.wait_rel("gpsimd", mmH, 1)
                    R.inc("gpsimd", lambda e, c=c, par=par, img=img: e.dma_start(
                        out=HB[:, 4096 * par:4096 * (par + 1)], in_=img[:, 4096 * c:4096 * (c + 1)]), hl, 16)
                    R.wait("tensor", hl)
                    R.wait_rel("tensor", acH, 1)
                    if first and c == 0:
                        R.wait("tensor", acA)
                        R.wait("tensor", dvA)
                    for p in range(2):
                        for k in range(32):
                            fn = lambda e, c=c, p=p, k=k, par=par, srcT=srcT: e.matmul(
                                ps_all[:, 512 * (2 * par + p):512 * (2 * par + p + 1)],
                                HB[:, 4096 * par + 128 * k:4096 * par + 128 * (k + 1)],
                                srcT[:, 1024 * k + 512 * p:1024 * k + 512 * (p + 1)],
                                start=(k == 0), stop=(k == 31))
                            if p == 1 and k == 31:
                                R.inc("tensor", fn, mmH)
                            else:
                                R.op("tensor", fn)
                    R.wait("scalar", mmH)
                    for p in range(2):
                        fn = lambda e, c=c, p=p, par=par, dstT=dstT, so=scol_off: e.activation(
                            out=dstT[:, 1024 * c + 512 * p:1024 * c + 512 * (p + 1)],
                            in_=ps_all[:, 512 * (2 * par + p):512 * (2 * par + p + 1)],
                            func=AF.Gelu, scale=S12[:, so + c:so + c + 1])
                        R.inc("scalar", fn, acH) if p == 1 else R.op("scalar", fn)

            h_stage(g["p2_h1"], A_, B_, 64, True)
            h_stage(g["p2_h2"], B_, A_, 96, False)

            for tg in range(2):
                R.wait("gpsimd", mmA)
                for c in range(32):
                    par = c % 2
                    R.wait_rel("gpsimd", mmW3c, 1)
                    R.inc("gpsimd", lambda e, c=c, par=par: e.dma_start(
                        out=W3B[:, 1024 * par:1024 * (par + 1)], in_=g["p2_w3"][:, 1024 * c:1024 * (c + 1)]), w3l, 16)
                    R.wait("tensor", w3l)
                    if c == 0:
                        R.wait("tensor", acH)
                        R.wait("tensor", esm["s_apply"])
                    for tt in range(4):
                        for j in range(2):
                            fn = lambda e, c=c, tt=tt, j=j, tc=tc, tg=tg, par=par: e.matmul(
                                ps_all[:, 512 * (2 * tt + j):512 * (2 * tt + j + 1)],
                                A_[:, 1024 * c + 128 * (tg * 4 + tt):1024 * c + 128 * (tg * 4 + tt) + 128],
                                W3B[:, 1024 * par + 512 * j:1024 * par + 512 * (j + 1)],
                                start=(c == 0), stop=(c == 31))
                            if tt == 3 and j == 1:
                                R.inc("tensor", fn, mmW3c)
                            else:
                                R.op("tensor", fn)
                R.wait("vector", mmW3c)
                for tt in range(4):
                    t_glob = tc * 8 + tg * 4 + tt
                    emit_epilogue_tile(
                        R, ep, esm, seq, t_glob,
                        (lambda tt=tt: ps_all[:, 1024 * tt:1024 * (tt + 1)]),
                        x_dst, scale_bcast=W3SB)
                    if t_glob < NT - 1:
                        epi_prefetch_x(R, ep, esm, x_src, seq, t_glob + 1)
    R.wait("gpsimd", esm["st"])





P1_DEBUG = {"stage": ""}


def emit_phase1(R, g, x_src, x_dst):
    R.init_epsc()
    dbg = P1_DEBUG["stage"]
    cl = R.sem("p1cl")
    ident = R.sb("p1ident", [128, 128], BF16)
    MASKD = R.sb("p1MASKD", [128, 128], F32)
    R.inc("sync", lambda e: e.dma_start(out=ident[:], in_=g["ident_bf"][:, :]), cl, 16)
    R.inc("sync", lambda e: e.dma_start(out=MASKD[:], in_=g["p1_mask"][:, 0:128]), cl, 16)
    for e in ENGINES:
        R.wait(e, cl)

    ps_all = R.ps("p1ps", [128, 4096], F32)
    ps_bf = ps_all[:, 0:1024].bitcast(BF16)
    psTv = ps_all[:, 3072:4096].bitcast(BF16)
    U_fm = R.sb("p1U", [128, 8 * 2048], BF16)
    AO = U_fm
    RQ = R.sb("p1RQ", [128, 8 * 2048], BF16)
    RK = R.sb("p1RK", [128, 8 * 2048], BF16)
    VT = R.sb("p1VT", [128, 16 * 1024], BF16)
    SCO = R.sb("p1SCO", [128, 2 * 4 * 2048], BF16)
    CC = SCO[:, 0:4096].bitcast(F32)
    SS = SCO[:, 4096:8192].bitcast(F32)
    PBF = R.sb("p1PBF", [128, 4 * 2048], BF16)
    PTS = R.sb("p1PTS", [128, 2 * 1024], BF16)
    RTMP = R.sb("p1RTMP", [128, 2 * 512], F32)
    VSB = RTMP[:, 0:1024].bitcast(BF16)
    QWB = R.sb("p1QWB", [128, 2 * 2048], BF16)
    WOB = QWB[:, 0:2048]
    SMT = R.sb("p1SMT", [128, 16], F32)
    pt = mk_preln_tensors(R, "p1pl")
    pt.update(ps_bf=ps_bf, ident=ident)
    psm = mk_preln_sems(R, "p1pl")
    ep = mk_epi_tensors(R, "p1ep", None, pt["XT"][:, 0:1024], TMPD=PBF[:, 0:2048].bitcast(F32))
    esm = mk_epi_sems(R, "p1ep")
    qw = R.sem("p1qw")
    mmQ = R.sem("p1mmQ")
    rpD = R.sem("p1rpD")
    vE = R.sem("p1vE")
    mmS = R.sem("p1mmS")
    sevE = R.sem("p1sevE")
    sf_d2a = R.sem("p1sfd2a")
    sf_a2d = R.sem("p1sfa2d")
    pN = R.sem("p1pN")
    mmPT = R.sem("p1mmPT")
    ptE = R.sem("p1ptE")
    mmAV = R.sem("p1mmAV")
    aoE = R.sem("p1aoE")
    wol = R.sem("p1wol")
    mmPc = R.sem("p1mmPc")
    ccl = R.sem("p1ccl")

    for seq in range(1 if dbg else NSEQ):
        if seq > 0:
            R.wait("sync", esm["st"])
            R.wait("vector", mmQ)
            R.wait("vector", mmPc)
            R.wait("tensor", esm["s_apply"])
            R.wait("tensor", esm["st"])
        emit_preln(R, pt, psm, x_src, seq, U_fm=U_fm)

        R.wait("sync", sf_a2d)
        R.wait("sync", sevE)
        R.inc("sync", lambda e: e.dma_start(out=CC, in_=g["p1_cc"][:, :]), ccl, 16)
        R.inc("sync", lambda e: e.dma_start(out=SS, in_=g["p1_ss"][:, :]), ccl, 16)
        R.wait("vector", ccl)
        R.wait("tensor", psm["s_ev"])
        for pi in range(8):
            slot = pi % 2
            dst = RQ if pi < 4 else RK
            gcol = (pi % 4) * 2 * 2048
            R.wait_rel("gpsimd", mmQ, 4)
            R.inc("gpsimd", lambda e, pi=pi, slot=slot: e.dma_start(
                out=QWB[:, 2048 * slot:2048 * slot + 1024], in_=g["p1_qkv"][:, 2048 * pi:2048 * pi + 1024]), qw, 16)
            R.inc("gpsimd", lambda e, pi=pi, slot=slot: e.dma_start(
                out=QWB[:, 2048 * slot + 1024:2048 * slot + 2048], in_=g["p1_qkv"][:, 2048 * pi + 1024:2048 * pi + 2048]), qw, 16)
            for p in range(4):
                pp = 4 * pi + p
                bA = 2 * (pp % 2)
                if p == 0:
                    R.wait("tensor", qw)
                R.wait_rel("tensor", rpD, 1)
                for ab in range(2):
                    for d in range(8):
                        fn = lambda e, p=p, d=d, ab=ab, slot=slot, bA=bA: e.matmul(
                            ps_all[:, 512 * (bA + ab):512 * (bA + ab + 1)],
                            QWB[:, 2048 * slot + 1024 * ab + 128 * d:2048 * slot + 1024 * ab + 128 * (d + 1)],
                            U_fm[:, 2048 * d + 512 * p:2048 * d + 512 * (p + 1)],
                            start=(d == 0), stop=(d == 7))
                        if ab == 1 and d == 7:
                            R.inc("tensor", fn, mmQ)
                        else:
                            R.op("tensor", fn)
                R.wait("vector", mmQ)
                psA = lambda bA=bA: ps_all[:, 512 * bA:512 * (bA + 1)]
                psB = lambda bA=bA: ps_all[:, 512 * (bA + 1):512 * (bA + 2)]
                ccp = lambda p=p: CC[:, 512 * p:512 * (p + 1)]
                ssp = lambda p=p: SS[:, 512 * p:512 * (p + 1)]
                R.op("vector", lambda e, psA=psA, ccp=ccp: e.tensor_tensor(RTMP[:, 0:512], ccp(), psA(), op=ALU.mult))
                R.op("vector", lambda e, psB=psB, ssp=ssp: e.tensor_tensor(RTMP[:, 512:1024], ssp(), psB(), op=ALU.mult))
                R.op("vector", lambda e: e.drain())
                R.op("vector", lambda e, p=p, gcol=gcol, dst=dst: e.tensor_tensor(
                    dst[:, gcol + 512 * p:gcol + 512 * (p + 1)], RTMP[:, 0:512], RTMP[:, 512:1024], op=ALU.subtract))
                R.op("vector", lambda e: e.drain())
                R.op("vector", lambda e, psA=psA, ssp=ssp: e.tensor_tensor(RTMP[:, 0:512], ssp(), psA(), op=ALU.mult))
                R.op("vector", lambda e, psB=psB, ccp=ccp: e.tensor_tensor(RTMP[:, 512:1024], ccp(), psB(), op=ALU.mult))
                R.op("vector", lambda e: e.drain())
                R.inc("vector", lambda e, p=p, gcol=gcol, dst=dst: e.tensor_tensor(
                    dst[:, gcol + 2048 + 512 * p:gcol + 2048 + 512 * (p + 1)], RTMP[:, 0:512], RTMP[:, 512:1024], op=ALU.add),
                    rpD)

        for vc in range(8):
            slot = vc % 2
            R.wait_rel("gpsimd", mmQ, 4)
            R.inc("gpsimd", lambda e, vc=vc, slot=slot: e.dma_start(
                out=QWB[:, 2048 * slot:2048 * slot + 1024], in_=g["p1_qkv"][:, 1024 * (16 + vc):1024 * (17 + vc)]), qw, 16)
            for p in range(4):
                pp2 = 4 * vc + p
                bk = 4 + (pp2 % 2)
                if p == 0:
                    R.wait("tensor", qw)
                    if vc == 0:
                        R.wait("tensor", rpD)
                R.wait_rel("tensor", vE, 1)
                for d in range(8):
                    fn = lambda e, p=p, d=d, slot=slot, bk=bk: e.matmul(
                        ps_all[:, 512 * bk:512 * (bk + 1)],
                        QWB[:, 2048 * slot + 128 * d:2048 * slot + 128 * (d + 1)],
                        U_fm[:, 2048 * d + 512 * p:2048 * d + 512 * (p + 1)],
                        start=(d == 0), stop=(d == 7))
                    if d == 7:
                        R.inc("tensor", fn, mmQ)
                    else:
                        R.op("tensor", fn)
                R.wait("scalar", mmQ)
                if p == 0:
                    R.wait("scalar", mmPT)
                if p == 0 and vc == 0:
                    R.wait("scalar", rpD)
                R.inc("scalar", lambda e, p=p, bk=bk: e.activation(
                    out=VSB[:, 512 * p:512 * (p + 1)], in_=ps_all[:, 512 * bk:512 * (bk + 1)], func=AF.Identity),
                    vE)
            R.wait("tensor", vE)
            for j in range(2):
                TB = mmPT.n % 2
                R.wait_rel("tensor", ptE, 1)
                for i in range(8):
                    fn = lambda e, j=j, i=i, TB=TB: e.transpose(
                        psTv[:, 1024 * TB + 128 * i:1024 * TB + 128 * (i + 1)],
                        VSB[:, 128 * (8 * j + i):128 * (8 * j + i + 1)], ident[:, :])
                    if i == 7:
                        R.inc("tensor", fn, mmPT)
                    else:
                        R.op("tensor", fn)
                R.wait("vector", mmPT)
                R.inc("vector", lambda e, j=j, vc=vc, TB=TB: e.tensor_copy(
                    VT[:, :].rearrange("p (kt r) -> p kt r", kt=16)[:, 8 * j:8 * (j + 1), 128 * vc:128 * (vc + 1)],
                    psTv[:, 1024 * TB:1024 * (TB + 1)].rearrange("p (k r) -> p k r", k=8)), ptE)

        if dbg == "qkv":
            dsem = R.sem("p1dbg")
            R.wait("gpsimd", rpD)
            R.wait("gpsimd", ptE)
            for nm2, buf in [("d_rq", RQ), ("d_rk", RK), ("d_vt", VT), ("d_u", U_fm)]:
                R.inc("gpsimd", lambda e, nm2=nm2, buf=buf: e.dma_start(out=g[nm2][:, :], in_=buf[:, :]), dsem, 16)
            R.wait("gpsimd", dsem)
            return
        R.wait("tensor", rpD)
        R.wait("tensor", ptE)
        R.wait("tensor", vE)

        def p1_scores(ci, gi, t):
            sl = (ci % 2) * 8192
            kpn = t // 4 + 1
            vloc = t % 4
            for kp in range(kpn):
                width = min(512, 128 * (t + 1) - 512 * kp)
                R.wait("tensor", sevE)
                for h in range(4):
                    for ab in range(2):
                        fn = lambda e, h=h, ab=ab, gi=gi, t=t, kp=kp, width=width: e.matmul(
                            ps_all[:, 512 * h:512 * h + width],
                            RQ[32 * h:32 * (h + 1), (2 * gi + ab) * 2048 + 128 * t:(2 * gi + ab) * 2048 + 128 * (t + 1)],
                            RK[32 * h:32 * (h + 1), (2 * gi + ab) * 2048 + 512 * kp:(2 * gi + ab) * 2048 + 512 * kp + width],
                            start=(ab == 0), stop=(ab == 1), tile_position=(32 * h, 0))
                        if h == 3 and ab == 1:
                            R.inc("tensor", fn, mmS)
                        else:
                            R.op("tensor", fn)
                diag = (kp == kpn - 1)
                R.wait("vector", mmS)
                if kp == 0:
                    R.wait_rel("vector", sf_a2d, 1)
                for h in range(4):
                    dst0 = sl + 2048 * h + 512 * kp
                    if not diag:
                        fn = lambda e, h=h, dst0=dst0: e.tensor_copy(
                            SCO[:, dst0:dst0 + 512], ps_all[:, 512 * h:512 * (h + 1)])
                        R.inc("vector", fn, sevE) if h == 3 else R.op("vector", fn)
                    else:
                        if vloc > 0:
                            R.op("vector", lambda e, h=h, dst0=dst0, vloc=vloc: e.tensor_copy(
                                SCO[:, dst0:dst0 + 128 * vloc], ps_all[:, 512 * h:512 * h + 128 * vloc]))
                        fn = lambda e, h=h, dst0=dst0, vloc=vloc, width=width: e.tensor_tensor(
                            SCO[:, dst0 + 128 * vloc:dst0 + width],
                            ps_all[:, 512 * h + 128 * vloc:512 * h + width],
                            MASKD[:, 0:width - 128 * vloc], op=ALU.add)
                        R.inc("vector", fn, sevE) if h == 3 else R.op("vector", fn)

        def p1_softmax_a(ci, t):
            sl = (ci % 2) * 8192
            L = 128 * (t + 1)
            R.wait("vector", sevE)
            for h in range(4):
                R.op("vector", lambda e, h=h, sl=sl, L=L: e.reduce_max(
                    SMT[:, h:h + 1], SCO[:, sl + 2048 * h:sl + 2048 * h + L], axis=AX.X, negate=True))
            R.op("vector", lambda e: e.drain())
            R.inc("vector", lambda e: e.tensor_scalar(
                SMT[:, 4:8], SMT[:, 0:4], 0.125, None, op0=ALU.mult), sf_d2a)
            R.wait("scalar", sf_d2a)
            R.wait("scalar", mmPT)
            for h in range(4):
                fn = lambda e, h=h, sl=sl, L=L: e.activation(
                    out=PBF[:, 2048 * h:2048 * h + L], in_=SCO[:, sl + 2048 * h:sl + 2048 * h + L],
                    func=AF.Exp, bias=SMT[:, 4 + h:5 + h], scale=0.125, accum_out=SMT[:, 8 + h:9 + h])
                R.inc("scalar", fn, sf_a2d) if h == 3 else R.op("scalar", fn)

        def p1_softmax_b(ci, t):
            L = 128 * (t + 1)
            R.wait("vector", sf_a2d)
            R.op("vector", lambda e: e.reciprocal(SMT[:, 12:16], SMT[:, 8:12]))
            R.op("vector", lambda e: e.drain())
            for h in range(4):
                fn = lambda e, h=h, L=L: e.tensor_scalar(
                    PBF[:, 2048 * h:2048 * h + L], PBF[:, 2048 * h:2048 * h + L],
                    SMT[:, 12 + h:13 + h], None, op0=ALU.mult)
                R.inc("vector", fn, pN) if h == 3 else R.op("vector", fn)

        def p1_tav(gi, t):
            R.wait("tensor", pN)
            R.wait("tensor", aoE)

            def emit_T(a, t):
                TB = mmPT.n % 2
                kts = [kt for kt in (2 * a, 2 * a + 1) if kt <= t]
                R.wait_rel("tensor", ptE, 1)
                for j, kt in enumerate(kts):
                    for h in range(4):
                        fn = lambda e, kt=kt, h=h, j=j, TB=TB: e.transpose(
                            psTv[:, 1024 * TB + 512 * j + 128 * h:1024 * TB + 512 * j + 128 * (h + 1)],
                            PBF[:, 2048 * h + 128 * kt:2048 * h + 128 * (kt + 1)], ident[:, :])
                        if j == len(kts) - 1 and h == 3:
                            R.inc("tensor", fn, mmPT)
                        else:
                            R.op("tensor", fn)
                eng = "vector" if a % 2 == 0 else "scalar"
                R.wait(eng, mmPT)
                w = 512 * len(kts)
                if eng == "vector":
                    R.inc(eng, lambda e, TB=TB, a=a, w=w: e.tensor_copy(
                        PTS[:, 1024 * (a % 2):1024 * (a % 2) + w], psTv[:, 1024 * TB:1024 * TB + w]), ptE)
                else:
                    R.inc(eng, lambda e, TB=TB, a=a, w=w: e.activation(
                        out=PTS[:, 1024 * (a % 2):1024 * (a % 2) + w], in_=psTv[:, 1024 * TB:1024 * TB + w],
                        func=AF.Identity), ptE)

            def emit_AV(a, t, gi):
                R.wait("tensor", ptE)
                kts = [kt for kt in (2 * a, 2 * a + 1) if kt <= t]
                for j, kt in enumerate(kts):
                    for pr in range(2):
                        for cg in range(2):
                            fn = lambda e, kt=kt, t=t, gi=gi, pr=pr, cg=cg, a=a, j=j: e.matmul(
                                ps_all[64 * cg:64 * (cg + 1), 2048 + 512 * pr:2048 + 512 * pr + 128],
                                VT[:, (kt * 8 + 2 * gi + pr) * 128 + 64 * cg:(kt * 8 + 2 * gi + pr) * 128 + 64 * (cg + 1)],
                                PTS[:, 1024 * (a % 2) + 512 * j + 128 * (2 * pr + cg):1024 * (a % 2) + 512 * j + 128 * (2 * pr + cg + 1)],
                                start=(kt == 0), stop=(kt == t), tile_position=(0, 64 * cg))
                            if pr == 1 and cg == 1 and kt == t:
                                R.inc("tensor", fn, mmAV)
                            else:
                                R.op("tensor", fn)

            na = (t + 2) // 2
            for a in range(na):
                emit_T(a, t)
                if a > 0:
                    emit_AV(a - 1, t, gi)
            emit_AV(na - 1, t, gi)
            R.wait("vector", mmAV)
            R.op("vector", lambda e, gi=gi, t=t: e.tensor_copy(
                AO[:, (2 * gi) * 2048 + 128 * t:(2 * gi) * 2048 + 128 * (t + 1)], ps_all[:, 2048:2176]))
            R.inc("vector", lambda e, gi=gi, t=t: e.tensor_copy(
                AO[:, (2 * gi + 1) * 2048 + 128 * t:(2 * gi + 1) * 2048 + 128 * (t + 1)], ps_all[:, 2560:2688]), aoE)

        combos = [(gi, t) for gi in range(4) for t in range(NT)]
        p1_scores(0, *combos[0])
        for ci, (gi, t) in enumerate(combos):
            p1_softmax_a(ci, t)
            if ci + 1 < len(combos):
                p1_scores(ci + 1, *combos[ci + 1])
            p1_softmax_b(ci, t)
            p1_tav(gi, t)

        R.wait("tensor", aoE)
        R.wait("tensor", ptE)
        R.wait("gpsimd", mmQ)
        for tgi in range(4):
            for c in range(8):
                R.wait_rel("gpsimd", mmPc, 1)
                R.inc("gpsimd", lambda e, c=c: e.dma_start(
                    out=WOB[:, 1024 * (c % 2):1024 * (c % 2 + 1)], in_=g["p1_wo"][:, 1024 * c:1024 * (c + 1)]), wol, 16)
                R.wait("tensor", wol)
                if c == 0:
                    R.wait("tensor", esm["s_apply"])
                for tt in range(4):
                    for j in range(2):
                        fn = lambda e, c=c, tt=tt, j=j, tgi=tgi: e.matmul(
                            ps_all[:, 512 * (2 * tt + j):512 * (2 * tt + j + 1)],
                            AO[:, 2048 * c + 128 * (4 * tgi + tt):2048 * c + 128 * (4 * tgi + tt) + 128],
                            WOB[:, 1024 * (c % 2) + 512 * j:1024 * (c % 2) + 512 * (j + 1)],
                            start=(c == 0), stop=(c == 7))
                        if tt == 3 and j == 1:
                            R.inc("tensor", fn, mmPc)
                        else:
                            R.op("tensor", fn)
            R.wait("vector", mmPc)
            if tgi == 0:
                R.wait("sync", psm["s_app"])
            for tt in range(4):
                t_glob = tgi * 4 + tt
                if t_glob == 0:
                    epi_prefetch_x(R, ep, esm, x_src, seq, 0)
                emit_epilogue_tile(
                    R, ep, esm, seq, t_glob,
                    (lambda tt=tt: ps_all[:, 1024 * tt:1024 * (tt + 1)]),
                    x_dst, direct_sbuf=True)
                if t_glob < NT - 1:
                    epi_prefetch_x(R, ep, esm, x_src, seq, t_glob + 1)
    R.wait("gpsimd", esm["st"])




def emit_phase3(R, g, x_src, x_dst):
    R.init_epsc()
    cl = R.sem("p3cl")
    ident = R.sb("p3ident", [128, 128], BF16)
    MASKD = R.sb("p3MASKD", [128, 128], F32)
    W3Q = R.sb("p3W", [128, 192], BF16)
    B3Q = R.sb("p3B", [128, 3], F32)
    PROJB = R.sb("p3PJB", [128, 1024], F32)
    R.inc("sync", lambda e: e.dma_start(out=ident[:], in_=g["ident_bf"][:, :]), cl, 16)
    R.inc("sync", lambda e: e.dma_start(out=MASKD[:], in_=g["p1_mask"][:, 0:128]), cl, 16)
    R.inc("gpsimd", lambda e: e.dma_start(out=W3Q[:], in_=g["p3_wqkv"][:, :]), cl, 16)
    R.inc("sync", lambda e: e.dma_start(out=B3Q[:], in_=g["p3_bqkv"][:, :]), cl, 16)
    R.inc("sync", lambda e: e.dma_start(out=PROJB[:], in_=g["p3_projb"][:, :]), cl, 16)
    for e in ENGINES:
        R.wait(e, cl)

    ps_all = R.ps("p3ps", [128, 4096], F32)
    ps_bf = ps_all[:, 0:1024].bitcast(BF16)
    psTv = ps_all[:, 3072:4096].bitcast(BF16)
    U_fm = R.sb("p3U", [128, 8 * 2048], BF16)
    AO = U_fm
    Q_fm = R.sb("p3Q", [128, 8 * 2048], BF16)
    K_fm = R.sb("p3K", [128, 8 * 2048], BF16)
    VT = R.sb("p3VT", [128, 16 * 1024], BF16)
    SCO = R.sb("p3SCO", [128, 2 * 2 * 2048], BF16)
    PBF = R.sb("p3PBF", [128, 2 * 2048], BF16)
    PTS = R.sb("p3PTS", [128, 2 * 512], BF16)
    VSB = R.sb("p3VSB", [128, 2048], BF16)
    WOB = R.sb("p3WOB", [128, 2 * 1024], BF16)
    SMT = R.sb("p3SMT", [128, 8], F32)
    pt = mk_preln_tensors(R, "p3pl")
    pt.update(ps_bf=ps_bf, ident=ident)
    psm = mk_preln_sems(R, "p3pl")
    ep = mk_epi_tensors(R, "p3ep", R.sb("p3DSB", [128, 1024], F32), pt["XT"][:, 0:1024],
                        TMPD=PBF[:, 0:2048].bitcast(F32))
    esm = mk_epi_sems(R, "p3ep")
    mmQ = R.sem("p3mmQ")
    qkE = R.sem("p3qkE")
    mmPT = R.sem("p3mmPT")
    ptE = R.sem("p3ptE")
    mmS = R.sem("p3mmS")
    sevE = R.sem("p3sevE")
    sf_d2a = R.sem("p3sfd2a")
    sf_a2d = R.sem("p3sfa2d")
    pN = R.sem("p3pN")
    mmAV = R.sem("p3mmAV")
    aoE = R.sem("p3aoE")
    wol = R.sem("p3wol")
    mmPc = R.sem("p3mmPc")

    for seq in range(NSEQ):
        if seq > 0:
            R.wait("sync", esm["st"])
            R.wait("vector", mmQ)
            R.wait("vector", mmPc)
            R.wait("tensor", esm["s_apply"])
            R.wait("tensor", esm["st"])
        emit_preln(R, pt, psm, x_src, seq, U_fm=U_fm)

        R.wait("tensor", psm["s_ev"])
        for pr in range(8):
            for p in range(4):
                pp = 4 * pr + p
                bq = 3 * (pp % 2)
                R.wait_rel("tensor", qkE, 3)
                for b in range(2):
                    base = 64 * b
                    for w in range(3):
                        fn = lambda e, pr=pr, p=p, b=b, w=w, bq=bq, base=base: e.matmul(
                            ps_all[base:base + 64, 512 * (bq + w):512 * (bq + w) + 512],
                            W3Q[base:base + 64, 64 * w:64 * (w + 1)],
                            U_fm[base:base + 64, pr * 2048 + 512 * p:pr * 2048 + 512 * (p + 1)],
                            start=True, stop=True, tile_position=(base, base))
                        if b == 1 and w == 2:
                            R.inc("tensor", fn, mmQ)
                        else:
                            R.op("tensor", fn)
                R.wait("scalar", mmQ)
                if p == 0:
                    R.wait("scalar", mmPT)
                R.op("scalar", lambda e, pr=pr, p=p, bq=bq: e.activation(
                    out=Q_fm[:, pr * 2048 + 512 * p:pr * 2048 + 512 * (p + 1)],
                    in_=ps_all[:, 512 * bq:512 * (bq + 1)], func=AF.Identity, bias=B3Q[:, 0:1]))
                R.op("scalar", lambda e, pr=pr, p=p, bq=bq: e.activation(
                    out=K_fm[:, pr * 2048 + 512 * p:pr * 2048 + 512 * (p + 1)],
                    in_=ps_all[:, 512 * (bq + 1):512 * (bq + 2)], func=AF.Identity, bias=B3Q[:, 1:2]))
                R.inc("scalar", lambda e, p=p, bq=bq: e.activation(
                    out=VSB[:, 512 * p:512 * (p + 1)],
                    in_=ps_all[:, 512 * (bq + 2):512 * (bq + 3)], func=AF.Identity, bias=B3Q[:, 2:3]), qkE, 3)
            R.wait("tensor", qkE)
            for j in range(2):
                TB = mmPT.n % 2
                R.wait_rel("tensor", ptE, 1)
                for i in range(8):
                    fn = lambda e, j=j, i=i, TB=TB: e.transpose(
                        psTv[:, 1024 * TB + 128 * i:1024 * TB + 128 * (i + 1)],
                        VSB[:, 128 * (8 * j + i):128 * (8 * j + i + 1)], ident[:, :])
                    if i == 7:
                        R.inc("tensor", fn, mmPT)
                    else:
                        R.op("tensor", fn)
                R.wait("vector", mmPT)
                R.inc("vector", lambda e, j=j, pr=pr, TB=TB: e.tensor_copy(
                    VT[:, :].rearrange("p (kt r) -> p kt r", kt=16)[:, 8 * j:8 * (j + 1), 128 * pr:128 * (pr + 1)],
                    psTv[:, 1024 * TB:1024 * (TB + 1)].rearrange("p (k r) -> p k r", k=8)), ptE)

        R.wait("tensor", qkE)
        R.wait("tensor", ptE)

        def p3_scores(ci, pr, t):
            sl = (ci % 2) * 4096
            kpn = t // 4 + 1
            vloc = t % 4
            for kp in range(kpn):
                width = min(512, 128 * (t + 1) - 512 * kp)
                R.wait("tensor", sevE)
                for b in range(2):
                    fn = lambda e, pr=pr, t=t, kp=kp, width=width, b=b: e.matmul(
                        ps_all[:, 512 * b:512 * b + width],
                        Q_fm[64 * b:64 * (b + 1), pr * 2048 + 128 * t:pr * 2048 + 128 * (t + 1)],
                        K_fm[64 * b:64 * (b + 1), pr * 2048 + 512 * kp:pr * 2048 + 512 * kp + width],
                        start=True, stop=True, tile_position=(64 * b, 0))
                    if b == 1:
                        R.inc("tensor", fn, mmS)
                    else:
                        R.op("tensor", fn)
                diag = (kp == kpn - 1)
                R.wait("vector", mmS)
                if kp == 0:
                    R.wait_rel("vector", sf_a2d, 1)
                for b in range(2):
                    dst0 = sl + 2048 * b + 512 * kp
                    if not diag:
                        fn = lambda e, b=b, dst0=dst0: e.tensor_copy(
                            SCO[:, dst0:dst0 + 512], ps_all[:, 512 * b:512 * (b + 1)])
                        R.inc("vector", fn, sevE) if b == 1 else R.op("vector", fn)
                    else:
                        if vloc > 0:
                            R.op("vector", lambda e, b=b, dst0=dst0, vloc=vloc: e.tensor_copy(
                                SCO[:, dst0:dst0 + 128 * vloc], ps_all[:, 512 * b:512 * b + 128 * vloc]))
                        fn = lambda e, b=b, dst0=dst0, vloc=vloc, width=width: e.tensor_tensor(
                            SCO[:, dst0 + 128 * vloc:dst0 + width],
                            ps_all[:, 512 * b + 128 * vloc:512 * b + width],
                            MASKD[:, 0:width - 128 * vloc], op=ALU.add)
                        R.inc("vector", fn, sevE) if b == 1 else R.op("vector", fn)

        def p3_softmax_a(ci, t):
            sl = (ci % 2) * 4096
            L = 128 * (t + 1)
            R.wait("vector", sevE)
            for b in range(2):
                R.op("vector", lambda e, b=b, sl=sl, L=L: e.reduce_max(
                    SMT[:, b:b + 1], SCO[:, sl + 2048 * b:sl + 2048 * b + L], axis=AX.X, negate=True))
            R.op("vector", lambda e: e.drain())
            R.inc("vector", lambda e: e.tensor_scalar(
                SMT[:, 2:4], SMT[:, 0:2], 0.125, None, op0=ALU.mult), sf_d2a)
            R.wait("scalar", sf_d2a)
            R.wait("scalar", mmPT)
            for b in range(2):
                fn = lambda e, b=b, sl=sl, L=L: e.activation(
                    out=PBF[:, 2048 * b:2048 * b + L], in_=SCO[:, sl + 2048 * b:sl + 2048 * b + L],
                    func=AF.Exp, bias=SMT[:, 2 + b:3 + b], scale=0.125, accum_out=SMT[:, 4 + b:5 + b])
                R.inc("scalar", fn, sf_a2d) if b == 1 else R.op("scalar", fn)

        def p3_softmax_b(ci, t):
            L = 128 * (t + 1)
            R.wait("vector", sf_a2d)
            R.op("vector", lambda e: e.reciprocal(SMT[:, 6:8], SMT[:, 4:6]))
            R.op("vector", lambda e: e.drain())
            for b in range(2):
                fn = lambda e, b=b, L=L: e.tensor_scalar(
                    PBF[:, 2048 * b:2048 * b + L], PBF[:, 2048 * b:2048 * b + L],
                    SMT[:, 6 + b:7 + b], None, op0=ALU.mult)
                R.inc("vector", fn, pN) if b == 1 else R.op("vector", fn)

        def p3_tav(pr, t):
            R.wait("tensor", pN)
            R.wait("tensor", aoE)

            def emit_T3(a, t):
                TB = mmPT.n % 2
                kts = [kt for kt in (2 * a, 2 * a + 1) if kt <= t]
                R.wait_rel("tensor", ptE, 1)
                for j, kt in enumerate(kts):
                    for b in range(2):
                        fn = lambda e, kt=kt, b=b, j=j, TB=TB: e.transpose(
                            psTv[:, 1024 * TB + 256 * j + 128 * b:1024 * TB + 256 * j + 128 * (b + 1)],
                            PBF[:, 2048 * b + 128 * kt:2048 * b + 128 * (kt + 1)], ident[:, :])
                        if j == len(kts) - 1 and b == 1:
                            R.inc("tensor", fn, mmPT)
                        else:
                            R.op("tensor", fn)
                eng = "vector" if a % 2 == 0 else "scalar"
                R.wait(eng, mmPT)
                w = 256 * len(kts)
                if eng == "vector":
                    R.inc(eng, lambda e, TB=TB, a=a, w=w: e.tensor_copy(
                        PTS[:, 512 * (a % 2):512 * (a % 2) + w], psTv[:, 1024 * TB:1024 * TB + w]), ptE)
                else:
                    R.inc(eng, lambda e, TB=TB, a=a, w=w: e.activation(
                        out=PTS[:, 512 * (a % 2):512 * (a % 2) + w], in_=psTv[:, 1024 * TB:1024 * TB + w],
                        func=AF.Identity), ptE)

            def emit_AV3(a, t, pr):
                R.wait("tensor", ptE)
                kts = [kt for kt in (2 * a, 2 * a + 1) if kt <= t]
                for j, kt in enumerate(kts):
                    for b in range(2):
                        fn = lambda e, kt=kt, t=t, pr=pr, b=b, a=a, j=j: e.matmul(
                            ps_all[64 * b:64 * (b + 1), 2048:2176],
                            VT[:, (kt * 8 + pr) * 128 + 64 * b:(kt * 8 + pr) * 128 + 64 * (b + 1)],
                            PTS[:, 512 * (a % 2) + 256 * j + 128 * b:512 * (a % 2) + 256 * j + 128 * (b + 1)],
                            start=(kt == 0), stop=(kt == t), tile_position=(0, 64 * b))
                        if b == 1 and kt == t:
                            R.inc("tensor", fn, mmAV)
                        else:
                            R.op("tensor", fn)

            na = (t + 2) // 2
            for a in range(na):
                emit_T3(a, t)
                if a > 0:
                    emit_AV3(a - 1, t, pr)
            emit_AV3(na - 1, t, pr)
            R.wait("vector", mmAV)
            R.inc("vector", lambda e, pr=pr, t=t: e.tensor_copy(
                AO[:, pr * 2048 + 128 * t:pr * 2048 + 128 * (t + 1)], ps_all[:, 2048:2176]), aoE)

        combos3 = [(pr, t) for pr in range(8) for t in range(NT)]
        p3_scores(0, *combos3[0])
        for ci, (pr, t) in enumerate(combos3):
            p3_softmax_a(ci, t)
            if ci + 1 < len(combos3):
                p3_scores(ci + 1, *combos3[ci + 1])
            p3_softmax_b(ci, t)
            p3_tav(pr, t)

        R.wait("tensor", aoE)
        R.wait("tensor", ptE)
        for tgi in range(4):
            for c in range(8):
                R.wait_rel("gpsimd", mmPc, 1)
                R.inc("gpsimd", lambda e, c=c: e.dma_start(
                    out=WOB[:, 1024 * (c % 2):1024 * (c % 2 + 1)], in_=g["p3_proj"][:, 1024 * c:1024 * (c + 1)]), wol, 16)
                R.wait("tensor", wol)
                if c == 0:
                    R.wait("tensor", esm["s_apply"])
                for tt in range(4):
                    for j in range(2):
                        fn = lambda e, c=c, tt=tt, j=j, tgi=tgi: e.matmul(
                            ps_all[:, 512 * (2 * tt + j):512 * (2 * tt + j + 1)],
                            AO[:, 2048 * c + 128 * (4 * tgi + tt):2048 * c + 128 * (4 * tgi + tt) + 128],
                            WOB[:, 1024 * (c % 2) + 512 * j:1024 * (c % 2) + 512 * (j + 1)],
                            start=(c == 0), stop=(c == 7))
                        if tt == 3 and j == 1:
                            R.inc("tensor", fn, mmPc)
                        else:
                            R.op("tensor", fn)
            R.wait("vector", mmPc)
            if tgi == 0:
                R.wait("sync", psm["s_app"])
            for tt in range(4):
                t_glob = tgi * 4 + tt
                if t_glob == 0:
                    epi_prefetch_x(R, ep, esm, x_src, seq, 0)
                emit_epilogue_tile(
                    R, ep, esm, seq, t_glob,
                    (lambda tt=tt: ps_all[:, 1024 * tt:1024 * (tt + 1)]),
                    x_dst, bias_bcast=PROJB)
                if t_glob < NT - 1:
                    epi_prefetch_x(R, ep, esm, x_src, seq, t_glob + 1)
    R.wait("gpsimd", esm["st"])




def emit_phase4(R, g, x_src, x_dst):
    R.init_epsc()
    cl = R.sem("p4cl")
    IDENTF = R.sb("p4idf", [128, 128], F32)
    CENT = R.sb("p4cent", [128, 64], F32)
    BETB = R.sb("p4betb", [128, 64], BF16)
    GAM = R.sb("p4gam", [128, 64], F32)
    B1IE = R.sb("p4b1i", [128, 32], F32)
    B1C = R.sb("p4b1c", [128, 32], F32)
    MA = R.sb("p4ma", [128, 128], F32)
    AFFS = R.sb("p4affs", [128, 8], F32)
    RMAL = R.sb("p4rmal", [128, 2], F32)
    ONESB = R.sb("p4ones", [1, 128], BF16)
    IDENTB = R.sb("p4idb", [128, 128], BF16)
    B2R = R.sb("p4b2r", [1, 512], BF16)
    R.inc("sync", lambda e: e.dma_start(out=IDENTF[:], in_=g["ident_f32"][:, :]), cl, 16)
    R.inc("sync", lambda e: e.dma_start(out=CENT[:], in_=g["p4_cent"][:, :]), cl, 16)
    R.inc("sync", lambda e: e.dma_start(out=GAM[:], in_=g["p4_gamma"][:, :]), cl, 16)

    R.inc("gpsimd", lambda e: e.dma_start(out=BETB[:], in_=g["p4_beta"][:, :]), cl, 16)
    R.inc("gpsimd", lambda e: e.dma_start(out=ONESB[:], in_=g["ones_row"][:, :]), cl, 16)
    R.inc("sync", lambda e: e.dma_start(out=IDENTB[:], in_=g["ident_bf"][:, :]), cl, 16)
    for e in ENGINES:
        R.wait(e, cl)

    ps_all = R.ps("p4ps", [128, 4096], F32)
    psXH = ps_all[:, 2048:2560].bitcast(BF16)
    u_acc = R.sb("p4ua", [128, 16 * 1024], F32)
    XHF = R.sb("p4xhf", [128, 8 * 2048], BF16)
    XS = R.sb("p4xs", [128, 16 * 1024], BF16)
    R_ = R.sb("p4r", [128, 32 * 1024], BF16)
    W1B = R.sb("p4w1b", [128, 4 * 1024], BF16)
    UF = R_[:, 0:2048].bitcast(F32)
    XH = R_[:, 2048:3072]
    pt = mk_preln_tensors(R, "p4pl", need_ufm=False, XT=XS[:, 0:4096].bitcast(F32))
    psm = mk_preln_sems(R, "p4pl")
    ep = mk_epi_tensors(R, "p4ep", None, pt["XT"][:, 0:1024], stats_from=pt,
                        TMPD=XS[:, 4096:6144].bitcast(F32))
    esm = mk_epi_sems(R, "p4ep")
    trU = R.sem("p4trU")
    trUe = R.sem("p4trUe")
    afM = R.sem("p4afM")
    roD = R.sem("p4roD")
    roA = R.sem("p4roA")
    trX = R.sem("p4trX")
    trXe = R.sem("p4trXe")
    w1l = R.sem("p4w1l")
    mmS1 = R.sem("p4mmS1")
    b1d = R.sem("p4b1d")
    relE = R.sem("p4relE")
    w2l = R.sem("p4w2l")
    b2l = R.sem("p4b2l")
    mmS2 = R.sem("p4mmS2")
    uaE = R.sem("p4uaE")

    for seq in range(NSEQ):
        if seq > 0:
            R.wait("sync", esm["st"])
            R.wait("sync", mmS1)
            R.wait("sync", mmS2)
            R.wait("sync", w2l)
            R.wait("tensor", uaE)
            R.wait("tensor", esm["st"])
            R.wait("vector", mmS2)
            R.wait("vector", esm["st"])
        emit_preln(R, pt, psm, x_src, seq, u_f32=u_acc)

        v = dict(BN6=pt["BN6"], MV=pt["MV"], SD=pt["SD"], RSTD=pt["RSTD"], NMR=pt["NMR"],
                 s_d2a=psm["s_d2a"], s_a2d=psm["s_a2d"])
        for t in range(NT):
            R.wait("vector", psm["s_app"], psm["s_app"].n - (NT - 1 - t))
            R.op("vector", lambda e, t=t: e.bn_stats(pt["BN6"][:, 12 * t:12 * t + 6], u_acc[:, 1024 * t:1024 * t + 512]))
            R.op("vector", lambda e, t=t: e.bn_stats(pt["BN6"][:, 12 * t + 6:12 * t + 12], u_acc[:, 1024 * t + 512:1024 * (t + 1)]))
            ln_rstd_chain(R, v, t)
            R.wait("vector", trX)
            R.inc("vector", lambda e, t=t: e.tensor_scalar(
                XH[:, :], u_acc[:, 1024 * t:1024 * (t + 1)],
                pt["MV"][:, 2 * t:2 * t + 1], pt["RSTD"][:, t:t + 1], op0=ALU.subtract, op1=ALU.mult), psm["s_app"])
            par = t % 2
            R.wait("tensor", psm["s_app"], psm["s_app"].n - 1)
            R.wait_rel("tensor", trUe, 1)
            for d in range(8):
                fn = lambda e, t=t, d=d, par=par: e.transpose(
                    ps_all[:, 1024 * par + 128 * d:1024 * par + 128 * (d + 1)],
                    u_acc[:, 1024 * t + 128 * d:1024 * t + 128 * (d + 1)], IDENTF[:, :])
                if d == 7:
                    R.inc("tensor", fn, trU)
                else:
                    R.op("tensor", fn)
            R.wait("vector", trU)
            R.inc("vector", lambda e, par=par: e.tensor_copy(UF[:, :], ps_all[:, 1024 * par:1024 * (par + 1)]), trUe)
            R.wait("tensor", trUe)
            R.wait("tensor", roD)
            for d in range(8):
                fn = lambda e, d=d: e.matmul(
                    ps_all[:, 3584:3592], UF[:, 128 * d:128 * (d + 1)], CENT[:, 8 * d:8 * (d + 1)],
                    start=(d == 0), stop=(d == 7))
                if d == 7:
                    R.inc("tensor", fn, afM)
                else:
                    R.op("tensor", fn)
            R.wait("vector", afM)
            R.op("vector", lambda e: e.tensor_copy(AFFS[:, :], ps_all[:, 3584:3592]))
            R.op("vector", lambda e: e.drain())
            R.op("vector", lambda e: e.reduce_max(RMAL[:, 0:1], AFFS[:, :], axis=AX.X))
            R.inc("vector", lambda e: e.drain(), roD)
            R.wait("scalar", roD)
            R.inc("scalar", lambda e: e.activation(out=RMAL[:, 1:2], in_=RMAL[:, 0:1], func=AF.Sigmoid), roA)
            R.wait("vector", roA)
            R.op("vector", lambda e, t=t: e.tensor_scalar(
                MA[:, 8 * t:8 * (t + 1)], AFFS[:, :], RMAL[:, 0:1], None, op0=ALU.is_ge))
            R.op("vector", lambda e: e.drain())
            R.op("vector", lambda e, t=t: e.tensor_scalar(
                MA[:, 8 * t:8 * (t + 1)], MA[:, 8 * t:8 * (t + 1)], RMAL[:, 1:2], None, op0=ALU.mult))
            R.wait("tensor", psm["s_app"])
            R.wait("tensor", trXe)
            for d in range(8):
                fn = lambda e, d=d: e.transpose(
                    psXH[:, 128 * d:128 * (d + 1)], XH[:, 128 * d:128 * (d + 1)], IDENTB[:, :])
                if d == 7:
                    R.inc("tensor", fn, trX)
                else:
                    R.op("tensor", fn)
            R.wait("vector", trX)
            R.inc("vector", lambda e, t=t: e.tensor_copy(
                XHF[:, :].rearrange("p (d s) -> p d s", d=8)[:, :, 128 * t:128 * (t + 1)],
                psXH[:, :].rearrange("p (d s) -> p d s", d=8)), trXe)

        R.wait("tensor", trXe)
        R.wait("tensor", trUe)
        R.wait("tensor", roD)
        for tc in range(2):
            for ex in range(8):
                R.wait("vector", mmS2)
                R.wait("vector", mmS1)
                for d in range(8):
                    fn = lambda e, d=d, ex=ex, tc=tc: e.tensor_scalar(
                        XS[:, 1024 * d:1024 * (d + 1)],
                        XHF[:, 2048 * d + 1024 * tc:2048 * d + 1024 * (tc + 1)],
                        GAM[:, 8 * ex + d:8 * ex + d + 1], None, op0=ALU.mult)
                    if d == 7:
                        R.inc("vector", fn, roD)
                    else:
                        R.op("vector", fn)
                R.wait("sync", b1d)
                R.inc("sync", lambda e, ex=ex: e.dma_start(out=B1IE[:, :], in_=g["p4_b1"][:, 32 * ex:32 * (ex + 1)]), cl, 16)
                R.wait("tensor", roD)
                R.wait("tensor", uaE)
                for c in range(32):
                    slot = c % 4
                    par = c % 2
                    if c % 2 == 0:
                        R.wait_rel("gpsimd", mmS1, 2)
                        R.inc("gpsimd", lambda e, c=c, ex=ex, slot=slot: e.dma_start(
                            out=W1B[:, 1024 * slot:1024 * (slot + 2)],
                            in_=g["p4_w1"][:, 1024 * (32 * ex + c):1024 * (32 * ex + c + 2)]), w1l, 16)
                    R.wait("tensor", w1l)
                    bcol = 2048 if par == 0 else 3584
                    R.wait_rel("tensor", b1d, 1)
                    for d in range(8):
                        R.op("tensor", lambda e, d=d, ex=ex, slot=slot, bcol=bcol: e.matmul(
                            ps_all[:, bcol:bcol + 1], W1B[:, 1024 * slot + 128 * d:1024 * slot + 128 * (d + 1)],
                            BETB[:, 8 * ex + d:8 * ex + d + 1],
                            start=(d == 0), stop=(d == 7)))
                    R.wait_rel("tensor", relE, 1)
                    for pc in range(2):
                        for d in range(8):
                            fn = lambda e, c=c, pc=pc, d=d, par=par, slot=slot: e.matmul(
                                ps_all[:, 512 * (2 * par + pc):512 * (2 * par + pc + 1)],
                                W1B[:, 1024 * slot + 128 * d:1024 * slot + 128 * (d + 1)],
                                XS[:, 1024 * d + 512 * pc:1024 * d + 512 * (pc + 1)],
                                start=(d == 0), stop=(d == 7))
                            if pc == 1 and d == 7:
                                R.inc("tensor", fn, mmS1)
                            else:
                                R.op("tensor", fn)
                    R.wait("vector", mmS1)
                    if c == 0:
                        R.wait("vector", cl)
                    R.inc("vector", lambda e, c=c, bcol=bcol: e.tensor_tensor(
                        B1C[:, c:c + 1], ps_all[:, bcol:bcol + 1], B1IE[:, c:c + 1], op=ALU.add), b1d)
                    R.wait("scalar", b1d)
                    R.wait("scalar", mmS1)
                    for pc in range(2):
                        fn = lambda e, c=c, pc=pc, par=par: e.activation(
                            out=R_[:, 512 * (2 * c + pc):512 * (2 * c + pc + 1)],
                            in_=ps_all[:, 512 * (2 * par + pc):512 * (2 * par + pc + 1)],
                            func=AF.Relu, bias=B1C[:, c:c + 1])
                        if pc == 1:
                            R.inc("scalar", fn, relE)
                        else:
                            R.op("scalar", fn)
                R.wait("gpsimd", mmS1)
                for half in range(2):
                    if half == 1:
                        R.wait("gpsimd", mmS2)
                    R.inc("gpsimd", lambda e, ex=ex, half=half: e.dma_start(
                        out=B2R[:, :], in_=g["p4_b2"][0:1, 1024 * ex + 512 * half:1024 * ex + 512 * (half + 1)]), b2l, 16)
                    R.inc("gpsimd", lambda e, ex=ex, half=half: e.dma_start(
                        out=XS[:, :].rearrange("p (c s) -> p c s", c=32),
                        in_=g["p4_w2"][:, :].rearrange("p (E c h s) -> p E c h s", E=8, c=32, h=2)[:, ex, :, half, :]),
                        w2l, 16)
                    R.wait("tensor", w2l)
                    R.wait("tensor", relE)
                    R.wait("tensor", b2l)
                    if half == 0:
                        R.wait("tensor", b1d)
                    R.wait("tensor", uaE)
                    for c in range(32):
                        for tt in range(8):
                            R.op("tensor", lambda e, c=c, tt=tt: e.matmul(
                                ps_all[:, 512 * tt:512 * (tt + 1)],
                                R_[:, 512 * (2 * c + tt // 4) + 128 * (tt % 4):512 * (2 * c + tt // 4) + 128 * (tt % 4) + 128],
                                XS[:, 512 * c:512 * (c + 1)],
                                start=(c == 0), stop=False))
                    for tt in range(8):
                        fn = lambda e, tt=tt: e.matmul(
                            ps_all[:, 512 * tt:512 * (tt + 1)], ONESB[:, :], B2R[0:1, 0:512],
                            start=False, stop=True)
                        if tt == 7:
                            R.inc("tensor", fn, mmS2)
                        else:
                            R.op("tensor", fn)
                    R.wait("vector", mmS2)
                    for tt in range(8):
                        T = tc * 8 + tt
                        fn = lambda e, tt=tt, T=T, ex=ex, half=half: e.scalar_tensor_tensor(
                            u_acc[:, 1024 * T + 512 * half:1024 * T + 512 * (half + 1)],
                            ps_all[:, 512 * tt:512 * (tt + 1)],
                            MA[:, 8 * T + ex:8 * T + ex + 1],
                            u_acc[:, 1024 * T + 512 * half:1024 * T + 512 * (half + 1)],
                            op0=ALU.mult, op1=ALU.add)
                        if tt == 7:
                            R.inc("vector", fn, uaE)
                        else:
                            R.op("vector", fn)

        R.wait("vector", uaE)
        R.wait("sync", psm["s_app"])
        R.wait("sync", mmS2)
        R.wait("sync", w2l)
        for t in range(NT):
            if t == 0:
                epi_prefetch_x(R, ep, esm, x_src, seq, 0)
            emit_epilogue_tile(
                R, ep, esm, seq, t,
                (lambda t=t: u_acc[:, 1024 * t:1024 * (t + 1)]),
                x_dst, direct_sbuf=True)
            if t < NT - 1:
                epi_prefetch_x(R, ep, esm, x_src, seq, t + 1)
    R.wait("gpsimd", esm["st"])


PHASE_EMITTERS = {1: emit_phase1, 2: emit_phase2, 3: emit_phase3, 4: emit_phase4}


def _run(phase_list, imgs, x_full):
    nc = bass.Bass("TRN2", target_bir_lowering=False)
    g = {}
    for name, arr in imgs.items():
        dt = IMG_DTYPES.get(name, F32)
        g[name] = nc.dram_tensor(name, list(arr.shape), dt, kind="ExternalInput").ap()
    x_in = nc.dram_tensor("x_in", [NSEQ * S, D], F32, kind="ExternalInput").ap()
    x_out = nc.dram_tensor("x_out", [NSEQ * S, D], F32, kind="ExternalOutput").ap()
    n_ph = len(phase_list)
    scratch = [nc.dram_tensor(f"x_mid{i}", [NSEQ * S, D], F32, kind="Internal").ap()
               for i in range(max(0, n_ph - 1))]

    with ExitStack() as stack:
        R = Rec(nc, stack)
        for i, ph in enumerate(phase_list):
            src = x_in if i == 0 else scratch[i - 1]
            dst = x_out if i == n_ph - 1 else scratch[i]
            R.phase_begin()
            with ExitStack() as phase_stack:
                R.stack = phase_stack
                PHASE_EMITTERS[ph](R, g, src, dst)
                R.stack = stack
            if i < n_ph - 1:
                R.barrier(f"pb{i}")
        R.replay()
    nc.finalize()

    in_maps = []
    for core in range(NCORES):
        m = dict(imgs)
        m["x_in"] = np.ascontiguousarray(x_full[core * NSEQ * S:(core + 1) * NSEQ * S, :])
        in_maps.append(m)
    res = run_bass_kernel_spmd(nc, in_maps, core_ids=list(range(NCORES)))
    return np.concatenate([res.results[i]["x_out"] for i in range(NCORES)], axis=0)


def kernel(**inputs):
    inputs = {k: np.asarray(v) for k, v in inputs.items()}
    imgs = build_images(inputs)
    x = np.ascontiguousarray(inputs["L0_attn_x"].astype(np.float32).reshape(B * S, D))
    phase_groups = os.environ.get("MK_PHASES", "1,2,3,4").split("/")
    cur = x
    for grp in phase_groups:
        phl = [int(p) for p in grp.split(",")]
        cur = _run(phl, imgs, cur)
    return cur.reshape(B, S, D)
